# version 15
# baseline (speedup 1.0000x reference)
import contextlib
import numpy as np
import ml_dtypes
import concourse.bass as bass
import concourse.mybir as mybir
from concourse.bass_utils import run_bass_kernel_spmd
from concourse.alu_op_type import AluOpType as OP

F32 = mybir.dt.float32
BF16 = mybir.dt.bfloat16
U8 = mybir.dt.uint8
AF = mybir.ActivationFunctionType
AX = mybir.AxisListType
bfnp = ml_dtypes.bfloat16

H, HD, D = 16, 64, 1024
NB, NKT, NG = 8192, 64, 16
NQ, NQT, NQG = 2048, 16, 4
CS = [4 * s + 4 for s in range(NQT)]
TSHIFT = 60.0
EPS = 1e-6
LNEPS = 1e-5
RD = 1.0 / D

DEBUG_TAPS = []


class Sem:
    def __init__(self, h):
        self.h = h
        self.n = 0


class Ph:
    def __init__(self, name):
        self.name = name
        self.q = {"t": [], "s": [], "v": [], "y": [], "g": []}

    def wait(self, ek, sem, val):
        if val > 0:
            self.q[ek].append(lambda e, s=sem.h, v=val: e.wait_ge(s, v))

    def op(self, ek, bld, sem=None, k=1):
        if sem is None:
            self.q[ek].append(lambda e, b=bld: b(e))
            return None
        sem.n += k
        v = sem.n
        self.q[ek].append(lambda e, b=bld, s=sem.h, kk=k: b(e).then_inc(s, kk))
        return v

    def run(self, nc):
        with nc.Block(self.name) as blk:
            for attr, key in (("sync", "y"), ("gpsimd", "g"), ("tensor", "t"),
                              ("scalar", "s"), ("vector", "v")):
                fns = self.q[key]
                if fns:
                    getattr(blk, attr)(lambda e, fns=fns: [f(e) for f in fns])


def build_nc():
    nc = bass.Bass("TRN2", target_bir_lowering=False)

    def din(name, shape, dt):
        return nc.dram_tensor(name, list(shape), dt, kind="ExternalInput").ap()

    XKV = din("xkv", [NB, D], F32)
    XQ = din("xq", [NQ, D], F32)
    WQ = din("wq", [D, D], F32)
    WK = din("wk", [D, D], F32)
    WV = din("wv", [D, D], F32)
    WO = din("wo", [D, D], F32)
    WG = din("wg", [D, D], F32)
    WE = din("we", [8, D, D], F32)
    BG = din("bg", [1, D], F32)
    BE = din("be", [1, 8 * D], F32)
    QS = din("qscale", [H, 1], F32)
    KSc = din("kscale", [H, 1], F32)
    IDB = din("identb", [128, 128], BF16)
    IDF = din("identf", [128, 128], F32)
    E16 = din("e16", [8, 128, 16], BF16)
    EBC = din("ebc", [8, 16, 128], BF16)
    IOTA = din("iota", [128, D], F32)
    NBI = din("negbigI", [128, 128], BF16)
    ONE = din("onesb", [1, 128], BF16)
    MQ = din("maskq", [NKT, 128, 128], BF16)

    OUT = nc.dram_tensor("out", [NQ, D], F32, kind="ExternalOutput").ap()

    Ks = nc.dram_tensor("Ks", [H, HD, NB], BF16).ap()
    Vs = nc.dram_tensor("Vs", [NKT, 128, H * 65], BF16).ap()
    Qs = nc.dram_tensor("Qs", [H, HD, NQ], BF16).ap()
    Ys = nc.dram_tensor("Ys", [NQ, D], F32).ap()
    scr = {"Ks": Ks, "Vs": Vs, "Qs": Qs, "Ys": Ys, "attn": Ys}
    taps = {}
    for tn in DEBUG_TAPS:
        if tn == "attn":
            continue
        dt = F32 if tn == "Ys" else BF16
        taps[tn] = nc.dram_tensor("tap_" + tn, list(scr[tn].shape), dt,
                                  kind="ExternalOutput").ap()
    tap_attn = tap_lall = None
    if "attn" in DEBUG_TAPS:
        taps.pop("attn", None)
        tap_attn = nc.dram_tensor("tap_attn", [128, 8 * NQ], BF16,
                                  kind="ExternalOutput").ap()
        tap_lall = nc.dram_tensor("tap_lall", [H, NQ], F32,
                                  kind="ExternalOutput").ap()

    top = contextlib.ExitStack()
    E = top.enter_context

    dS = Sem(E(nc.semaphore("dS")))
    dG = Sem(E(nc.semaphore("dG")))
    sT = Sem(E(nc.semaphore("sT")))
    sA = Sem(E(nc.semaphore("sA")))
    sV = Sem(E(nc.semaphore("sV")))
    sG = Sem(E(nc.semaphore("sG")))

    idb = E(nc.sbuf_tensor("idb", [128, 128], BF16))
    idf = E(nc.sbuf_tensor("idf", [128, 128], F32))
    e16 = E(nc.sbuf_tensor("e16_sb", [128, 8 * 16], BF16))
    ebc = E(nc.sbuf_tensor("ebc_sb", [16, 8 * 128], BF16))
    nbi = E(nc.sbuf_tensor("nbi", [128, 128], BF16))
    oneb = E(nc.sbuf_tensor("oneb", [1, 128], BF16))
    qs_c = E(nc.sbuf_tensor("qs_c", [H, 1], F32))
    ks_c = E(nc.sbuf_tensor("ks_c", [H, 1], F32))
    bgb = E(nc.sbuf_tensor("bgb", [1, D], BF16))
    beb = E(nc.sbuf_tensor("beb", [1, 8 * D], BF16))

    S23 = contextlib.ExitStack()
    c_own = S23.enter_context(nc.sbuf_tensor("c_own", [H, NQ], BF16))
    SW = contextlib.ExitStack()
    wT_q = SW.enter_context(nc.sbuf_tensor("wT_q", [128, 8 * D], BF16))
    wT_k = SW.enter_context(nc.sbuf_tensor("wT_k", [128, 8 * D], BF16))
    wT_v = SW.enter_context(nc.sbuf_tensor("wT_v", [128, 8 * D], BF16))
    S34 = contextlib.ExitStack()
    S45 = contextlib.ExitStack()

    with contextlib.ExitStack() as L:
        p = Ph("prep")
        wst = [L.enter_context(nc.sbuf_tensor(f"wst{i}", [128, D], F32)) for i in range(2)]
        wca = [L.enter_context(nc.sbuf_tensor(f"wca{i}", [128, D], BF16)) for i in range(2)]
        pstr = L.enter_context(nc.psum_tensor("pstr0", [128, 1024], F32))

        for sb, src in [(idb, IDB), (idf, IDF), (nbi, NBI), (oneb, ONE),
                        (qs_c, QS), (ks_c, KSc)]:
            p.op("y", lambda y, a=sb, b=src: y.dma_start(a[:, :], b[:, :]), dS, 16)
        constmark = dS.n
        p.op("y", lambda y: y.dma_start(e16[:, :], E16[:, :, :].rearrange("c p h -> p c h")), dS, 16)
        p.op("y", lambda y: y.dma_start(ebc[:, :], EBC[:, :, :].rearrange("c h p -> h c p")), dS, 16)
        p.op("g", lambda g: g.dma_start(bgb[:, :], BG[:, :]), dG, 16)
        p.op("g", lambda g: g.dma_start(beb[:, :], BE[:, :]), dG, 16)

        ldmark = [0, 0]
        castmark = [0, 0]
        trmark = [0, 0]
        evmark = [0, 0]
        it = 0
        for Wsrc, wT in [(WQ, wT_q), (WK, wT_k), (WV, wT_v)]:
            for fc in range(8):
                par = it % 2
                p.wait("y", sG, castmark[par])
                ldmark[par] = p.op("y", lambda y, w=Wsrc, fc=fc, pr=par:
                                   y.dma_start(wst[pr][:, :], w[128 * fc:128 * fc + 128, :]),
                                   dS, 16)
                p.wait("g", dS, ldmark[par])
                p.wait("g", sT, trmark[par])
                castmark[par] = p.op("g", lambda g, pr=par:
                                     g.tensor_copy(wca[pr][:, :], wst[pr][:, :]), sG, 1)
                p.wait("t", sG, castmark[par])
                for half in range(2):
                    p.wait("t", sV, evmark[half])
                    for k4 in range(4):
                        dc = half * 4 + k4
                        p.op("t", lambda t, pr=par, dc=dc, k4=k4, hh=half:
                             t.matmul(pstr[:, 512 * hh + 128 * k4:512 * hh + 128 * k4 + 128],
                                      wca[pr][:, 128 * dc:128 * dc + 128], idb[:, :],
                                      start=True, stop=True, skip_group_check=True),
                             sT if k4 == 3 else None, 1)
                    tmark = sT.n
                    dst = wT[:, :].rearrange("p (dc f) -> p dc f", f=D)[
                        :, half * 4:half * 4 + 4, 128 * fc:128 * fc + 128]
                    src_ = pstr[:, 512 * half:512 * half + 512].rearrange(
                        "p (a b) -> p a b", b=128)
                    p.wait("v", sT, tmark)
                    evmark[half] = p.op("v", lambda v, d=dst, s=src_: v.tensor_copy(d, s), sV, 1)
                trmark[par] = sT.n
                it += 1
        p.wait("y", dS, dS.n)
        p.wait("g", dG, dG.n)
        p.run(nc)

    with contextlib.ExitStack() as L:
        p = Ph("qkv")
        xb = [L.enter_context(nc.sbuf_tensor(f"xb{i}", [128, 4 * D], BF16)) for i in range(2)]
        xfm = [L.enter_context(nc.sbuf_tensor(f"xfm{i}", [128, 4 * D], BF16)) for i in range(2)]
        hold = [L.enter_context(nc.sbuf_tensor(f"hold{i}", [128, 4096], BF16)) for i in range(2)]
        ksq = [L.enter_context(nc.sbuf_tensor(f"ksq{i}", [128, 1024], BF16)) for i in range(2)]
        vx = [L.enter_context(nc.sbuf_tensor(f"vx{i}", [128, H * 65], BF16)) for i in range(2)]
        r2c = [L.enter_context(nc.sbuf_tensor(f"r2c{i}", [16, 512], F32)) for i in range(2)]
        t1c = [L.enter_context(nc.sbuf_tensor(f"t1c{i}", [16, 512], F32)) for i in range(2)]
        rcc = [L.enter_context(nc.sbuf_tensor(f"rcc{i}", [16, 512], F32)) for i in range(2)]
        recc = [L.enter_context(nc.sbuf_tensor(f"recc{i}", [16, 512], F32)) for i in range(2)]
        Bbc = [L.enter_context(nc.sbuf_tensor(f"Bbc{i}", [16, 512], BF16)) for i in range(2)]
        brt = L.enter_context(nc.sbuf_tensor("brt", [16, 512], F32))
        cc = L.enter_context(nc.sbuf_tensor("ccs", [16, 8], F32))
        ptr = L.enter_context(nc.psum_tensor("ptr1", [128, 1024], F32))
        pkq = L.enter_context(nc.psum_tensor("pkq1", [128, 2048], F32))
        pr2 = L.enter_context(nc.psum_tensor("pr2", [16, 512], F32))

        for i in range(2):
            p.op("g", lambda g, i=i: g.memset(vx[i][:, :], 1.0), sG, 1)
        p.op("g", lambda g: g.memset(cc[:, 2:3], 0.0), sG, 1)
        msmark = sG.n

        def cv1(bld):
            v = p.op("v", bld, sV, 1)
            p.wait("v", sV, v)
            return v

        p.wait("v", dS, constmark)
        cv1(lambda v: v.tensor_scalar(cc[:, 0:1], ks_c[:, :], -1.0, None, OP.mult))
        cv1(lambda v: v.tensor_max(cc[:, 0:1], cc[:, 0:1], ks_c[:, :]))
        cv1(lambda v: v.tensor_scalar(cc[:, 1:2], ks_c[:, :], 0.0, None, OP.is_ge))
        cv1(lambda v: v.tensor_scalar(cc[:, 1:2], cc[:, 1:2], 2.0, -1.0, OP.mult, OP.add))
        cv1(lambda v: v.tensor_mul(cc[:, 4:5], qs_c[:, :], cc[:, 1:2]))

        trlast = [0, 0]
        gend = [0, 0]
        xld = [0, 0]
        evdone = [0, 0]
        sqmm = [0, 0]
        vxdma = [0, 0]
        trev = [0, 0]
        r2evac = [0]
        holddma = [0, 0]
        lastsq = [0, 0]
        chain_s = [0, 0]
        chain_v = [0, 0]
        bcast_t = [0, 0]
        tailfree = [0, 0]
        tmul = [0, 0, 0, 0]

        def emit_chain(par, gidx, is_q):
            p.wait("v", sT, sqmm[1])
            p.wait("v", sA, chain_s[par])
            cp = p.op("v", lambda v, pr=par: v.tensor_copy(r2c[pr][:, :], pr2[:, :]), sV, 1)
            r2evac[0] = cp
            p.wait("s", sV, cp)
            m = p.op("s", lambda s, pr=par:
                     s.activation(t1c[pr][:, :], r2c[pr][:, :], AF.Ln,
                                  bias=np.float32(1e-30)), sA, 1)
            p.wait("s", sA, m)
            chain_s[par] = p.op("s", lambda s, pr=par:
                                s.activation(rcc[pr][:, :], t1c[pr][:, :], AF.Exp,
                                             scale=np.float32(0.5)), sA, 1)
            p.wait("v", sA, chain_s[par])
            cv1(lambda v, pr=par: v.tensor_scalar(t1c[pr][:, :], rcc[pr][:, :], EPS,
                                                  None, OP.add))
            cv1(lambda v, pr=par: v.reciprocal(recc[pr][:, :], t1c[pr][:, :]))
            p.wait("v", sT, bcast_t[par])
            if not is_q:
                cv1(lambda v, pr=par: v.scalar_tensor_tensor(Bbc[pr][:, :], rcc[pr][:, :],
                                                             cc[:, 0:1], recc[pr][:, :],
                                                             OP.mult, OP.mult))
                cv1(lambda v, pr=par: v.tensor_mul(brt[:, :], Bbc[pr][:, :], rcc[pr][:, :]))
                cv1(lambda v: v.reduce_max(cc[:, 3:4], brt[:, :], axis=AX.X))
                if gidx == 0:
                    p.wait("v", sG, msmark)
                chain_v[par] = cv1(lambda v: v.tensor_max(cc[:, 2:3], cc[:, 2:3], cc[:, 3:4]))
            else:
                if gidx == 0:
                    cv1(lambda v: v.tensor_scalar(cc[:, 5:6], qs_c[:, :], -1.0, None, OP.mult))
                    cv1(lambda v: v.tensor_max(cc[:, 5:6], cc[:, 5:6], qs_c[:, :]))
                    cv1(lambda v: v.tensor_mul(cc[:, 5:6], cc[:, 5:6], cc[:, 2:3]))
                cv1(lambda v, pr=par: v.scalar_tensor_tensor(Bbc[pr][:, :], rcc[pr][:, :],
                                                             cc[:, 4:5], recc[pr][:, :],
                                                             OP.mult, OP.mult))
                chain_v[par] = cv1(lambda v, pr=par, g2=gidx:
                                   v.scalar_tensor_tensor(c_own[:, 512 * g2:512 * g2 + 512],
                                                          r2c[pr][:, :], cc[:, 5:6],
                                                          recc[pr][:, :], OP.mult, OP.mult))

        def emit_tail(par, gidx, is_q):
            Dst = Qs if is_q else Ks
            for fc in range(8):
                reg = fc % 4
                if fc == 0:
                    p.wait("t", sV, chain_v[par])
                    p.wait("t", sA, evdone[0])
                    p.wait("t", sA, evdone[1])
                else:
                    p.wait("t", sV, tmul[reg])
                p.op("t", lambda t, pr=par, fc=fc, reg=reg:
                     t.matmul(pkq[:, 512 * reg:512 * reg + 512],
                              ebc[0:16, 128 * fc:128 * fc + 128], Bbc[pr][:, :],
                              start=True, stop=True, skip_group_check=True), sT, 1)
                mm = sT.n
                p.wait("v", sT, mm)
                if fc == 0:
                    p.wait("v", sA, lastsq[par])
                tmul[reg] = p.op("v", lambda v, pr=par, fc=fc, reg=reg:
                                 v.tensor_mul(hold[pr][:, 512 * fc:512 * fc + 512],
                                              hold[pr][:, 512 * fc:512 * fc + 512],
                                              pkq[:, 512 * reg:512 * reg + 512]), sV, 1)
                p.wait("y", sV, tmul[reg])
                d0 = Dst[2 * fc, :, 512 * gidx:512 * gidx + 512]
                d1 = Dst[2 * fc + 1, :, 512 * gidx:512 * gidx + 512]
                p.op("y", lambda g, pr=par, fc=fc, d=d0:
                     g.dma_start(d, hold[pr][0:64, 512 * fc:512 * fc + 512]), dS, 16)
                holddma[par] = p.op("y", lambda g, pr=par, fc=fc, d=d1:
                                    g.dma_start(d, hold[pr][64:128, 512 * fc:512 * fc + 512]),
                                    dS, 16)
            bcast_t[par] = sT.n
            tailfree[0] = max(tmul[0], tmul[1])
            tailfree[1] = max(tmul[2], tmul[3])

        gi = 0
        for is_q, n_groups in ((False, NG), (True, NQG)):
            for gidx in range(n_groups):
                par = gi % 2
                Xsrc = XQ if is_q else XKV
                wTx = wT_q if is_q else wT_k
                Dst = Qs if is_q else Ks
                prx = pr2

                p.wait("g", sT, trlast[par])
                src = Xsrc[512 * gidx:512 * gidx + 512, :].rearrange("(j p) d -> p j d", p=128)
                xdst = xb[par][:, :].rearrange("p (j d) -> p j d", d=D)
                xld[par] = p.op("g", lambda g, s=src, d=xdst: g.dma_start(d, s), dG, 16)

                p.wait("t", dG, xld[par])
                first_ev = True
                for dc in range(8):
                    hb = dc % 2
                    p.wait("t", sV, trev[hb])
                    for j in range(4):
                        p.op("t", lambda t, pr=par, dc=dc, j=j, hb=hb:
                             t.matmul(ptr[:, 512 * hb + 128 * j:512 * hb + 128 * j + 128],
                                      xb[pr][:, 1024 * j + 128 * dc:1024 * j + 128 * dc + 128],
                                      idb[:, :], start=True, stop=True, skip_group_check=True),
                             sT if j == 3 else None, 1)
                    tmark = sT.n
                    p.wait("v", sT, tmark)
                    if first_ev:
                        p.wait("v", sT, gend[par])
                        first_ev = False
                    trev[hb] = p.op("v", lambda v, pr=par, dc=dc, hb=hb:
                                    v.tensor_copy(xfm[pr][:, 512 * dc:512 * dc + 512],
                                                  ptr[:, 512 * hb:512 * hb + 512]), sV, 1)
                trlast[par] = sT.n

                pend_e16 = []
                for fp2 in range(4):
                    sl2 = fp2 % 2
                    p.wait("t", sA, evdone[sl2])
                    if fp2 < 2:
                        p.wait("t", sV, tailfree[sl2])
                    if fp2 == 0:
                        p.wait("t", sV, trev[0])
                        p.wait("t", sV, trev[1])
                        p.wait("t", sV, r2evac[0])
                    for dc in range(8):
                        for hb in range(2):
                            fc = 2 * fp2 + hb
                            p.op("t", lambda t, pr=par, fc=fc, dc=dc, hb=hb, sl2=sl2, w=wTx:
                                 t.matmul(pkq[:, 1024 * sl2 + 512 * hb:1024 * sl2 + 512 * hb + 512],
                                          w[:, 1024 * dc + 128 * fc:1024 * dc + 128 * fc + 128],
                                          xfm[pr][:, 512 * dc:512 * dc + 512],
                                          start=(dc == 0), stop=(dc == 7),
                                          skip_group_check=True),
                                 sT if (dc == 7 and hb == 1) else None, 1)
                    mmk = sT.n
                    for (fcp, slp, sqd) in pend_e16:
                        p.wait("t", sA, sqd)
                        for hb2 in range(2):
                            sqmm[slp] = p.op(
                                "t", lambda t, p2=slp, fc=fcp + hb2, hb2=hb2, pr_=prx:
                                t.matmul(pr_[:, :], e16[:, 16 * fc:16 * fc + 16],
                                         ksq[p2][:, 512 * hb2:512 * hb2 + 512],
                                         start=(fc == 0), stop=(fc == 7),
                                         skip_group_check=True), sT, 1)
                    pend_e16 = []
                    p.wait("s", sT, mmk)
                    if fp2 == 0:
                        p.wait("s", dS, holddma[par])
                    kevac_ = p.op("s", lambda s, pr=par, fp2=fp2, sl2=sl2:
                                  s.activation(hold[pr][:, 1024 * fp2:1024 * fp2 + 1024],
                                               pkq[:, 1024 * sl2:1024 * sl2 + 1024],
                                               AF.Silu), sA, 1)
                    evdone[sl2] = kevac_
                    p.wait("s", sT, sqmm[sl2])
                    sqdone = p.op("s", lambda s, pr=par, fp2=fp2, p2=sl2:
                                  s.activation(ksq[p2][:, :],
                                               hold[pr][:, 1024 * fp2:1024 * fp2 + 1024],
                                               AF.Square), sA, 1)
                    lastsq[par] = sqdone
                    pend_e16.append((2 * fp2, sl2, sqdone))

                flush_e16 = pend_e16
                pend_e16 = []

                if not is_q:
                    for j in range(4):
                        pv_ = j % 2
                        sl2 = j % 2
                        p.wait("t", sA, evdone[sl2])
                        p.wait("t", dS, vxdma[pv_])
                        for dc in range(8):
                            for half in range(2):
                                p.op("t", lambda t, pr=par, j=j, dc=dc, hf=half, sl2=sl2:
                                     t.matmul(pkq[:, 1024 * sl2 + 512 * hf:1024 * sl2 + 512 * hf + 512],
                                              xfm[pr][:, 512 * dc + 128 * j:512 * dc + 128 * j + 128],
                                              wT_v[:, 1024 * dc + 512 * hf:1024 * dc + 512 * hf + 512],
                                              start=(dc == 0), stop=(dc == 7),
                                              skip_group_check=True),
                                     sT if (dc == 7 and half == 1) else None, 1)
                        vmm = sT.n
                        if flush_e16:
                            for (fcp, slp, sqd) in flush_e16:
                                p.wait("t", sA, sqd)
                                for hb2 in range(2):
                                    sqmm[slp] = p.op(
                                        "t", lambda t, p2=slp, fc=fcp + hb2, hb2=hb2, pr_=prx:
                                        t.matmul(pr_[:, :], e16[:, 16 * fc:16 * fc + 16],
                                                 ksq[p2][:, 512 * hb2:512 * hb2 + 512],
                                                 start=(fc == 0), stop=(fc == 7),
                                                 skip_group_check=True), sT, 1)
                            flush_e16 = []
                            emit_chain(par, gidx, False)
                        p.wait("s", sT, vmm)
                        p.wait("s", dS, vxdma[pv_])
                        dr = vx[pv_][:, :].rearrange("p (h j2) -> p h j2", j2=65)
                        evdone[sl2] = p.op("s", lambda s, pv=pv_, dr=dr, sl2=sl2:
                                           s.activation(dr[:, :, 0:64],
                                                        pkq[:, 1024 * sl2:1024 * sl2 + 1024]
                                                        .rearrange("p (h f) -> p h f", f=64),
                                                        AF.Silu), sA, 1)
                        kt = 4 * gidx + j
                        p.wait("y", sA, evdone[sl2])
                        vxdma[pv_] = p.op("y", lambda g, pv=pv_, kt=kt:
                                          g.dma_start(Vs[kt, :, :], vx[pv][:, :]), dS, 16)
                    emit_tail(par, gidx, False)
                else:
                    for (fcp, slp, sqd) in flush_e16:
                        p.wait("t", sA, sqd)
                        for hb2 in range(2):
                            sqmm[slp] = p.op(
                                "t", lambda t, p2=slp, fc=fcp + hb2, hb2=hb2, pr_=prx:
                                t.matmul(pr_[:, :], e16[:, 16 * fc:16 * fc + 16],
                                         ksq[p2][:, 512 * hb2:512 * hb2 + 512],
                                         start=(fc == 0), stop=(fc == 7),
                                         skip_group_check=True), sT, 1)
                    flush_e16 = []
                    emit_chain(par, gidx, True)
                    emit_tail(par, gidx, True)
                gend[par] = sT.n
                gi += 1
        p.wait("g", dG, dG.n)
        p.wait("y", dS, dS.n)
        p.run(nc)
    SW.close()

    attn = S34.enter_context(nc.sbuf_tensor("attn", [128, 8 * NQ], BF16, side="right"))
    l_all = S34.enter_context(nc.sbuf_tensor("l_all", [H, NQ], F32, side="right"))
    with contextlib.ExitStack() as L:
        p = Ph("attn")
        ksb = [L.enter_context(nc.sbuf_tensor(f"ksb{i}", [65, NB], BF16)) for i in range(2)]
        vsb = [L.enter_context(nc.sbuf_tensor(f"vsb{i}", [128, NKT * 65], BF16)) for i in range(2)]
        qsb = [L.enter_context(nc.sbuf_tensor(f"qsb{i}", [65, NQ // 2], BF16)) for i in range(2)]
        pts3 = L.enter_context(nc.sbuf_tensor("pts3", [128, 3072], BF16))
        lst = L.enter_context(nc.sbuf_tensor("lst", [1, NQ], F32))
        mq = L.enter_context(nc.sbuf_tensor("mq", [128, NKT * 128], BF16))
        pst = L.enter_context(nc.psum_tensor("pst3", [128, 3072], F32))
        ppv = L.enter_context(nc.psum_tensor("ppv3", [65, 1024], F32))

        p.op("y", lambda y: y.dma_start(mq[:, :], MQ[:, :, :].rearrange("j p q -> p j q")),
             dS, 16)
        for i in range(2):
            p.op("g", lambda g, i=i: g.memset(ksb[i][64:65, :], -1.0), sG, 1)
        msks = sG.n

        sweep_done = [0, 0]
        exp_done = [0, 0, 0]
        pv_done = [0, 0, 0]
        pv_evac = {}
        ldma_prev = 0
        lds_a = [0] * 33
        ldg_a = [0] * 33

        items = [(P_, h) for P_ in range(2) for h in range(H)]

        def emit_loads(it2):
            P2, h2 = items[it2]
            par2 = it2 % 2
            J2 = 32 if P2 == 0 else NKT
            p.wait("y", sV, sweep_done[par2])
            p.op("y", lambda y, h=h2, pr=par2, J=J2:
                 y.dma_start(ksb[pr][0:64, 0:128 * J], Ks[h, :, 0:128 * J]), dS, 16)
            p.op("y", lambda y, h=h2, pr=par2, P_=P2:
                 y.dma_start(qsb[pr][0:64, :], Qs[h, :, 1024 * P_:1024 * P_ + 1024]),
                 dS, 16)
            vsrc = Vs[0:J2, :, 65 * h2:65 * h2 + 65].rearrange("k p j -> p k j")
            vdst3 = vsb[par2][:, 0:65 * J2].rearrange("p (k j) -> p k j", j=65)
            lds_a[it2] = p.op("y", lambda y, s=vsrc, d=vdst3: y.dma_start(d, s), dS, 16)
            p.wait("g", sV, sweep_done[par2])
            ldg_a[it2] = p.op("g", lambda g, h=h2, pr=par2, P_=P2:
                              g.dma_start(qsb[pr][64:65, :],
                                          c_own[h:h + 1, 1024 * P_:1024 * P_ + 1024]), dG, 16)

        emit_loads(0)
        emit_loads(1)
        pend_exp = None
        for it, (P_, h) in enumerate(items):
            par = it % 2
            JP = 32 if P_ == 0 else NKT
            p.wait("t", dS, lds_a[it])
            p.wait("t", dG, ldg_a[it])
            if it < 2:
                p.wait("t", sG, msks)

            def emit_pv(j):
                bj = j % 3
                p.wait("t", sA, exp_done[bj])
                s0 = j // 4
                slo = max(s0, 8 * P_) - 8 * P_
                for bl in range(slo // 4, 2):
                    sa = max(slo, 4 * bl)
                    n = (4 * bl + 4 - sa) * 128
                    pcol = 128 * sa
                    first = (j == 0)
                    bg = 2 * P_ + bl
                    lastj = (j == CS[4 * bg + 3] - 1)
                    if first and it > 1:
                        p.wait("t", sV, pv_evac.get((it - 2, bl), 0))
                    p.op("t", lambda t, pr=par, j=j, n=n, pcol=pcol,
                         first=first, lastj=lastj, bj=bj:
                         t.matmul(ppv[:, pcol:pcol + n],
                                  vsb[pr][:, 65 * j:65 * j + 65],
                                  pts3[:, 1024 * bj + pcol:1024 * bj + pcol + n],
                                  start=first, stop=lastj,
                                  skip_group_check=True), sT, 1)
                    pv_done[bj] = sT.n
                    if lastj:
                        pvm = sT.n
                        p.wait("v", sT, pvm)
                        sg = 8 * P_ + 4 * bl
                        bc = 512 * bl
                        dsta = attn[64 * (h % 2):64 * (h % 2) + 64,
                                    (h // 2) * NQ + 128 * sg:(h // 2) * NQ + 128 * sg + 512]
                        p.op("v", lambda v, d=dsta, bc=bc:
                             v.tensor_copy(d, ppv[0:64, bc:bc + 512]), sV, 1)
                        if bl == 0:
                            p.wait("v", dG, ldma_prev)
                        pv_evac[(it, bl)] = p.op(
                            "v", lambda v, bc=bc, sg=sg:
                            v.tensor_copy(lst[0:1, 128 * sg:128 * sg + 512],
                                          ppv[64:65, bc:bc + 512]), sV, 1)

            for j in range(JP):
                bj = j % 3
                s0 = j // 4
                slo = max(s0, 8 * P_) - 8 * P_
                p.wait("t", sA, exp_done[bj])
                s = slo
                while s < 8:
                    s2 = min((s // 4) * 4 + 3, 7)
                    n = (s2 - s + 1) * 128
                    tail = (s <= s0 - 8 * P_ <= s2) and (s0 >= 8 * P_)
                    col = 1024 * bj + 128 * s
                    is_last = (s2 == 7) and not tail
                    p.op("t", lambda t, pr=par, j=j, col=col, n=n, s=s:
                         t.matmul(pst[:, col:col + n],
                                  ksb[pr][0:65, 128 * j:128 * j + 128],
                                  qsb[pr][0:65, 128 * s:128 * s + n],
                                  start=True, stop=not tail,
                                  skip_group_check=True),
                         sT if is_last else None, 1)
                    if tail:
                        mcol = 1024 * bj + 128 * (s0 - 8 * P_)
                        is_last2 = (s2 == 7)
                        p.op("t", lambda t, j=j, mcol=mcol:
                             t.matmul(pst[:, mcol:mcol + 128], nbi[:, :],
                                      mq[:, 128 * j:128 * j + 128],
                                      start=False, stop=True,
                                      skip_group_check=True),
                             sT if is_last2 else None, 1)
                    s = s2 + 1
                stm = sT.n
                if bj == 0 and j < JP - 1:
                    pend_exp = (j, slo, stm)
                else:
                    if bj == 1:
                        lo = 128 * pend_exp[1]
                        hi = 2048
                        pvw = max(pv_done[0], pv_done[1])
                        banks = (0, 1)
                    elif bj == 2:
                        lo = 2048 + 128 * slo
                        hi = 3072
                        pvw = pv_done[2]
                        banks = (2,)
                    else:
                        lo = 128 * slo
                        hi = 1024
                        pvw = pv_done[0]
                        banks = (0,)
                    pend_exp = None
                    p.wait("s", sT, stm)
                    p.wait("s", sT, pvw)
                    eid = p.op("s", lambda s_, lo=lo, hi=hi:
                               s_.activation(pts3[:, lo:hi], pst[:, lo:hi],
                                             AF.Exp, bias=np.float32(TSHIFT)), sA, 1)
                    for b_ in banks:
                        exp_done[b_] = eid
                if j >= 2:
                    emit_pv(j - 2)
            emit_pv(JP - 2)
            emit_pv(JP - 1)
            p.wait("g", sV, pv_evac[(it, 1)])
            ldma_prev = p.op("g", lambda g, h=h, P_=P_:
                             g.dma_start(l_all[h:h + 1, 1024 * P_:1024 * P_ + 1024],
                                         lst[0:1, 1024 * P_:1024 * P_ + 1024]), dG, 16)
            sweep_done[par] = pv_evac[(it, 1)]
            if it + 2 < len(items):
                emit_loads(it + 2)
        p.wait("g", dG, dG.n)
        p.wait("y", dS, dS.n)
        p.run(nc)
    S23.close()

    y_fm = S45.enter_context(nc.sbuf_tensor("y_fm", [128, 8 * NQ], BF16))
    p_all = S45.enter_context(nc.sbuf_tensor("p_all", [128, NQT], F32))
    oh_all = S45.enter_context(nc.sbuf_tensor("oh_all", [128, NQT * 8], F32))
    with contextlib.ExitStack() as L:
        p = Ph("woln")
        wst = [L.enter_context(nc.sbuf_tensor(f"wst4{i}", [128, D], F32)) for i in range(2)]
        wca = [L.enter_context(nc.sbuf_tensor(f"wca4{i}", [128, D], BF16)) for i in range(2)]
        wT_o = L.enter_context(nc.sbuf_tensor("wT_o", [128, 8 * D], BF16))
        lrec = L.enter_context(nc.sbuf_tensor("lrec", [H, NQ], F32))
        lrecb = L.enter_context(nc.sbuf_tensor("lrecb", [H, NQ], BF16))
        xres = [L.enter_context(nc.sbuf_tensor(f"xres{i}", [128, D], F32)) for i in range(3)]
        y0 = [L.enter_context(nc.sbuf_tensor(f"y0_{i}", [128, D], F32)) for i in range(3)]
        y2 = [L.enter_context(nc.sbuf_tensor(f"y2_{i}", [128, D], F32)) for i in range(3)]
        ybf = [L.enter_context(nc.sbuf_tensor(f"ybf{i}", [128, D], BF16)) for i in range(3)]
        sqs = L.enter_context(nc.sbuf_tensor("sqs", [128, D], BF16))
        smc = L.enter_context(nc.sbuf_tensor("smc", [128, 8], F32))
        pwo = L.enter_context(nc.psum_tensor("pwo", [128, 3072], F32))
        ptr4 = L.enter_context(nc.psum_tensor("ptr4", [128, 1024], F32))

        if tap_attn is not None:
            p.op("g", lambda g: g.dma_start(tap_lall[:, :], l_all[:, :]), dG, 16)
        ldmark = [0, 0]
        castmark = [0, 0]
        trmark = [0, 0]
        evmark = [0, 0]
        for fc in range(8):
            par = fc % 2
            p.wait("y", sG, castmark[par])
            ldmark[par] = p.op("y", lambda y, fc=fc, pr=par:
                               y.dma_start(wst[pr][:, :], WO[128 * fc:128 * fc + 128, :]), dS, 16)
            p.wait("g", dS, ldmark[par])
            p.wait("g", sT, trmark[par])
            castmark[par] = p.op("g", lambda g, pr=par:
                                 g.tensor_copy(wca[pr][:, :], wst[pr][:, :]), sG, 1)
            p.wait("t", sG, castmark[par])
            for half in range(2):
                p.wait("t", sV, evmark[half])
                for k4 in range(4):
                    dc = half * 4 + k4
                    p.op("t", lambda t, pr=par, dc=dc, k4=k4, hh=half:
                         t.matmul(ptr4[:, 512 * hh + 128 * k4:512 * hh + 128 * k4 + 128],
                                  wca[pr][:, 128 * dc:128 * dc + 128], idb[:, :],
                                  start=True, stop=True, skip_group_check=True),
                         sT if k4 == 3 else None, 1)
                tmark = sT.n
                dst = wT_o[:, :].rearrange("p (dc f) -> p dc f", f=D)[
                    :, half * 4:half * 4 + 4, 128 * fc:128 * fc + 128]
                src_ = ptr4[:, 512 * half:512 * half + 512].rearrange("p (a b) -> p a b", b=128)
                p.wait("v", sT, tmark)
                evmark[half] = p.op("v", lambda v, d=dst, s=src_: v.tensor_copy(d, s), sV, 1)
            trmark[par] = sT.n

        la = p.op("v", lambda v: v.reciprocal(lrec[:, :], l_all[:, :]), sV, 1)
        p.wait("v", sV, la)
        lr = p.op("v", lambda v: v.tensor_copy(lrecb[:, :], lrec[:, :]), sV, 1)
        p.wait("v", sV, lr)
        p.wait("t", sV, lr)
        nmark = [evmark[0], evmark[1]]
        for u in range(32):
            ch, g2 = u // 4, u % 4
            sl_ = u % 2
            p.wait("t", sV, nmark[sl_])
            bm = p.op("t", lambda t, ch=ch, g2=g2, sl_=sl_:
                      t.matmul(ptr4[:, 512 * sl_:512 * sl_ + 512],
                               ebc[0:16, 128 * ch:128 * ch + 128],
                               lrecb[0:16, 512 * g2:512 * g2 + 512], start=True, stop=True,
                               skip_group_check=True), sT, 1)
            p.wait("v", sT, bm)
            dsta = attn[:, 2048 * ch + 512 * g2:2048 * ch + 512 * g2 + 512]
            nmark[sl_] = p.op("v", lambda v, d=dsta, sl_=sl_:
                              v.tensor_mul(d, d, ptr4[:, 512 * sl_:512 * sl_ + 512]), sV, 1)
        normdone = max(nmark)
        if tap_attn is not None:
            p.wait("g", sV, normdone)
            p.op("g", lambda g: g.dma_start(tap_attn[:, :], attn[:, :]), dG, 16)
        p.wait("t", sV, normdone)

        h1mark = [0, 0, 0]
        ymark = [0, 0, 0]
        ydma = [0, 0, 0]
        ybfm = [0, 0, 0]
        ytr = [0, 0, 0]
        xload = [0, 0, 0]
        tr4ev = [normdone, normdone]
        for qt in range(NQT):
            sl = qt % 3
            if qt == 0:
                p.wait("t", sV, max(evmark))
            p.wait("t", sV, h1mark[sl])
            for dc in range(8):
                for half in range(2):
                    p.op("t", lambda t, qt=qt, dc=dc, hf=half, sl=sl:
                         t.matmul(pwo[:, 1024 * sl + 512 * hf:1024 * sl + 512 * hf + 512],
                                  attn[:, 2048 * dc + 128 * qt:2048 * dc + 128 * qt + 128],
                                  wT_o[:, 1024 * dc + 512 * hf:1024 * dc + 512 * hf + 512],
                                  start=(dc == 0), stop=(dc == 7), skip_group_check=True),
                         sT if (dc == 7 and half == 1) else None, 1)
            womm = sT.n
            p.wait("y", sV, h1mark[sl])
            xload[sl] = p.op("y", lambda y, qt=qt, sl=sl:
                             y.dma_start(xres[sl][:, :], XQ[128 * qt:128 * qt + 128, :]), dS, 16)
            p.wait("v", sT, womm)
            p.wait("v", dS, xload[sl])
            p.wait("v", dG, ydma[sl])
            vv = p.op("v", lambda v, sl=sl:
                      v.tensor_add(y0[sl][:, :], xres[sl][:, :],
                                   pwo[:, 1024 * sl:1024 * sl + 1024]), sV, 1)
            h1mark[sl] = vv
            p.wait("s", sV, vv)
            sqa = p.op("s", lambda s, sl=sl:
                       s.activation(sqs[:, :], y0[sl][:, :], AF.Square,
                                    accum_out=smc[:, 2:3]), sA, 1)
            p.wait("v", sV, vv)
            vv = p.op("v", lambda v, sl=sl:
                      v.reduce_sum(smc[:, 0:1], y0[sl][:, :], axis=AX.X, negate=True), sV, 1)
            p.wait("v", sV, vv)
            vv = p.op("v", lambda v: v.tensor_scalar(smc[:, 1:2], smc[:, 0:1], RD, None,
                                                     OP.mult), sV, 1)
            p.wait("v", sA, sqa)
            p.wait("v", sV, vv)
            vv = p.op("v", lambda v: v.tensor_scalar(smc[:, 5:6], smc[:, 2:3], RD, None,
                                                     OP.mult), sV, 1)
            p.wait("v", sV, vv)
            vv = p.op("v", lambda v: v.tensor_mul(smc[:, 6:7], smc[:, 1:2], smc[:, 1:2]), sV, 1)
            p.wait("v", sV, vv)
            vv = p.op("v", lambda v: v.tensor_sub(smc[:, 5:6], smc[:, 5:6], smc[:, 6:7]), sV, 1)
            p.wait("s", sV, vv)
            aa = p.op("s", lambda s: s.activation(smc[:, 3:4], smc[:, 5:6], AF.Ln,
                                                  bias=np.float32(LNEPS)), sA, 1)
            p.wait("s", sA, aa)
            aa = p.op("s", lambda s: s.activation(smc[:, 4:5], smc[:, 3:4], AF.Exp,
                                                  scale=np.float32(-0.5)), sA, 1)
            p.wait("v", sA, aa)
            vv = p.op("v", lambda v, sl=sl:
                      v.tensor_scalar(y2[sl][:, :], y0[sl][:, :], smc[:, 1:2], smc[:, 4:5],
                                      OP.add, OP.mult), sV, 1)
            ymark[sl] = vv
            p.wait("g", sV, vv)
            ydma[sl] = p.op("g", lambda g, qt=qt, sl=sl:
                            g.dma_start(Ys[128 * qt:128 * qt + 128, :], y2[sl][:, :]), dG, 16)
            p.wait("v", sV, vv)
            p.wait("v", sT, ytr[sl])
            vv = p.op("v", lambda v, sl=sl: v.tensor_copy(ybf[sl][:, :], y2[sl][:, :]), sV, 1)
            ybfm[sl] = vv
            p.wait("t", sV, ybfm[sl])
            for hb in range(2):
                p.wait("t", sV, tr4ev[hb])
                for k4 in range(4):
                    dc = hb * 4 + k4
                    p.op("t", lambda t, sl=sl, dc=dc, k4=k4, hb=hb:
                         t.matmul(ptr4[:, 512 * hb + 128 * k4:512 * hb + 128 * k4 + 128],
                                  ybf[sl][:, 128 * dc:128 * dc + 128], idb[:, :],
                                  start=True, stop=True, skip_group_check=True),
                         sT if k4 == 3 else None, 1)
                tmark = sT.n
                dst = y_fm[:, :].rearrange("p (dc t) -> p dc t", t=NQ)[
                    :, hb * 4:hb * 4 + 4, 128 * qt:128 * qt + 128]
                src_ = ptr4[:, 512 * hb:512 * hb + 512].rearrange("p (a b) -> p a b", b=128)
                p.wait("v", sT, tmark)
                tr4ev[hb] = p.op("v", lambda v, d=dst, s=src_: v.tensor_copy(d, s), sV, 1)
            ytr[sl] = sT.n
        p.wait("g", dG, dG.n)
        p.wait("y", dS, dS.n)
        p.run(nc)
    S34.close()

    with contextlib.ExitStack() as L:
        p = Ph("ffn")
        ffna = L.enter_context(nc.sbuf_tensor("ffna", [128, NQT * D], BF16))
        iosb = L.enter_context(nc.sbuf_tensor("iosb", [128, D], F32))
        big = L.enter_context(nc.sbuf_tensor("big", [128, D], F32))
        wgb = L.enter_context(nc.sbuf_tensor("wgb", [128, 8 * D], BF16))
        web = [L.enter_context(nc.sbuf_tensor(f"web{i}", [128, 8 * D], BF16)) for i in range(2)]
        lg = [L.enter_context(nc.sbuf_tensor(f"lg{i}", [128, D], F32)) for i in range(2)]
        esc = L.enter_context(nc.sbuf_tensor("esc", [128, D], BF16))
        mku = L.enter_context(nc.sbuf_tensor("mku", [128, D], U8))
        tsel = L.enter_context(nc.sbuf_tensor("tsel", [128, D], F32))
        cnd = L.enter_context(nc.sbuf_tensor("cnd", [128, 16], F32))
        he = [L.enter_context(nc.sbuf_tensor(f"he{i}", [128, D], BF16)) for i in range(2)]
        yre = [L.enter_context(nc.sbuf_tensor(f"yre{i}", [128, D], F32)) for i in range(2)]
        h2 = [L.enter_context(nc.sbuf_tensor(f"h2_{i}", [128, D], F32)) for i in range(2)]
        osb = [L.enter_context(nc.sbuf_tensor(f"osb{i}", [128, D], F32)) for i in range(2)]
        sq5 = L.enter_context(nc.sbuf_tensor("sq5", [128, D], BF16))
        sm5 = L.enter_context(nc.sbuf_tensor("sm5", [128, 8], F32))
        pg = L.enter_context(nc.psum_tensor("pg5", [128, 2048], F32))
        pe_ = L.enter_context(nc.psum_tensor("pe5", [128, 2048], F32))

        iota_ld = p.op("y", lambda y: y.dma_start(iosb[:, :], IOTA[:, :]), dS, 16)
        big_ms = p.op("g", lambda g: g.memset(big[:, :], 30000.0), sG, 1)
        wgsrc = WG[:, :].rearrange("(c p) d -> p c d", p=128)
        wgd3 = wgb[:, :].rearrange("p (c d) -> p c d", d=D)
        wgld = p.op("g", lambda g, s=wgsrc, d=wgd3: g.dma_start(d, s), dG, 16)
        weld = [0, 0]
        wesrc0 = WE[0, :, :].rearrange("(c p) d -> p c d", p=128)
        wed0 = web[0][:, :].rearrange("p (c d) -> p c d", d=D)
        weld[0] = p.op("g", lambda g, s=wesrc0, d=wed0: g.dma_start(d, s), dG, 16)

        gevac = [0, 0]
        prev_exp = 0
        for qt in range(NQT):
            sl = qt % 2
            p.wait("t", dG, wgld)
            p.wait("t", sV, gevac[sl])
            for dc in range(8):
                for half in range(2):
                    p.op("t", lambda t, qt=qt, dc=dc, hf=half, sl=sl:
                         t.matmul(pg[:, 1024 * sl + 512 * hf:1024 * sl + 512 * hf + 512],
                                  y_fm[:, 2048 * dc + 128 * qt:2048 * dc + 128 * qt + 128],
                                  wgb[:, 1024 * dc + 512 * hf:1024 * dc + 512 * hf + 512],
                                  start=(dc == 0), stop=False, skip_group_check=True),
                         None)
            for half in range(2):
                p.op("t", lambda t, hf=half, sl=sl:
                     t.matmul(pg[:, 1024 * sl + 512 * hf:1024 * sl + 512 * hf + 512],
                              oneb[:, :], bgb[:, 512 * hf:512 * hf + 512],
                              start=False, stop=True, skip_group_check=True),
                     sT if half == 1 else None, 1)
            gmm = sT.n
            p.wait("v", sT, gmm)
            p.wait("v", sA, prev_exp)
            vv = p.op("v", lambda v, sl=sl:
                      v.reduce_max(sm5[:, 0:1], pg[:, 1024 * sl:1024 * sl + 1024],
                                   axis=AX.X, negate=True), sV, 1)
            p.wait("s", sV, vv)
            aa = p.op("s", lambda s, sl=sl:
                      s.activation(esc[:, :], pg[:, 1024 * sl:1024 * sl + 1024], AF.Exp,
                                   bias=sm5[:, 0:1], accum_out=sm5[:, 1:2]), sA, 1)
            prev_exp = aa
            p.wait("v", sA, aa)
            vv = p.op("v", lambda v, qt=qt:
                      v.reciprocal(p_all[:, qt:qt + 1], sm5[:, 1:2]), sV, 1)
            p.wait("v", sV, vv)
            vv = p.op("v", lambda v, sl=sl:
                      v.tensor_scalar(mku[:, :], pg[:, 1024 * sl:1024 * sl + 1024],
                                      sm5[:, 0:1], 0.0, OP.add, OP.is_ge), sV, 1)
            gevac[sl] = vv
            p.wait("v", sV, vv)
            if qt == 0:
                p.wait("v", dS, iota_ld)
                p.wait("v", sG, big_ms)
            vv = p.op("v", lambda v: v.select(tsel[:, :], mku[:, :], iosb[:, :], big[:, :]),
                      sV, 1)
            p.wait("v", sV, vv)
            vv = p.op("v", lambda v:
                      v.tensor_reduce(cnd[:, 0:8],
                                      tsel[:, :].rearrange("p (g e) -> p e g", e=8),
                                      axis=AX.X, op=OP.min), sV, 1)
            p.wait("v", sV, vv)
            vv = p.op("v", lambda v: v.tensor_reduce(sm5[:, 2:3], cnd[:, 0:8], axis=AX.X,
                                                     op=OP.min), sV, 1)
            p.wait("v", sV, vv)
            vv = p.op("v", lambda v, qt=qt:
                      v.tensor_scalar(oh_all[:, 8 * qt:8 * qt + 8], cnd[:, 0:8], sm5[:, 2:3],
                                      None, OP.is_le), sV, 1)
            p.wait("v", sV, vv)
        gatedone_t = sT.n
        gatedone_v = sV.n

        yrel = [0, 0]
        h2m = [0, 0]
        outd = [0, 0]

        def emit_final(qt):
            sl = qt % 2
            sl = qt % 2
            p.wait("y", sV, h2m[sl])
            yrel[sl] = p.op("y", lambda y, qt=qt, sl=sl:
                            y.dma_start(yre[sl][:, :], Ys[128 * qt:128 * qt + 128, :]), dS, 16)
            p.wait("v", dS, yrel[sl])
            p.wait("v", sV, acc_mark[qt])
            p.wait("v", dS, outd[sl])
            vv = p.op("v", lambda v, qt=qt, sl=sl:
                      v.scalar_tensor_tensor(h2[sl][:, :], ffna[:, D * qt:D * qt + D],
                                             p_all[:, qt:qt + 1], yre[sl][:, :],
                                             OP.mult, OP.add), sV, 1)
            h2m[sl] = vv
            p.wait("s", sV, vv)
            sqa = p.op("s", lambda s, sl=sl:
                       s.activation(sq5[:, :], h2[sl][:, :], AF.Square,
                                    accum_out=sm5[:, 6:7]), sA, 1)
            p.wait("v", sV, vv)
            vv = p.op("v", lambda v, sl=sl:
                      v.reduce_sum(sm5[:, 4:5], h2[sl][:, :], axis=AX.X, negate=True), sV, 1)
            p.wait("v", sV, vv)
            vv = p.op("v", lambda v: v.tensor_scalar(sm5[:, 5:6], sm5[:, 4:5], RD, None,
                                                     OP.mult), sV, 1)
            p.wait("v", sA, sqa)
            p.wait("v", sV, vv)
            vv = p.op("v", lambda v: v.tensor_scalar(sm5[:, 7:8], sm5[:, 6:7], RD, None,
                                                     OP.mult), sV, 1)
            p.wait("v", sV, vv)
            vv = p.op("v", lambda v: v.tensor_mul(sm5[:, 4:5], sm5[:, 5:6], sm5[:, 5:6]), sV, 1)
            p.wait("v", sV, vv)
            vv = p.op("v", lambda v: v.tensor_sub(sm5[:, 7:8], sm5[:, 7:8], sm5[:, 4:5]), sV, 1)
            p.wait("s", sV, vv)
            aa = p.op("s", lambda s: s.activation(sm5[:, 3:4], sm5[:, 7:8], AF.Ln,
                                                  bias=np.float32(LNEPS)), sA, 1)
            p.wait("s", sA, aa)
            aa = p.op("s", lambda s: s.activation(sm5[:, 2:3], sm5[:, 3:4], AF.Exp,
                                                  scale=np.float32(-0.5)), sA, 1)
            p.wait("v", sA, aa)
            vv = p.op("v", lambda v, sl=sl:
                      v.tensor_scalar(osb[sl][:, :], h2[sl][:, :], sm5[:, 5:6], sm5[:, 2:3],
                                      OP.add, OP.mult), sV, 1)
            p.wait("y", sV, vv)
            outd[sl] = p.op("y", lambda y, qt=qt, sl=sl:
                            y.dma_start(OUT[128 * qt:128 * qt + 128, :], osb[sl][:, :]), dS, 16)

        relu_done = [0, 0]
        stt_done = [0, 0]
        acc_mark = {}
        welast = [0, gatedone_t]
        for ei in range(8):
            parw = ei % 2
            if ei < 7:
                nxt = (ei + 1) % 2
                p.wait("g", sT, welast[nxt])
                wes = WE[ei + 1, :, :].rearrange("(c p) d -> p c d", p=128)
                wedn = web[nxt][:, :].rearrange("p (c d) -> p c d", d=D)
                weld[nxt] = p.op("g", lambda g, s=wes, d=wedn: g.dma_start(d, s), dG, 16)
            for qt in range(NQT):
                u = ei * NQT + qt
                sl = u % 2
                p.wait("t", dG, weld[parw])
                p.wait("t", sA, relu_done[sl])
                for dc in range(8):
                    for half in range(2):
                        p.op("t", lambda t, qt=qt, dc=dc, hf=half, sl=sl, pw=parw:
                             t.matmul(pe_[:, 1024 * sl + 512 * hf:1024 * sl + 512 * hf + 512],
                                      y_fm[:, 2048 * dc + 128 * qt:2048 * dc + 128 * qt + 128],
                                      web[pw][:, 1024 * dc + 512 * hf:1024 * dc + 512 * hf + 512],
                                      start=(dc == 0), stop=False, skip_group_check=True),
                             None)
                for half in range(2):
                    p.op("t", lambda t, hf=half, sl=sl, ei=ei:
                         t.matmul(pe_[:, 1024 * sl + 512 * hf:1024 * sl + 512 * hf + 512],
                                  oneb[:, :], beb[:, 1024 * ei + 512 * hf:1024 * ei + 512 * hf + 512],
                                  start=False, stop=True, skip_group_check=True),
                         sT if half == 1 else None, 1)
                emm = sT.n
                welast[parw] = emm
                p.wait("s", sT, emm)
                p.wait("s", sV, stt_done[sl])
                relu_done[sl] = p.op("s", lambda s, sl=sl:
                                     s.activation(he[sl][:, :], pe_[:, 1024 * sl:1024 * sl + 1024],
                                                  AF.Relu), sA, 1)
                p.wait("v", sA, relu_done[sl])
                p.wait("v", sV, acc_mark.get(qt, 0))
                dstf = ffna[:, D * qt:D * qt + D]
                ohap = oh_all[:, 8 * qt + ei:8 * qt + ei + 1]
                if ei == 0:
                    vv = p.op("v", lambda v, d=dstf, sl=sl, o=ohap:
                              v.tensor_scalar(d, he[sl][:, :], o, None, OP.mult), sV, 1)
                else:
                    vv = p.op("v", lambda v, d=dstf, sl=sl, o=ohap:
                              v.scalar_tensor_tensor(d, he[sl][:, :], o, d, OP.mult, OP.add),
                              sV, 1)
                stt_done[sl] = vv
                acc_mark[qt] = vv
                if ei == 7:
                    emit_final(qt)

        for tn, tap in taps.items():
            p.op("g", lambda g, d=tap, s=scr[tn]: g.dma_start(d, s), dG, 16)
        p.wait("g", dG, dG.n)
        p.wait("y", dS, dS.n)
        p.run(nc)
    S45.close()
    top.close()
    return nc


def _host_inputs(inputs):
    x = np.ascontiguousarray(np.asarray(inputs["L0_attn_x"]), dtype=np.float32)
    cores = []
    tri = np.zeros((128, 128), dtype=bfnp)
    tri[np.arange(128)[:, None] > np.arange(128)[None, :]] = 1.0
    e16 = np.zeros((8, 128, 16), dtype=bfnp)
    for c in range(8):
        for pp in range(128):
            e16[c, pp, 2 * c + pp // 64] = 1.0
    ebc = np.transpose(e16, (0, 2, 1)).copy()
    f32a = lambda a: np.ascontiguousarray(np.asarray(a), dtype=np.float32)
    consts = {
        "wq": f32a(inputs["L0_attn_Wq"]), "wk": f32a(inputs["L0_attn_Wk"]),
        "wv": f32a(inputs["L0_attn_Wv"]), "wo": f32a(inputs["L0_attn_Wo"]),
        "wg": f32a(inputs["L0_ffn_Wg"]), "we": f32a(inputs["L0_ffn_We"]),
        "bg": f32a(inputs["L0_ffn_bg"]).reshape(1, D),
        "be": f32a(inputs["L0_ffn_be"]).reshape(1, 8 * D),
        "qscale": f32a(inputs["L0_attn_q_scale"]).reshape(H, 1),
        "kscale": f32a(inputs["L0_attn_k_scale"]).reshape(H, 1),
        "identb": np.eye(128, dtype=bfnp),
        "identf": np.eye(128, dtype=np.float32),
        "e16": e16, "ebc": ebc,
        "iota": np.broadcast_to(np.arange(D, dtype=np.float32), (128, D)).copy(),
        "negbigI": (np.eye(128) * -1.0e34).astype(bfnp),
        "onesb": np.ones((1, 128), dtype=bfnp),
    }
    for core in range(8):
        b, jc = core // 4, core % 4
        qtiles = [4 * s + (jc + s) % 4 for s in range(NQT)]
        xq = np.concatenate([x[b, 128 * t:128 * t + 128, :] for t in qtiles], axis=0)
        maskq = np.zeros((NKT, 128, 128), dtype=bfnp)
        for s in range(NQT):
            sig = (jc + s) % 4
            for r in range(4):
                if r == sig:
                    maskq[4 * s + r] = tri
                elif r > sig:
                    maskq[4 * s + r] = 1.0
        m = dict(consts)
        m["xkv"] = np.ascontiguousarray(x[b])
        m["xq"] = np.ascontiguousarray(xq)
        m["maskq"] = maskq
        cores.append(m)
    return cores


def kernel(**inputs):
    nc = build_nc()
    in_maps = _host_inputs(inputs)
    res = run_bass_kernel_spmd(nc, in_maps, core_ids=list(range(8)))
    if DEBUG_TAPS:
        global LAST_TAPS
        LAST_TAPS = [{k: v for k, v in r.items() if k.startswith("tap_")}
                     for r in res.results]
    out = np.empty((2, NB, D), dtype=np.float32)
    for core in range(8):
        b, jc = core // 4, core % 4
        o = res.results[core]["out"]
        for s in range(NQT):
            t = 4 * s + (jc + s) % 4
            out[b, 128 * t:128 * t + 128, :] = o[128 * s:128 * s + 128, :]
    return out


# version 17
# speedup vs baseline: 1.2275x; 1.2275x over previous
import contextlib
import numpy as np
import ml_dtypes
import concourse.bass as bass
import concourse.mybir as mybir
from concourse.bass_utils import run_bass_kernel_spmd
from concourse.alu_op_type import AluOpType as OP

F32 = mybir.dt.float32
BF16 = mybir.dt.bfloat16
U8 = mybir.dt.uint8
AF = mybir.ActivationFunctionType
AX = mybir.AxisListType
bfnp = ml_dtypes.bfloat16

H, HD, D = 16, 64, 1024
NB, NKT, NG = 8192, 64, 16
NQ, NQT, NQG = 2048, 16, 4
CS = [4 * s + 4 for s in range(NQT)]
TSHIFT = 60.0
EPS = 1e-6
LNEPS = 1e-5
RD = 1.0 / D

DEBUG_TAPS = []


class Sem:
    def __init__(self, h):
        self.h = h
        self.n = 0


class Ph:
    def __init__(self, name):
        self.name = name
        self.q = {"t": [], "s": [], "v": [], "y": [], "g": []}

    def wait(self, ek, sem, val):
        if val > 0:
            self.q[ek].append(lambda e, s=sem.h, v=val: e.wait_ge(s, v))

    def op(self, ek, bld, sem=None, k=1):
        if sem is None:
            self.q[ek].append(lambda e, b=bld: b(e))
            return None
        sem.n += k
        v = sem.n
        self.q[ek].append(lambda e, b=bld, s=sem.h, kk=k: b(e).then_inc(s, kk))
        return v

    def run(self, nc):
        with nc.Block(self.name) as blk:
            for attr, key in (("sync", "y"), ("gpsimd", "g"), ("tensor", "t"),
                              ("scalar", "s"), ("vector", "v")):
                fns = self.q[key]
                if fns:
                    getattr(blk, attr)(lambda e, fns=fns: [f(e) for f in fns])


def build_nc():
    nc = bass.Bass("TRN2", target_bir_lowering=False)

    def din(name, shape, dt):
        return nc.dram_tensor(name, list(shape), dt, kind="ExternalInput").ap()

    XKV = din("xkv", [NB, D], F32)
    XQ = din("xq", [NQ, D], F32)
    WQ = din("wq", [D, D], F32)
    WK = din("wk", [D, D], F32)
    WV = din("wv", [D, D], F32)
    WO = din("wo", [D, D], F32)
    WG = din("wg", [D, D], F32)
    WE = din("we", [8, D, D], F32)
    BG = din("bg", [1, D], F32)
    BE = din("be", [1, 8 * D], F32)
    QS = din("qscale", [H, 1], F32)
    KSc = din("kscale", [H, 1], F32)
    IDB = din("identb", [128, 128], BF16)
    IDF = din("identf", [128, 128], F32)
    E16 = din("e16", [8, 128, 16], BF16)
    EBC = din("ebc", [8, 16, 128], BF16)
    IOTA = din("iota", [128, D], F32)
    NBI = din("negbigI", [128, 128], BF16)
    ONE = din("onesb", [1, 128], BF16)
    MQ = din("maskq", [NKT, 128, 128], BF16)

    OUT = nc.dram_tensor("out", [NQ, D], F32, kind="ExternalOutput").ap()

    Ks = nc.dram_tensor("Ks", [H, HD, NB], BF16).ap()
    Vs = nc.dram_tensor("Vs", [NKT, 128, H * 65], BF16).ap()
    Qs = nc.dram_tensor("Qs", [H, HD, NQ], BF16).ap()
    Ys = nc.dram_tensor("Ys", [NQ, D], F32).ap()
    scr = {"Ks": Ks, "Vs": Vs, "Qs": Qs, "Ys": Ys, "attn": Ys}
    taps = {}
    for tn in DEBUG_TAPS:
        if tn == "attn":
            continue
        dt = F32 if tn == "Ys" else BF16
        taps[tn] = nc.dram_tensor("tap_" + tn, list(scr[tn].shape), dt,
                                  kind="ExternalOutput").ap()
    tap_attn = tap_lall = None
    if "attn" in DEBUG_TAPS:
        taps.pop("attn", None)
        tap_attn = nc.dram_tensor("tap_attn", [128, 8 * NQ], BF16,
                                  kind="ExternalOutput").ap()
        tap_lall = nc.dram_tensor("tap_lall", [H, NQ], F32,
                                  kind="ExternalOutput").ap()

    top = contextlib.ExitStack()
    E = top.enter_context

    dS = Sem(E(nc.semaphore("dS")))
    dG = Sem(E(nc.semaphore("dG")))
    sT = Sem(E(nc.semaphore("sT")))
    sA = Sem(E(nc.semaphore("sA")))
    sV = Sem(E(nc.semaphore("sV")))
    sG = Sem(E(nc.semaphore("sG")))

    idb = E(nc.sbuf_tensor("idb", [128, 128], BF16))
    idf = E(nc.sbuf_tensor("idf", [128, 128], F32))
    e16 = E(nc.sbuf_tensor("e16_sb", [128, 8 * 16], BF16))
    ebc = E(nc.sbuf_tensor("ebc_sb", [16, 8 * 128], BF16))
    nbi = E(nc.sbuf_tensor("nbi", [128, 128], BF16))
    oneb = E(nc.sbuf_tensor("oneb", [1, 128], BF16))
    qs_c = E(nc.sbuf_tensor("qs_c", [H, 1], F32))
    ks_c = E(nc.sbuf_tensor("ks_c", [H, 1], F32))
    bgb = E(nc.sbuf_tensor("bgb", [1, D], BF16))
    beb = E(nc.sbuf_tensor("beb", [1, 8 * D], BF16))

    SW = contextlib.ExitStack()
    wT_q = SW.enter_context(nc.sbuf_tensor("wT_q", [128, 8 * D], BF16))
    wT_k = SW.enter_context(nc.sbuf_tensor("wT_k", [128, 8 * D], BF16))
    wT_v = SW.enter_context(nc.sbuf_tensor("wT_v", [128, 8 * D], BF16))
    S12 = contextlib.ExitStack()
    r2k = S12.enter_context(nc.sbuf_tensor("r2k", [H, NB], F32, side="right"))
    r2q = S12.enter_context(nc.sbuf_tensor("r2q", [H, NQ], F32, side="right"))
    S23 = contextlib.ExitStack()
    S34 = contextlib.ExitStack()
    S45 = contextlib.ExitStack()

    with contextlib.ExitStack() as L:
        p = Ph("prep")
        wst = [L.enter_context(nc.sbuf_tensor(f"wst{i}", [128, D], F32)) for i in range(2)]
        wca = [L.enter_context(nc.sbuf_tensor(f"wca{i}", [128, D], BF16)) for i in range(2)]
        pstr = L.enter_context(nc.psum_tensor("pstr0", [128, 1024], F32))

        for sb, src in [(idb, IDB), (idf, IDF), (nbi, NBI), (oneb, ONE),
                        (qs_c, QS), (ks_c, KSc)]:
            p.op("y", lambda y, a=sb, b=src: y.dma_start(a[:, :], b[:, :]), dS, 16)
        p.op("y", lambda y: y.dma_start(e16[:, :], E16[:, :, :].rearrange("c p h -> p c h")), dS, 16)
        p.op("y", lambda y: y.dma_start(ebc[:, :], EBC[:, :, :].rearrange("c h p -> h c p")), dS, 16)
        p.op("g", lambda g: g.dma_start(bgb[:, :], BG[:, :]), dG, 16)
        p.op("g", lambda g: g.dma_start(beb[:, :], BE[:, :]), dG, 16)

        ldmark = [0, 0]
        castmark = [0, 0]
        trmark = [0, 0]
        evmark = [0, 0]
        it = 0
        for Wsrc, wT in [(WQ, wT_q), (WK, wT_k), (WV, wT_v)]:
            for fc in range(8):
                par = it % 2
                p.wait("y", sG, castmark[par])
                ldmark[par] = p.op("y", lambda y, w=Wsrc, fc=fc, pr=par:
                                   y.dma_start(wst[pr][:, :], w[128 * fc:128 * fc + 128, :]),
                                   dS, 16)
                p.wait("g", dS, ldmark[par])
                p.wait("g", sT, trmark[par])
                castmark[par] = p.op("g", lambda g, pr=par:
                                     g.tensor_copy(wca[pr][:, :], wst[pr][:, :]), sG, 1)
                p.wait("t", sG, castmark[par])
                for half in range(2):
                    p.wait("t", sV, evmark[half])
                    for k4 in range(4):
                        dc = half * 4 + k4
                        p.op("t", lambda t, pr=par, dc=dc, k4=k4, hh=half:
                             t.matmul(pstr[:, 512 * hh + 128 * k4:512 * hh + 128 * k4 + 128],
                                      wca[pr][:, 128 * dc:128 * dc + 128], idb[:, :],
                                      start=True, stop=True, skip_group_check=True),
                             sT if k4 == 3 else None, 1)
                    tmark = sT.n
                    dst = wT[:, :].rearrange("p (dc f) -> p dc f", f=D)[
                        :, half * 4:half * 4 + 4, 128 * fc:128 * fc + 128]
                    src_ = pstr[:, 512 * half:512 * half + 512].rearrange(
                        "p (a b) -> p a b", b=128)
                    p.wait("v", sT, tmark)
                    evmark[half] = p.op("v", lambda v, d=dst, s=src_: v.tensor_copy(d, s), sV, 1)
                trmark[par] = sT.n
                it += 1
        p.wait("y", dS, dS.n)
        p.wait("g", dG, dG.n)
        p.run(nc)

    with contextlib.ExitStack() as L:
        p = Ph("qkv")
        xb = [L.enter_context(nc.sbuf_tensor(f"xb{i}", [128, 4 * D], BF16)) for i in range(2)]
        xfm = [L.enter_context(nc.sbuf_tensor(f"xfm{i}", [128, 4 * D], BF16)) for i in range(2)]
        kf = [L.enter_context(nc.sbuf_tensor(f"kf{i}", [128, 1024], BF16)) for i in range(2)]
        ksq = [L.enter_context(nc.sbuf_tensor(f"ksq{i}", [128, 1024], BF16)) for i in range(2)]
        vx = [L.enter_context(nc.sbuf_tensor(f"vx{i}", [128, H * 65], BF16)) for i in range(2)]
        ptr = L.enter_context(nc.psum_tensor("ptr1", [128, 1024], F32))
        pkq = L.enter_context(nc.psum_tensor("pkq1", [128, 2048], F32))
        pr2 = L.enter_context(nc.psum_tensor("pr2", [16, 512], F32))

        for i in range(2):
            p.op("g", lambda g, i=i: g.memset(vx[i][:, :], 1.0), sG, 1)

        trlast = [0, 0]
        gend = [0, 0]
        xld = [0, 0]
        evdone = [0, 0]
        kfdma = [0, 0]
        sqmm = [0, 0]
        vxdma = [0, 0]
        trev = [0, 0]
        r2evac = [0]

        gi = 0
        for is_q, n_groups in ((False, NG), (True, NQG)):
            for gidx in range(n_groups):
                par = gi % 2
                Xsrc = XQ if is_q else XKV
                wTx = wT_q if is_q else wT_k
                Dst = Qs if is_q else Ks
                prx = pr2

                p.wait("g", sT, trlast[par])
                src = Xsrc[512 * gidx:512 * gidx + 512, :].rearrange("(j p) d -> p j d", p=128)
                xdst = xb[par][:, :].rearrange("p (j d) -> p j d", d=D)
                xld[par] = p.op("g", lambda g, s=src, d=xdst: g.dma_start(d, s), dG, 16)

                p.wait("t", dG, xld[par])
                first_ev = True
                for dc in range(8):
                    hb = dc % 2
                    p.wait("t", sV, trev[hb])
                    for j in range(4):
                        p.op("t", lambda t, pr=par, dc=dc, j=j, hb=hb:
                             t.matmul(ptr[:, 512 * hb + 128 * j:512 * hb + 128 * j + 128],
                                      xb[pr][:, 1024 * j + 128 * dc:1024 * j + 128 * dc + 128],
                                      idb[:, :], start=True, stop=True, skip_group_check=True),
                             sT if j == 3 else None, 1)
                    tmark = sT.n
                    p.wait("v", sT, tmark)
                    if first_ev:
                        p.wait("v", sT, gend[par])
                        first_ev = False
                    trev[hb] = p.op("v", lambda v, pr=par, dc=dc, hb=hb:
                                    v.tensor_copy(xfm[pr][:, 512 * dc:512 * dc + 512],
                                                  ptr[:, 512 * hb:512 * hb + 512]), sV, 1)
                trlast[par] = sT.n

                pend_e16 = []
                for fp2 in range(4):
                    sl2 = fp2 % 2
                    p.wait("t", sA, evdone[sl2])
                    if fp2 == 0:
                        p.wait("t", sV, trev[0])
                        p.wait("t", sV, trev[1])
                        p.wait("t", sV, r2evac[0])
                    for dc in range(8):
                        for hb in range(2):
                            fc = 2 * fp2 + hb
                            p.op("t", lambda t, pr=par, fc=fc, dc=dc, hb=hb, sl2=sl2, w=wTx:
                                 t.matmul(pkq[:, 1024 * sl2 + 512 * hb:1024 * sl2 + 512 * hb + 512],
                                          w[:, 1024 * dc + 128 * fc:1024 * dc + 128 * fc + 128],
                                          xfm[pr][:, 512 * dc:512 * dc + 512],
                                          start=(dc == 0), stop=(dc == 7),
                                          skip_group_check=True),
                                 sT if (dc == 7 and hb == 1) else None, 1)
                    mmk = sT.n
                    for (fcp, slp, sqd) in pend_e16:
                        p.wait("t", sA, sqd)
                        for hb2 in range(2):
                            sqmm[slp] = p.op(
                                "t", lambda t, p2=slp, fc=fcp + hb2, hb2=hb2, pr_=prx:
                                t.matmul(pr_[:, :], e16[:, 16 * fc:16 * fc + 16],
                                         ksq[p2][:, 512 * hb2:512 * hb2 + 512],
                                         start=(fc == 0), stop=(fc == 7),
                                         skip_group_check=True), sT, 1)
                    pend_e16 = []
                    p.wait("s", sT, mmk)
                    p.wait("s", dS, kfdma[sl2])
                    kevac_ = p.op("s", lambda s, p2=sl2, sl2=sl2:
                                  s.activation(kf[p2][:, :],
                                               pkq[:, 1024 * sl2:1024 * sl2 + 1024],
                                               AF.Silu), sA, 1)
                    evdone[sl2] = kevac_
                    p.wait("s", sT, sqmm[sl2])
                    sqdone = p.op("s", lambda s, p2=sl2:
                                  s.activation(ksq[p2][:, :], kf[p2][:, :], AF.Square),
                                  sA, 1)
                    pend_e16.append((2 * fp2, sl2, sqdone))
                    p.wait("y", sA, kevac_)
                    for hb in range(2):
                        fc = 2 * fp2 + hb
                        d0 = Dst[2 * fc, :, 512 * gidx:512 * gidx + 512]
                        d1 = Dst[2 * fc + 1, :, 512 * gidx:512 * gidx + 512]
                        p.op("y", lambda g, p2=sl2, d=d0, hb=hb:
                             g.dma_start(d, kf[p2][0:64, 512 * hb:512 * hb + 512]), dS, 16)
                        kfdma[sl2] = p.op(
                            "y", lambda g, p2=sl2, d=d1, hb=hb:
                            g.dma_start(d, kf[p2][64:128, 512 * hb:512 * hb + 512]), dS, 16)

                flush_e16 = pend_e16
                pend_e16 = []

                if not is_q:
                    for j in range(4):
                        pv_ = j % 2
                        sl2 = j % 2
                        p.wait("t", sA, evdone[sl2])
                        p.wait("t", dS, vxdma[pv_])
                        for dc in range(8):
                            for half in range(2):
                                p.op("t", lambda t, pr=par, j=j, dc=dc, hf=half, sl2=sl2:
                                     t.matmul(pkq[:, 1024 * sl2 + 512 * hf:1024 * sl2 + 512 * hf + 512],
                                              xfm[pr][:, 512 * dc + 128 * j:512 * dc + 128 * j + 128],
                                              wT_v[:, 1024 * dc + 512 * hf:1024 * dc + 512 * hf + 512],
                                              start=(dc == 0), stop=(dc == 7),
                                              skip_group_check=True),
                                     sT if (dc == 7 and half == 1) else None, 1)
                        vmm = sT.n
                        if flush_e16:
                            for (fcp, slp, sqd) in flush_e16:
                                p.wait("t", sA, sqd)
                                for hb2 in range(2):
                                    sqmm[slp] = p.op(
                                        "t", lambda t, p2=slp, fc=fcp + hb2, hb2=hb2, pr_=prx:
                                        t.matmul(pr_[:, :], e16[:, 16 * fc:16 * fc + 16],
                                                 ksq[p2][:, 512 * hb2:512 * hb2 + 512],
                                                 start=(fc == 0), stop=(fc == 7),
                                                 skip_group_check=True), sT, 1)
                            flush_e16 = []
                            p.wait("v", sT, sqmm[1])
                            r2dst = (r2q if is_q else r2k)[:, 512 * gidx:512 * gidx + 512]
                            r2evac[0] = p.op("v", lambda v, d=r2dst, pr_=prx:
                                             v.tensor_copy(d, pr_[:, :]), sV, 1)
                        p.wait("s", sT, vmm)
                        p.wait("s", dS, vxdma[pv_])
                        dr = vx[pv_][:, :].rearrange("p (h j2) -> p h j2", j2=65)
                        evdone[sl2] = p.op("s", lambda s, pv=pv_, dr=dr, sl2=sl2:
                                           s.activation(dr[:, :, 0:64],
                                                        pkq[:, 1024 * sl2:1024 * sl2 + 1024]
                                                        .rearrange("p (h f) -> p h f", f=64),
                                                        AF.Silu), sA, 1)
                        kt = 4 * gidx + j
                        p.wait("y", sA, evdone[sl2])
                        vxdma[pv_] = p.op("y", lambda g, pv=pv_, kt=kt:
                                          g.dma_start(Vs[kt, :, :], vx[pv][:, :]), dS, 16)
                else:
                    for (fcp, slp, sqd) in flush_e16:
                        p.wait("t", sA, sqd)
                        for hb2 in range(2):
                            sqmm[slp] = p.op(
                                "t", lambda t, p2=slp, fc=fcp + hb2, hb2=hb2, pr_=prx:
                                t.matmul(pr_[:, :], e16[:, 16 * fc:16 * fc + 16],
                                         ksq[p2][:, 512 * hb2:512 * hb2 + 512],
                                         start=(fc == 0), stop=(fc == 7),
                                         skip_group_check=True), sT, 1)
                    flush_e16 = []
                    p.wait("v", sT, sqmm[1])
                    r2dst = (r2q if is_q else r2k)[:, 512 * gidx:512 * gidx + 512]
                    r2evac[0] = p.op("v", lambda v, d=r2dst, pr_=prx:
                                     v.tensor_copy(d, pr_[:, :]), sV, 1)
                gend[par] = sT.n
                gi += 1
        p.wait("g", dG, dG.n)
        p.wait("y", dS, dS.n)
        p.run(nc)
    SW.close()

    nrB = S23.enter_context(nc.sbuf_tensor("nrB", [H, NB], BF16))
    bcols = S23.enter_context(nc.sbuf_tensor("bcols", [128, H * NKT], F32))
    c_own = S23.enter_context(nc.sbuf_tensor("c_own", [H, NQ], BF16))
    a_own = S23.enter_context(nc.sbuf_tensor("a_own", [H, NQ], BF16))
    with contextlib.ExitStack() as L:
        p = Ph("coeff")
        T1 = L.enter_context(nc.sbuf_tensor("cT1", [H, NB], F32))
        T2 = L.enter_context(nc.sbuf_tensor("cT2", [H, NB], F32))
        q1 = L.enter_context(nc.sbuf_tensor("cq1", [H, NQ], F32))
        q2 = L.enter_context(nc.sbuf_tensor("cq2", [H, NQ], F32))
        sm = L.enter_context(nc.sbuf_tensor("csm", [H, 8], F32))
        ptc = L.enter_context(nc.psum_tensor("ptc", [128, 1024], F32))

        def cv(bld):
            v = p.op("v", bld, sV, 1)
            p.wait("v", sV, v)
            return v

        def ca(bld):
            v = p.op("s", bld, sA, 1)
            p.wait("s", sA, v)
            return v

        ca(lambda s: s.activation(T1[:, :], r2k[:, :], AF.Ln, bias=np.float32(1e-30)))
        m = p.op("s", lambda s: s.activation(r2k[:, :], T1[:, :], AF.Exp, scale=np.float32(0.5)), sA, 1)
        p.wait("v", sA, m)
        cv(lambda v: v.tensor_scalar(sm[:, 0:1], ks_c[:, :], -1.0, None, OP.mult))
        cv(lambda v: v.tensor_max(sm[:, 0:1], sm[:, 0:1], ks_c[:, :]))
        cv(lambda v: v.tensor_scalar(T1[:, :], r2k[:, :], EPS, None, OP.add))
        cv(lambda v: v.reciprocal(T2[:, :], T1[:, :]))
        cv(lambda v: v.scalar_tensor_tensor(T1[:, :], r2k[:, :], sm[:, 0:1], T2[:, :],
                                            OP.mult, OP.mult))
        cv(lambda v: v.reciprocal(T2[:, :], T1[:, :]))
        cv(lambda v: v.tensor_scalar(nrB[:, :], T2[:, :], -1.0, None, OP.mult))
        p.wait("t", sV, sV.n)
        evhalf = [0, 0]
        for ch in range(NKT):
            hb = ch % 2
            p.wait("t", sV, evhalf[hb])
            p.op("t", lambda t, ch=ch, hb=hb:
                 t.transpose(ptc[:, 512 * hb:512 * hb + 16], T1[0:16, 128 * ch:128 * ch + 128],
                             idf[0:16, 0:16]), sT, 1)
            p.wait("v", sT, sT.n)
            dstb = bcols[:, :].rearrange("p (h k) -> p h k", k=NKT)[:, :, ch:ch + 1]
            evhalf[hb] = p.op("v", lambda v, hb=hb, d=dstb:
                              v.tensor_copy(d, ptc[:, 512 * hb:512 * hb + 16]), sV, 1)
        p.wait("v", sV, max(evhalf))
        p.wait("v", sT, sT.n)
        cv(lambda v: v.tensor_mul(T1[:, :], T1[:, :], r2k[:, :]))
        cv(lambda v: v.reduce_max(sm[:, 2:3], T1[:, :], axis=AX.X))
        ca(lambda s: s.activation(q1[:, :], r2q[:, :], AF.Ln, bias=np.float32(1e-30)))
        m = p.op("s", lambda s: s.activation(r2q[:, :], q1[:, :], AF.Exp, scale=np.float32(0.5)), sA, 1)
        p.wait("v", sA, m)
        p.wait("s", sA, m)
        ca(lambda s: s.activation(q1[:, :], r2q[:, :], AF.Ln, bias=np.float32(EPS)))
        m = p.op("s", lambda s: s.activation(q2[:, :], q1[:, :], AF.Exp, scale=np.float32(-1.0)), sA, 1)
        p.wait("v", sA, m)
        cv(lambda v: v.scalar_tensor_tensor(q1[:, :], r2q[:, :], qs_c[:, :], q2[:, :],
                                            OP.mult, OP.mult))
        cv(lambda v: v.tensor_scalar(sm[:, 1:2], ks_c[:, :], 0.0, None, OP.is_ge))
        cv(lambda v: v.tensor_scalar(sm[:, 1:2], sm[:, 1:2], 2.0, -1.0, OP.mult, OP.add))
        cv(lambda v: v.tensor_scalar(a_own[:, :], q1[:, :], sm[:, 1:2], None, OP.mult))
        cv(lambda v: v.tensor_scalar(q2[:, :], q1[:, :], -1.0, None, OP.mult))
        cv(lambda v: v.tensor_max(q2[:, :], q2[:, :], q1[:, :]))
        cv(lambda v: v.tensor_mul(q2[:, :], q2[:, :], r2q[:, :]))
        cv(lambda v: v.tensor_scalar(c_own[:, :], q2[:, :], sm[:, 2:3], None, OP.mult))
        p.run(nc)
    S12.close()

    attn = S34.enter_context(nc.sbuf_tensor("attn", [128, 8 * NQ], BF16, side="right"))
    l_all = S34.enter_context(nc.sbuf_tensor("l_all", [H, NQ], F32, side="right"))
    with contextlib.ExitStack() as L:
        p = Ph("attn")
        ksb = [L.enter_context(nc.sbuf_tensor(f"ksb{i}", [65, NB], BF16)) for i in range(2)]
        vsb = [L.enter_context(nc.sbuf_tensor(f"vsb{i}", [128, NKT * 65], BF16)) for i in range(2)]
        qsb = [L.enter_context(nc.sbuf_tensor(f"qsb{i}", [65, NQ // 2], BF16)) for i in range(2)]
        arow = [L.enter_context(nc.sbuf_tensor(f"ar{i}", [1, NQ // 2], BF16)) for i in range(2)]
        pts = [L.enter_context(nc.sbuf_tensor(f"pts{i}", [128, 1024], BF16)) for i in range(3)]
        lst = L.enter_context(nc.sbuf_tensor("lst", [1, NQ], F32))
        mq = L.enter_context(nc.sbuf_tensor("mq", [128, NKT * 128], BF16))
        pst = L.enter_context(nc.psum_tensor("pst3", [128, 3072], F32))
        ppv = L.enter_context(nc.psum_tensor("ppv3", [65, 1024], F32))

        p.op("y", lambda y: y.dma_start(mq[:, :], MQ[:, :, :].rearrange("j p q -> p j q")),
             dS, 16)

        sweep_done = [0, 0]
        exp_done = [0, 0, 0]
        pv_done = [0, 0, 0]
        pv_evac = {}
        ldma_prev = 0
        lds_a = [0] * 33
        ldg_a = [0] * 33
        asv_a = [0] * 33

        items = [(P_, h) for P_ in range(2) for h in range(H)]

        def emit_loads(it2):
            P2, h2 = items[it2]
            par2 = it2 % 2
            J2 = 32 if P2 == 0 else NKT
            p.wait("y", sV, sweep_done[par2])
            p.op("y", lambda y, h=h2, pr=par2, J=J2:
                 y.dma_start(ksb[pr][0:64, 0:128 * J], Ks[h, :, 0:128 * J]), dS, 16)
            p.op("y", lambda y, h=h2, pr=par2, P_=P2:
                 y.dma_start(qsb[pr][0:64, :], Qs[h, :, 1024 * P_:1024 * P_ + 1024]),
                 dS, 16)
            vsrc = Vs[0:J2, :, 65 * h2:65 * h2 + 65].rearrange("k p j -> p k j")
            vdst3 = vsb[par2][:, 0:65 * J2].rearrange("p (k j) -> p k j", j=65)
            lds_a[it2] = p.op("y", lambda y, s=vsrc, d=vdst3: y.dma_start(d, s), dS, 16)
            p.wait("g", sV, sweep_done[par2])
            p.op("g", lambda g, h=h2, pr=par2, J=J2:
                 g.dma_start(ksb[pr][64:65, 0:128 * J], nrB[h:h + 1, 0:128 * J]), dG, 16)
            p.op("g", lambda g, h=h2, pr=par2, P_=P2:
                 g.dma_start(qsb[pr][64:65, :], c_own[h:h + 1, 1024 * P_:1024 * P_ + 1024]),
                 dG, 16)
            ldg_a[it2] = p.op("g", lambda g, h=h2, pr=par2, P_=P2:
                              g.dma_start(arow[pr][:, :],
                                          a_own[h:h + 1, 1024 * P_:1024 * P_ + 1024]), dG, 16)

        def emit_ascale(it2, bA):
            par2 = it2 % 2
            p.wait("t", dS, lds_a[it2])
            p.wait("t", dG, ldg_a[it2])
            p.wait("t", sA, exp_done[bA])
            asv = 0
            for i2 in range(2):
                p.op("t", lambda t, pr=par2, i2=i2, bA=bA:
                     t.matmul(pst[0:64, 1024 * bA + 512 * i2:1024 * bA + 512 * i2 + 512],
                              oneb[0:1, 0:64], arow[pr][:, 512 * i2:512 * i2 + 512],
                              start=True, stop=True, skip_group_check=True),
                     sT if i2 == 1 else None, 1)
            am = sT.n
            p.wait("v", sT, am)
            p.wait("v", dS, lds_a[it2])
            for i2 in range(2):
                asv = p.op("v", lambda v, pr=par2, i2=i2, bA=bA:
                           v.tensor_mul(qsb[pr][0:64, 512 * i2:512 * i2 + 512],
                                        qsb[pr][0:64, 512 * i2:512 * i2 + 512],
                                        pst[0:64, 1024 * bA + 512 * i2:1024 * bA + 512 * i2 + 512]),
                           sV, 1)
            asv_a[it2] = asv

        emit_loads(0)
        emit_loads(1)
        emit_ascale(0, 2)
        for it, (P_, h) in enumerate(items):
            par = it % 2
            JP = 32 if P_ == 0 else NKT
            p.wait("t", sV, asv_a[it])

            def emit_pv(j):
                bj = j % 3
                p.wait("t", sA, exp_done[bj])
                s0 = j // 4
                slo = max(s0, 8 * P_) - 8 * P_
                for bl in range(slo // 4, 2):
                    sa = max(slo, 4 * bl)
                    n = (4 * bl + 4 - sa) * 128
                    pcol = 128 * sa
                    first = (j == 0)
                    bg = 2 * P_ + bl
                    lastj = (j == CS[4 * bg + 3] - 1)
                    if first and it > 1:
                        p.wait("t", sV, pv_evac.get((it - 2, bl), 0))
                    p.op("t", lambda t, pr=par, j=j, n=n, pcol=pcol,
                         first=first, lastj=lastj, bj=bj:
                         t.matmul(ppv[:, pcol:pcol + n],
                                  vsb[pr][:, 65 * j:65 * j + 65],
                                  pts[bj][:, pcol:pcol + n],
                                  start=first, stop=lastj,
                                  skip_group_check=True), sT, 1)
                    pv_done[bj] = sT.n
                    if lastj:
                        pvm = sT.n
                        p.wait("v", sT, pvm)
                        sg = 8 * P_ + 4 * bl
                        bc = 512 * bl
                        dsta = attn[64 * (h % 2):64 * (h % 2) + 64,
                                    (h // 2) * NQ + 128 * sg:(h // 2) * NQ + 128 * sg + 512]
                        p.op("v", lambda v, d=dsta, bc=bc:
                             v.tensor_copy(d, ppv[0:64, bc:bc + 512]), sV, 1)
                        if bl == 0:
                            p.wait("v", dG, ldma_prev)
                        pv_evac[(it, bl)] = p.op(
                            "v", lambda v, bc=bc, sg=sg:
                            v.tensor_copy(lst[0:1, 128 * sg:128 * sg + 512],
                                          ppv[64:65, bc:bc + 512]), sV, 1)

            for j in range(JP):
                bj = j % 3
                s0 = j // 4
                slo = max(s0, 8 * P_) - 8 * P_
                p.wait("t", sA, exp_done[bj])
                s = slo
                while s < 8:
                    s2 = min((s // 4) * 4 + 3, 7)
                    n = (s2 - s + 1) * 128
                    tail = (s <= s0 - 8 * P_ <= s2) and (s0 >= 8 * P_)
                    col = 1024 * bj + 128 * s
                    is_last = (s2 == 7) and not tail
                    p.op("t", lambda t, pr=par, j=j, col=col, n=n, s=s:
                         t.matmul(pst[:, col:col + n],
                                  ksb[pr][0:65, 128 * j:128 * j + 128],
                                  qsb[pr][0:65, 128 * s:128 * s + n],
                                  start=True, stop=not tail,
                                  skip_group_check=True),
                         sT if is_last else None, 1)
                    if tail:
                        mcol = 1024 * bj + 128 * (s0 - 8 * P_)
                        is_last2 = (s2 == 7)
                        p.op("t", lambda t, j=j, mcol=mcol:
                             t.matmul(pst[:, mcol:mcol + 128], nbi[:, :],
                                      mq[:, 128 * j:128 * j + 128],
                                      start=False, stop=True,
                                      skip_group_check=True),
                             sT if is_last2 else None, 1)
                    s = s2 + 1
                stm = sT.n
                cL = 128 * slo
                nA = 1024 - cL
                p.wait("s", sT, stm)
                p.wait("s", sT, pv_done[bj])
                exp_done[bj] = p.op("s", lambda s_, h=h, j=j, cL=cL, nA=nA, bj=bj:
                                    s_.activation(pts[bj][:, cL:cL + nA],
                                                  pst[:, 1024 * bj + cL:1024 * bj + cL + nA],
                                                  AF.Exp, bias=np.float32(TSHIFT),
                                                  scale=bcols[:, NKT * h + j:NKT * h + j + 1]),
                                    sA, 1)
                if j == JP - 2 and it + 1 < len(items):
                    emit_ascale(it + 1, JP % 3)
                if j >= 2:
                    emit_pv(j - 2)
            emit_pv(JP - 2)
            emit_pv(JP - 1)
            p.wait("g", sV, pv_evac[(it, 1)])
            ldma_prev = p.op("g", lambda g, h=h, P_=P_:
                             g.dma_start(l_all[h:h + 1, 1024 * P_:1024 * P_ + 1024],
                                         lst[0:1, 1024 * P_:1024 * P_ + 1024]), dG, 16)
            sweep_done[par] = pv_evac[(it, 1)]
            if it + 2 < len(items):
                emit_loads(it + 2)
        p.wait("g", dG, dG.n)
        p.wait("y", dS, dS.n)
        p.run(nc)
    S23.close()

    y_fm = S45.enter_context(nc.sbuf_tensor("y_fm", [128, 8 * NQ], BF16))
    p_all = S45.enter_context(nc.sbuf_tensor("p_all", [128, NQT], F32))
    oh_all = S45.enter_context(nc.sbuf_tensor("oh_all", [128, NQT * 8], F32))
    with contextlib.ExitStack() as L:
        p = Ph("woln")
        wst = [L.enter_context(nc.sbuf_tensor(f"wst4{i}", [128, D], F32)) for i in range(2)]
        wca = [L.enter_context(nc.sbuf_tensor(f"wca4{i}", [128, D], BF16)) for i in range(2)]
        wT_o = L.enter_context(nc.sbuf_tensor("wT_o", [128, 8 * D], BF16))
        lrec = L.enter_context(nc.sbuf_tensor("lrec", [H, NQ], F32))
        lrecb = L.enter_context(nc.sbuf_tensor("lrecb", [H, NQ], BF16))
        xres = [L.enter_context(nc.sbuf_tensor(f"xres{i}", [128, D], F32)) for i in range(3)]
        y0 = [L.enter_context(nc.sbuf_tensor(f"y0_{i}", [128, D], F32)) for i in range(3)]
        y2 = [L.enter_context(nc.sbuf_tensor(f"y2_{i}", [128, D], F32)) for i in range(3)]
        ybf = [L.enter_context(nc.sbuf_tensor(f"ybf{i}", [128, D], BF16)) for i in range(3)]
        sqs = L.enter_context(nc.sbuf_tensor("sqs", [128, D], BF16))
        smc = L.enter_context(nc.sbuf_tensor("smc", [128, 8], F32))
        pwo = L.enter_context(nc.psum_tensor("pwo", [128, 3072], F32))
        ptr4 = L.enter_context(nc.psum_tensor("ptr4", [128, 1024], F32))

        if tap_attn is not None:
            p.op("g", lambda g: g.dma_start(tap_lall[:, :], l_all[:, :]), dG, 16)
        ldmark = [0, 0]
        castmark = [0, 0]
        trmark = [0, 0]
        evmark = [0, 0]
        for fc in range(8):
            par = fc % 2
            p.wait("y", sG, castmark[par])
            ldmark[par] = p.op("y", lambda y, fc=fc, pr=par:
                               y.dma_start(wst[pr][:, :], WO[128 * fc:128 * fc + 128, :]), dS, 16)
            p.wait("g", dS, ldmark[par])
            p.wait("g", sT, trmark[par])
            castmark[par] = p.op("g", lambda g, pr=par:
                                 g.tensor_copy(wca[pr][:, :], wst[pr][:, :]), sG, 1)
            p.wait("t", sG, castmark[par])
            for half in range(2):
                p.wait("t", sV, evmark[half])
                for k4 in range(4):
                    dc = half * 4 + k4
                    p.op("t", lambda t, pr=par, dc=dc, k4=k4, hh=half:
                         t.matmul(ptr4[:, 512 * hh + 128 * k4:512 * hh + 128 * k4 + 128],
                                  wca[pr][:, 128 * dc:128 * dc + 128], idb[:, :],
                                  start=True, stop=True, skip_group_check=True),
                         sT if k4 == 3 else None, 1)
                tmark = sT.n
                dst = wT_o[:, :].rearrange("p (dc f) -> p dc f", f=D)[
                    :, half * 4:half * 4 + 4, 128 * fc:128 * fc + 128]
                src_ = ptr4[:, 512 * half:512 * half + 512].rearrange("p (a b) -> p a b", b=128)
                p.wait("v", sT, tmark)
                evmark[half] = p.op("v", lambda v, d=dst, s=src_: v.tensor_copy(d, s), sV, 1)
            trmark[par] = sT.n

        la = p.op("v", lambda v: v.reciprocal(lrec[:, :], l_all[:, :]), sV, 1)
        p.wait("v", sV, la)
        lr = p.op("v", lambda v: v.tensor_copy(lrecb[:, :], lrec[:, :]), sV, 1)
        p.wait("v", sV, lr)
        p.wait("t", sV, lr)
        nmark = [evmark[0], evmark[1]]
        for u in range(32):
            ch, g2 = u // 4, u % 4
            sl_ = u % 2
            p.wait("t", sV, nmark[sl_])
            bm = p.op("t", lambda t, ch=ch, g2=g2, sl_=sl_:
                      t.matmul(ptr4[:, 512 * sl_:512 * sl_ + 512],
                               ebc[0:16, 128 * ch:128 * ch + 128],
                               lrecb[0:16, 512 * g2:512 * g2 + 512], start=True, stop=True,
                               skip_group_check=True), sT, 1)
            p.wait("v", sT, bm)
            dsta = attn[:, 2048 * ch + 512 * g2:2048 * ch + 512 * g2 + 512]
            nmark[sl_] = p.op("v", lambda v, d=dsta, sl_=sl_:
                              v.tensor_mul(d, d, ptr4[:, 512 * sl_:512 * sl_ + 512]), sV, 1)
        normdone = max(nmark)
        if tap_attn is not None:
            p.wait("g", sV, normdone)
            p.op("g", lambda g: g.dma_start(tap_attn[:, :], attn[:, :]), dG, 16)
        p.wait("t", sV, normdone)

        h1mark = [0, 0, 0]
        ymark = [0, 0, 0]
        ydma = [0, 0, 0]
        ybfm = [0, 0, 0]
        ytr = [0, 0, 0]
        xload = [0, 0, 0]
        tr4ev = [normdone, normdone]
        for qt in range(NQT):
            sl = qt % 3
            if qt == 0:
                p.wait("t", sV, max(evmark))
            p.wait("t", sV, h1mark[sl])
            for dc in range(8):
                for half in range(2):
                    p.op("t", lambda t, qt=qt, dc=dc, hf=half, sl=sl:
                         t.matmul(pwo[:, 1024 * sl + 512 * hf:1024 * sl + 512 * hf + 512],
                                  attn[:, 2048 * dc + 128 * qt:2048 * dc + 128 * qt + 128],
                                  wT_o[:, 1024 * dc + 512 * hf:1024 * dc + 512 * hf + 512],
                                  start=(dc == 0), stop=(dc == 7), skip_group_check=True),
                         sT if (dc == 7 and half == 1) else None, 1)
            womm = sT.n
            p.wait("y", sV, h1mark[sl])
            xload[sl] = p.op("y", lambda y, qt=qt, sl=sl:
                             y.dma_start(xres[sl][:, :], XQ[128 * qt:128 * qt + 128, :]), dS, 16)
            p.wait("v", sT, womm)
            p.wait("v", dS, xload[sl])
            p.wait("v", dG, ydma[sl])
            vv = p.op("v", lambda v, sl=sl:
                      v.tensor_add(y0[sl][:, :], xres[sl][:, :],
                                   pwo[:, 1024 * sl:1024 * sl + 1024]), sV, 1)
            h1mark[sl] = vv
            p.wait("s", sV, vv)
            sqa = p.op("s", lambda s, sl=sl:
                       s.activation(sqs[:, :], y0[sl][:, :], AF.Square,
                                    accum_out=smc[:, 2:3]), sA, 1)
            p.wait("v", sV, vv)
            vv = p.op("v", lambda v, sl=sl:
                      v.reduce_sum(smc[:, 0:1], y0[sl][:, :], axis=AX.X, negate=True), sV, 1)
            p.wait("v", sV, vv)
            vv = p.op("v", lambda v: v.tensor_scalar(smc[:, 1:2], smc[:, 0:1], RD, None,
                                                     OP.mult), sV, 1)
            p.wait("v", sA, sqa)
            p.wait("v", sV, vv)
            vv = p.op("v", lambda v: v.tensor_scalar(smc[:, 5:6], smc[:, 2:3], RD, None,
                                                     OP.mult), sV, 1)
            p.wait("v", sV, vv)
            vv = p.op("v", lambda v: v.tensor_mul(smc[:, 6:7], smc[:, 1:2], smc[:, 1:2]), sV, 1)
            p.wait("v", sV, vv)
            vv = p.op("v", lambda v: v.tensor_sub(smc[:, 5:6], smc[:, 5:6], smc[:, 6:7]), sV, 1)
            p.wait("s", sV, vv)
            aa = p.op("s", lambda s: s.activation(smc[:, 3:4], smc[:, 5:6], AF.Ln,
                                                  bias=np.float32(LNEPS)), sA, 1)
            p.wait("s", sA, aa)
            aa = p.op("s", lambda s: s.activation(smc[:, 4:5], smc[:, 3:4], AF.Exp,
                                                  scale=np.float32(-0.5)), sA, 1)
            p.wait("v", sA, aa)
            vv = p.op("v", lambda v, sl=sl:
                      v.tensor_scalar(y2[sl][:, :], y0[sl][:, :], smc[:, 1:2], smc[:, 4:5],
                                      OP.add, OP.mult), sV, 1)
            ymark[sl] = vv
            p.wait("g", sV, vv)
            ydma[sl] = p.op("g", lambda g, qt=qt, sl=sl:
                            g.dma_start(Ys[128 * qt:128 * qt + 128, :], y2[sl][:, :]), dG, 16)
            p.wait("v", sV, vv)
            p.wait("v", sT, ytr[sl])
            vv = p.op("v", lambda v, sl=sl: v.tensor_copy(ybf[sl][:, :], y2[sl][:, :]), sV, 1)
            ybfm[sl] = vv
            p.wait("t", sV, ybfm[sl])
            for hb in range(2):
                p.wait("t", sV, tr4ev[hb])
                for k4 in range(4):
                    dc = hb * 4 + k4
                    p.op("t", lambda t, sl=sl, dc=dc, k4=k4, hb=hb:
                         t.matmul(ptr4[:, 512 * hb + 128 * k4:512 * hb + 128 * k4 + 128],
                                  ybf[sl][:, 128 * dc:128 * dc + 128], idb[:, :],
                                  start=True, stop=True, skip_group_check=True),
                         sT if k4 == 3 else None, 1)
                tmark = sT.n
                dst = y_fm[:, :].rearrange("p (dc t) -> p dc t", t=NQ)[
                    :, hb * 4:hb * 4 + 4, 128 * qt:128 * qt + 128]
                src_ = ptr4[:, 512 * hb:512 * hb + 512].rearrange("p (a b) -> p a b", b=128)
                p.wait("v", sT, tmark)
                tr4ev[hb] = p.op("v", lambda v, d=dst, s=src_: v.tensor_copy(d, s), sV, 1)
            ytr[sl] = sT.n
        p.wait("g", dG, dG.n)
        p.wait("y", dS, dS.n)
        p.run(nc)
    S34.close()

    with contextlib.ExitStack() as L:
        p = Ph("ffn")
        ffna = L.enter_context(nc.sbuf_tensor("ffna", [128, NQT * D], BF16))
        iosb = L.enter_context(nc.sbuf_tensor("iosb", [128, D], F32))
        big = L.enter_context(nc.sbuf_tensor("big", [128, D], F32))
        wgb = L.enter_context(nc.sbuf_tensor("wgb", [128, 8 * D], BF16))
        web = [L.enter_context(nc.sbuf_tensor(f"web{i}", [128, 8 * D], BF16)) for i in range(2)]
        lg = [L.enter_context(nc.sbuf_tensor(f"lg{i}", [128, D], F32)) for i in range(2)]
        esc = L.enter_context(nc.sbuf_tensor("esc", [128, D], BF16))
        mku = L.enter_context(nc.sbuf_tensor("mku", [128, D], U8))
        tsel = L.enter_context(nc.sbuf_tensor("tsel", [128, D], F32))
        cnd = L.enter_context(nc.sbuf_tensor("cnd", [128, 16], F32))
        he = [L.enter_context(nc.sbuf_tensor(f"he{i}", [128, D], BF16)) for i in range(2)]
        yre = [L.enter_context(nc.sbuf_tensor(f"yre{i}", [128, D], F32)) for i in range(2)]
        h2 = [L.enter_context(nc.sbuf_tensor(f"h2_{i}", [128, D], F32)) for i in range(2)]
        osb = [L.enter_context(nc.sbuf_tensor(f"osb{i}", [128, D], F32)) for i in range(2)]
        sq5 = L.enter_context(nc.sbuf_tensor("sq5", [128, D], BF16))
        sm5 = L.enter_context(nc.sbuf_tensor("sm5", [128, 8], F32))
        pg = L.enter_context(nc.psum_tensor("pg5", [128, 2048], F32))
        pe_ = L.enter_context(nc.psum_tensor("pe5", [128, 2048], F32))

        iota_ld = p.op("y", lambda y: y.dma_start(iosb[:, :], IOTA[:, :]), dS, 16)
        big_ms = p.op("g", lambda g: g.memset(big[:, :], 30000.0), sG, 1)
        wgsrc = WG[:, :].rearrange("(c p) d -> p c d", p=128)
        wgd3 = wgb[:, :].rearrange("p (c d) -> p c d", d=D)
        wgld = p.op("g", lambda g, s=wgsrc, d=wgd3: g.dma_start(d, s), dG, 16)
        weld = [0, 0]
        wesrc0 = WE[0, :, :].rearrange("(c p) d -> p c d", p=128)
        wed0 = web[0][:, :].rearrange("p (c d) -> p c d", d=D)
        weld[0] = p.op("g", lambda g, s=wesrc0, d=wed0: g.dma_start(d, s), dG, 16)

        gevac = [0, 0]
        prev_exp = [0]

        def emit_gate(qt):
            sl = qt % 2
            p.wait("t", dG, wgld)
            p.wait("t", sV, gevac[sl])
            for dc in range(8):
                for half in range(2):
                    p.op("t", lambda t, qt=qt, dc=dc, hf=half, sl=sl:
                         t.matmul(pg[:, 1024 * sl + 512 * hf:1024 * sl + 512 * hf + 512],
                                  y_fm[:, 2048 * dc + 128 * qt:2048 * dc + 128 * qt + 128],
                                  wgb[:, 1024 * dc + 512 * hf:1024 * dc + 512 * hf + 512],
                                  start=(dc == 0), stop=False, skip_group_check=True),
                         None)
            for half in range(2):
                p.op("t", lambda t, hf=half, sl=sl:
                     t.matmul(pg[:, 1024 * sl + 512 * hf:1024 * sl + 512 * hf + 512],
                              oneb[:, :], bgb[:, 512 * hf:512 * hf + 512],
                              start=False, stop=True, skip_group_check=True),
                     sT if half == 1 else None, 1)
            gmm = sT.n
            p.wait("v", sT, gmm)
            p.wait("v", sA, prev_exp[0])
            vv = p.op("v", lambda v, sl=sl:
                      v.reduce_max(sm5[:, 0:1], pg[:, 1024 * sl:1024 * sl + 1024],
                                   axis=AX.X, negate=True), sV, 1)
            p.wait("s", sV, vv)
            aa = p.op("s", lambda s, sl=sl:
                      s.activation(esc[:, :], pg[:, 1024 * sl:1024 * sl + 1024], AF.Exp,
                                   bias=sm5[:, 0:1], accum_out=sm5[:, 1:2]), sA, 1)
            prev_exp[0] = aa
            p.wait("v", sA, aa)
            vv = p.op("v", lambda v, qt=qt:
                      v.reciprocal(p_all[:, qt:qt + 1], sm5[:, 1:2]), sV, 1)
            p.wait("v", sV, vv)
            vv = p.op("v", lambda v, sl=sl:
                      v.tensor_scalar(mku[:, :], pg[:, 1024 * sl:1024 * sl + 1024],
                                      sm5[:, 0:1], 0.0, OP.add, OP.is_ge), sV, 1)
            gevac[sl] = vv
            p.wait("v", sV, vv)
            if qt == 0:
                p.wait("v", dS, iota_ld)
                p.wait("v", sG, big_ms)
            vv = p.op("v", lambda v: v.select(tsel[:, :], mku[:, :], iosb[:, :], big[:, :]),
                      sV, 1)
            p.wait("v", sV, vv)
            vv = p.op("v", lambda v:
                      v.tensor_reduce(cnd[:, 0:8],
                                      tsel[:, :].rearrange("p (g e) -> p e g", e=8),
                                      axis=AX.X, op=OP.min), sV, 1)
            p.wait("v", sV, vv)
            vv = p.op("v", lambda v: v.tensor_reduce(sm5[:, 2:3], cnd[:, 0:8], axis=AX.X,
                                                     op=OP.min), sV, 1)
            p.wait("v", sV, vv)
            vv = p.op("v", lambda v, qt=qt:
                      v.tensor_scalar(oh_all[:, 8 * qt:8 * qt + 8], cnd[:, 0:8], sm5[:, 2:3],
                                      None, OP.is_le), sV, 1)
            p.wait("v", sV, vv)

        yrel = [0, 0]
        h2m = [0, 0]
        outd = [0, 0]

        def emit_final(qt):
            sl = qt % 2
            sl = qt % 2
            p.wait("y", sV, h2m[sl])
            yrel[sl] = p.op("y", lambda y, qt=qt, sl=sl:
                            y.dma_start(yre[sl][:, :], Ys[128 * qt:128 * qt + 128, :]), dS, 16)
            p.wait("v", dS, yrel[sl])
            p.wait("v", sV, acc_mark[qt])
            p.wait("v", dS, outd[sl])
            vv = p.op("v", lambda v, qt=qt, sl=sl:
                      v.scalar_tensor_tensor(h2[sl][:, :], ffna[:, D * qt:D * qt + D],
                                             p_all[:, qt:qt + 1], yre[sl][:, :],
                                             OP.mult, OP.add), sV, 1)
            h2m[sl] = vv
            p.wait("s", sV, vv)
            sqa = p.op("s", lambda s, sl=sl:
                       s.activation(sq5[:, :], h2[sl][:, :], AF.Square,
                                    accum_out=sm5[:, 6:7]), sA, 1)
            p.wait("v", sV, vv)
            vv = p.op("v", lambda v, sl=sl:
                      v.reduce_sum(sm5[:, 4:5], h2[sl][:, :], axis=AX.X, negate=True), sV, 1)
            p.wait("v", sV, vv)
            vv = p.op("v", lambda v: v.tensor_scalar(sm5[:, 5:6], sm5[:, 4:5], RD, None,
                                                     OP.mult), sV, 1)
            p.wait("v", sA, sqa)
            p.wait("v", sV, vv)
            vv = p.op("v", lambda v: v.tensor_scalar(sm5[:, 7:8], sm5[:, 6:7], RD, None,
                                                     OP.mult), sV, 1)
            p.wait("v", sV, vv)
            vv = p.op("v", lambda v: v.tensor_mul(sm5[:, 4:5], sm5[:, 5:6], sm5[:, 5:6]), sV, 1)
            p.wait("v", sV, vv)
            vv = p.op("v", lambda v: v.tensor_sub(sm5[:, 7:8], sm5[:, 7:8], sm5[:, 4:5]), sV, 1)
            p.wait("s", sV, vv)
            aa = p.op("s", lambda s: s.activation(sm5[:, 3:4], sm5[:, 7:8], AF.Ln,
                                                  bias=np.float32(LNEPS)), sA, 1)
            p.wait("s", sA, aa)
            aa = p.op("s", lambda s: s.activation(sm5[:, 2:3], sm5[:, 3:4], AF.Exp,
                                                  scale=np.float32(-0.5)), sA, 1)
            p.wait("v", sA, aa)
            vv = p.op("v", lambda v, sl=sl:
                      v.tensor_scalar(osb[sl][:, :], h2[sl][:, :], sm5[:, 5:6], sm5[:, 2:3],
                                      OP.add, OP.mult), sV, 1)
            p.wait("y", sV, vv)
            outd[sl] = p.op("y", lambda y, qt=qt, sl=sl:
                            y.dma_start(OUT[128 * qt:128 * qt + 128, :], osb[sl][:, :]), dS, 16)

        relu_done = [0, 0]
        stt_done = [0, 0]
        acc_mark = {}
        welast = [0, 0]
        for ei in range(8):
            parw = ei % 2
            if ei < 7:
                nxt = (ei + 1) % 2
                p.wait("g", sT, welast[nxt])
                wes = WE[ei + 1, :, :].rearrange("(c p) d -> p c d", p=128)
                wedn = web[nxt][:, :].rearrange("p (c d) -> p c d", d=D)
                weld[nxt] = p.op("g", lambda g, s=wes, d=wedn: g.dma_start(d, s), dG, 16)
            for qt in range(NQT):
                if ei == 0:
                    emit_gate(qt)
                u = ei * NQT + qt
                sl = u % 2
                p.wait("t", dG, weld[parw])
                p.wait("t", sA, relu_done[sl])
                for dc in range(8):
                    for half in range(2):
                        p.op("t", lambda t, qt=qt, dc=dc, hf=half, sl=sl, pw=parw:
                             t.matmul(pe_[:, 1024 * sl + 512 * hf:1024 * sl + 512 * hf + 512],
                                      y_fm[:, 2048 * dc + 128 * qt:2048 * dc + 128 * qt + 128],
                                      web[pw][:, 1024 * dc + 512 * hf:1024 * dc + 512 * hf + 512],
                                      start=(dc == 0), stop=False, skip_group_check=True),
                             None)
                for half in range(2):
                    p.op("t", lambda t, hf=half, sl=sl, ei=ei:
                         t.matmul(pe_[:, 1024 * sl + 512 * hf:1024 * sl + 512 * hf + 512],
                                  oneb[:, :], beb[:, 1024 * ei + 512 * hf:1024 * ei + 512 * hf + 512],
                                  start=False, stop=True, skip_group_check=True),
                         sT if half == 1 else None, 1)
                emm = sT.n
                welast[parw] = emm
                p.wait("s", sT, emm)
                p.wait("s", sV, stt_done[sl])
                relu_done[sl] = p.op("s", lambda s, sl=sl:
                                     s.activation(he[sl][:, :], pe_[:, 1024 * sl:1024 * sl + 1024],
                                                  AF.Relu), sA, 1)
                p.wait("v", sA, relu_done[sl])
                p.wait("v", sV, acc_mark.get(qt, 0))
                dstf = ffna[:, D * qt:D * qt + D]
                ohap = oh_all[:, 8 * qt + ei:8 * qt + ei + 1]
                if ei == 0:
                    vv = p.op("v", lambda v, d=dstf, sl=sl, o=ohap:
                              v.tensor_scalar(d, he[sl][:, :], o, None, OP.mult), sV, 1)
                else:
                    vv = p.op("v", lambda v, d=dstf, sl=sl, o=ohap:
                              v.scalar_tensor_tensor(d, he[sl][:, :], o, d, OP.mult, OP.add),
                              sV, 1)
                stt_done[sl] = vv
                acc_mark[qt] = vv
                if ei == 7:
                    emit_final(qt)

        for tn, tap in taps.items():
            p.op("g", lambda g, d=tap, s=scr[tn]: g.dma_start(d, s), dG, 16)
        p.wait("g", dG, dG.n)
        p.wait("y", dS, dS.n)
        p.run(nc)
    S45.close()
    top.close()
    return nc


def _host_inputs(inputs):
    x = np.ascontiguousarray(np.asarray(inputs["L0_attn_x"]), dtype=np.float32)
    cores = []
    tri = np.zeros((128, 128), dtype=bfnp)
    tri[np.arange(128)[:, None] > np.arange(128)[None, :]] = 1.0
    e16 = np.zeros((8, 128, 16), dtype=bfnp)
    for c in range(8):
        for pp in range(128):
            e16[c, pp, 2 * c + pp // 64] = 1.0
    ebc = np.transpose(e16, (0, 2, 1)).copy()
    f32a = lambda a: np.ascontiguousarray(np.asarray(a), dtype=np.float32)
    consts = {
        "wq": f32a(inputs["L0_attn_Wq"]), "wk": f32a(inputs["L0_attn_Wk"]),
        "wv": f32a(inputs["L0_attn_Wv"]), "wo": f32a(inputs["L0_attn_Wo"]),
        "wg": f32a(inputs["L0_ffn_Wg"]), "we": f32a(inputs["L0_ffn_We"]),
        "bg": f32a(inputs["L0_ffn_bg"]).reshape(1, D),
        "be": f32a(inputs["L0_ffn_be"]).reshape(1, 8 * D),
        "qscale": f32a(inputs["L0_attn_q_scale"]).reshape(H, 1),
        "kscale": f32a(inputs["L0_attn_k_scale"]).reshape(H, 1),
        "identb": np.eye(128, dtype=bfnp),
        "identf": np.eye(128, dtype=np.float32),
        "e16": e16, "ebc": ebc,
        "iota": np.broadcast_to(np.arange(D, dtype=np.float32), (128, D)).copy(),
        "negbigI": (np.eye(128) * -1.0e34).astype(bfnp),
        "onesb": np.ones((1, 128), dtype=bfnp),
    }
    for core in range(8):
        b, jc = core // 4, core % 4
        qtiles = [4 * s + (jc + s) % 4 for s in range(NQT)]
        xq = np.concatenate([x[b, 128 * t:128 * t + 128, :] for t in qtiles], axis=0)
        maskq = np.zeros((NKT, 128, 128), dtype=bfnp)
        for s in range(NQT):
            sig = (jc + s) % 4
            for r in range(4):
                if r == sig:
                    maskq[4 * s + r] = tri
                elif r > sig:
                    maskq[4 * s + r] = 1.0
        m = dict(consts)
        m["xkv"] = np.ascontiguousarray(x[b])
        m["xq"] = np.ascontiguousarray(xq)
        m["maskq"] = maskq
        cores.append(m)
    return cores


def kernel(**inputs):
    nc = build_nc()
    in_maps = _host_inputs(inputs)
    res = run_bass_kernel_spmd(nc, in_maps, core_ids=list(range(8)))
    if DEBUG_TAPS:
        global LAST_TAPS
        LAST_TAPS = [{k: v for k, v in r.items() if k.startswith("tap_")}
                     for r in res.results]
    out = np.empty((2, NB, D), dtype=np.float32)
    for core in range(8):
        b, jc = core // 4, core % 4
        o = res.results[core]["out"]
        for s in range(NQT):
            t = 4 * s + (jc + s) % 4
            out[b, 128 * t:128 * t + 128, :] = o[128 * s:128 * s + 128, :]
    return out


# version 20
# speedup vs baseline: 1.2769x; 1.0403x over previous
import contextlib
import numpy as np
import ml_dtypes
import concourse.bass as bass
import concourse.mybir as mybir
from concourse.bass_utils import run_bass_kernel_spmd
from concourse.alu_op_type import AluOpType as OP

F32 = mybir.dt.float32
BF16 = mybir.dt.bfloat16
U8 = mybir.dt.uint8
AF = mybir.ActivationFunctionType
AX = mybir.AxisListType
bfnp = ml_dtypes.bfloat16

H, HD, D = 16, 64, 1024
NB, NKT, NG = 8192, 64, 16
NQ, NQT, NQG = 2048, 16, 4
CS = [4 * s + 4 for s in range(NQT)]
TSHIFT = 60.0
EPS = 1e-6
LNEPS = 1e-5
RD = 1.0 / D

DEBUG_TAPS = []


class Sem:
    def __init__(self, h):
        self.h = h
        self.n = 0


class Ph:
    def __init__(self, name):
        self.name = name
        self.q = {"t": [], "s": [], "v": [], "y": [], "g": []}

    def wait(self, ek, sem, val):
        if val > 0:
            self.q[ek].append(lambda e, s=sem.h, v=val: e.wait_ge(s, v))

    def op(self, ek, bld, sem=None, k=1):
        if sem is None:
            self.q[ek].append(lambda e, b=bld: b(e))
            return None
        sem.n += k
        v = sem.n
        self.q[ek].append(lambda e, b=bld, s=sem.h, kk=k: b(e).then_inc(s, kk))
        return v

    def run(self, nc):
        with nc.Block(self.name) as blk:
            for attr, key in (("sync", "y"), ("gpsimd", "g"), ("tensor", "t"),
                              ("scalar", "s"), ("vector", "v")):
                fns = self.q[key]
                if fns:
                    getattr(blk, attr)(lambda e, fns=fns: [f(e) for f in fns])


def build_nc():
    nc = bass.Bass("TRN2", target_bir_lowering=False)

    def din(name, shape, dt):
        return nc.dram_tensor(name, list(shape), dt, kind="ExternalInput").ap()

    XKV = din("xkv", [NB, D], F32)
    XQ = din("xq", [NQ, D], F32)
    WQ = din("wq", [D, D], F32)
    WK = din("wk", [D, D], F32)
    WV = din("wv", [D, D], F32)
    WO = din("wo", [D, D], F32)
    WG = din("wg", [D, D], F32)
    WE = din("we", [8, D, D], F32)
    BG = din("bg", [1, D], F32)
    BE = din("be", [1, 8 * D], F32)
    QS = din("qscale", [H, 1], F32)
    KSc = din("kscale", [H, 1], F32)
    IDB = din("identb", [128, 128], BF16)
    IDF = din("identf", [128, 128], F32)
    E16 = din("e16", [8, 128, 16], BF16)
    EBC = din("ebc", [8, 16, 128], BF16)
    IOTA = din("iota", [128, D], F32)
    NBI = din("negbigI", [128, 128], BF16)
    ONE = din("onesb", [1, 128], BF16)
    MQ = din("maskq", [NKT, 128, 128], BF16)

    OUT = nc.dram_tensor("out", [NQ, D], F32, kind="ExternalOutput").ap()

    Ks = nc.dram_tensor("Ks", [H, HD, NB], BF16).ap()
    Vs = nc.dram_tensor("Vs", [NKT, 128, H * 65], BF16).ap()
    Qs = nc.dram_tensor("Qs", [H, HD, NQ], BF16).ap()
    Ys = nc.dram_tensor("Ys", [NQ, D], F32).ap()
    scr = {"Ks": Ks, "Vs": Vs, "Qs": Qs, "Ys": Ys, "attn": Ys}
    taps = {}
    for tn in DEBUG_TAPS:
        if tn == "attn":
            continue
        dt = F32 if tn == "Ys" else BF16
        taps[tn] = nc.dram_tensor("tap_" + tn, list(scr[tn].shape), dt,
                                  kind="ExternalOutput").ap()
    tap_attn = tap_lall = None
    if "attn" in DEBUG_TAPS:
        taps.pop("attn", None)
        tap_attn = nc.dram_tensor("tap_attn", [128, 8 * NQ], BF16,
                                  kind="ExternalOutput").ap()
        tap_lall = nc.dram_tensor("tap_lall", [H, NQ], F32,
                                  kind="ExternalOutput").ap()

    top = contextlib.ExitStack()
    E = top.enter_context

    dS = Sem(E(nc.semaphore("dS")))
    dG = Sem(E(nc.semaphore("dG")))
    sT = Sem(E(nc.semaphore("sT")))
    sA = Sem(E(nc.semaphore("sA")))
    sV = Sem(E(nc.semaphore("sV")))
    sG = Sem(E(nc.semaphore("sG")))

    idb = E(nc.sbuf_tensor("idb", [128, 128], BF16))
    idf = E(nc.sbuf_tensor("idf", [128, 128], F32))
    e16 = E(nc.sbuf_tensor("e16_sb", [128, 8 * 16], BF16))
    ebc = E(nc.sbuf_tensor("ebc_sb", [16, 8 * 128], BF16))
    nbi = E(nc.sbuf_tensor("nbi", [128, 128], BF16))
    oneb = E(nc.sbuf_tensor("oneb", [1, 128], BF16))
    qs_c = E(nc.sbuf_tensor("qs_c", [H, 1], F32))
    ks_c = E(nc.sbuf_tensor("ks_c", [H, 1], F32))
    bgb = E(nc.sbuf_tensor("bgb", [1, D], BF16))
    beb = E(nc.sbuf_tensor("beb", [1, 8 * D], BF16))

    S23 = contextlib.ExitStack()
    nrB = S23.enter_context(nc.sbuf_tensor("nrB", [H, NB], BF16))
    bcols = S23.enter_context(nc.sbuf_tensor("bcols", [128, H * NKT], F32))
    r2k = S23.enter_context(nc.sbuf_tensor("r2k", [H, NB], BF16))
    c_own = S23.enter_context(nc.sbuf_tensor("c_own", [H, NQ], BF16))
    a_own = S23.enter_context(nc.sbuf_tensor("a_own", [H, NQ], BF16))
    r2q = S23.enter_context(nc.sbuf_tensor("r2q", [H, NQ], BF16))
    cc = S23.enter_context(nc.sbuf_tensor("ccs", [16, 8], F32))
    SW = contextlib.ExitStack()
    wT_q = SW.enter_context(nc.sbuf_tensor("wT_q", [128, 8 * D], BF16))
    wT_k = SW.enter_context(nc.sbuf_tensor("wT_k", [128, 8 * D], BF16))
    wT_v = SW.enter_context(nc.sbuf_tensor("wT_v", [128, 8 * D], BF16))
    S34 = contextlib.ExitStack()
    S45 = contextlib.ExitStack()

    with contextlib.ExitStack() as L:
        p = Ph("prep")
        wst = [L.enter_context(nc.sbuf_tensor(f"wst{i}", [128, D], F32)) for i in range(2)]
        wca = [L.enter_context(nc.sbuf_tensor(f"wca{i}", [128, D], BF16)) for i in range(2)]
        pstr = L.enter_context(nc.psum_tensor("pstr0", [128, 1024], F32))

        for sb, src in [(idb, IDB), (idf, IDF), (nbi, NBI), (oneb, ONE),
                        (qs_c, QS), (ks_c, KSc)]:
            p.op("y", lambda y, a=sb, b=src: y.dma_start(a[:, :], b[:, :]), dS, 16)
        constmark = dS.n
        p.op("y", lambda y: y.dma_start(e16[:, :], E16[:, :, :].rearrange("c p h -> p c h")), dS, 16)
        p.op("y", lambda y: y.dma_start(ebc[:, :], EBC[:, :, :].rearrange("c h p -> h c p")), dS, 16)
        p.op("g", lambda g: g.dma_start(bgb[:, :], BG[:, :]), dG, 16)
        p.op("g", lambda g: g.dma_start(beb[:, :], BE[:, :]), dG, 16)

        ldmark = [0, 0]
        castmark = [0, 0]
        trmark = [0, 0]
        evmark = [0, 0]
        it = 0
        for Wsrc, wT in [(WQ, wT_q), (WK, wT_k), (WV, wT_v)]:
            for fc in range(8):
                par = it % 2
                p.wait("y", sG, castmark[par])
                ldmark[par] = p.op("y", lambda y, w=Wsrc, fc=fc, pr=par:
                                   y.dma_start(wst[pr][:, :], w[128 * fc:128 * fc + 128, :]),
                                   dS, 16)
                p.wait("g", dS, ldmark[par])
                p.wait("g", sT, trmark[par])
                castmark[par] = p.op("g", lambda g, pr=par:
                                     g.tensor_copy(wca[pr][:, :], wst[pr][:, :]), sG, 1)
                p.wait("t", sG, castmark[par])
                for half in range(2):
                    p.wait("t", sV, evmark[half])
                    for k4 in range(4):
                        dc = half * 4 + k4
                        p.op("t", lambda t, pr=par, dc=dc, k4=k4, hh=half:
                             t.matmul(pstr[:, 512 * hh + 128 * k4:512 * hh + 128 * k4 + 128],
                                      wca[pr][:, 128 * dc:128 * dc + 128], idb[:, :],
                                      start=True, stop=True, skip_group_check=True),
                             sT if k4 == 3 else None, 1)
                    tmark = sT.n
                    dst = wT[:, :].rearrange("p (dc f) -> p dc f", f=D)[
                        :, half * 4:half * 4 + 4, 128 * fc:128 * fc + 128]
                    src_ = pstr[:, 512 * half:512 * half + 512].rearrange(
                        "p (a b) -> p a b", b=128)
                    p.wait("v", sT, tmark)
                    evmark[half] = p.op("v", lambda v, d=dst, s=src_: v.tensor_copy(d, s), sV, 1)
                trmark[par] = sT.n
                it += 1
        p.wait("y", dS, dS.n)
        p.wait("g", dG, dG.n)
        p.run(nc)

    with contextlib.ExitStack() as L:
        p = Ph("qkv")
        xb = [L.enter_context(nc.sbuf_tensor(f"xb{i}", [128, 4 * D], BF16)) for i in range(2)]
        xfm = [L.enter_context(nc.sbuf_tensor(f"xfm{i}", [128, 4 * D], BF16)) for i in range(2)]
        kf = [L.enter_context(nc.sbuf_tensor(f"kf{i}", [128, 1024], BF16)) for i in range(2)]
        ksq = [L.enter_context(nc.sbuf_tensor(f"ksq{i}", [128, 1024], BF16)) for i in range(2)]
        vx = [L.enter_context(nc.sbuf_tensor(f"vx{i}", [128, H * 65], BF16)) for i in range(2)]
        r2c = [L.enter_context(nc.sbuf_tensor(f"r2c{i}", [16, 512], F32)) for i in range(2)]
        ptr = L.enter_context(nc.psum_tensor("ptr1", [128, 1024], F32))
        pkq = L.enter_context(nc.psum_tensor("pkq1", [128, 2048], F32))
        pr2 = L.enter_context(nc.psum_tensor("pr2", [16, 512], F32))
        pbc = L.enter_context(nc.psum_tensor("pbc", [128, 32], F32))

        for i in range(2):
            p.op("g", lambda g, i=i: g.memset(vx[i][:, :], 1.0), sG, 1)
        p.op("g", lambda g: g.memset(cc[:, 2:3], 0.0), sG, 1)
        msmark = sG.n

        def cv1(bld):
            v = p.op("v", bld, sV, 1)
            p.wait("v", sV, v)
            return v

        def cs1(bld):
            v = p.op("s", bld, sA, 1)
            p.wait("s", sA, v)
            return v

        p.wait("v", dS, constmark)
        cv1(lambda v: v.tensor_scalar(cc[:, 0:1], ks_c[:, :], -1.0, None, OP.mult))
        cv1(lambda v: v.tensor_max(cc[:, 0:1], cc[:, 0:1], ks_c[:, :]))

        chain_s = [0, 0]
        chain_v = [0, 0]
        bmark = [0, 0]
        btr_t = [0, 0]
        bevac = [0, 0]

        def emit_kchain(par, gidx):
            p.wait("v", sT, sqmm[1])
            p.wait("v", sT, btr_t[par])
            cpm = p.op("v", lambda v, pr=par: v.tensor_copy(r2c[pr][:, :], pr2[:, :]), sV, 1)
            r2evac[0] = cpm
            bmark[par] = cpm
            p.wait("v", sV, cpm)
            cv1(lambda v, pr=par, g2=gidx:
                v.tensor_copy(r2k[:, 512 * g2:512 * g2 + 512], r2c[pr][:, :]))
            cv1(lambda v, pr=par: v.reduce_max(cc[:, 3:4], r2c[pr][:, :], axis=AX.X))
            if gidx == 0:
                p.wait("v", sG, msmark)
            cv1(lambda v: v.tensor_max(cc[:, 2:3], cc[:, 2:3], cc[:, 3:4]))

        def emit_ktail(par, gidx):
            for i in range(4):
                ch = 4 * gidx + i
                sl2b = i % 2
                p.wait("t", sV, bevac[sl2b])
                if i == 0:
                    p.wait("t", sV, bmark[par])
                p.op("t", lambda t, pr=par, i=i, sl2b=sl2b:
                     t.transpose(pbc[:, 16 * sl2b:16 * sl2b + 16],
                                 r2c[pr][0:16, 128 * i:128 * i + 128],
                                 idf[0:16, 0:16]), sT, 1)
                tm = sT.n
                p.wait("v", sT, tm)
                dstb = bcols[:, :].rearrange("p (h k) -> p h k", k=NKT)[:, :, ch:ch + 1]
                bevac[sl2b] = p.op("v", lambda v, d=dstb, sl2b=sl2b:
                                   v.tensor_copy(d, pbc[:, 16 * sl2b:16 * sl2b + 16]), sV, 1)
            btr_t[par] = sT.n

        trlast = [0, 0]
        gend = [0, 0]
        xld = [0, 0]
        evdone = [0, 0]
        kfdma = [0, 0]
        sqmm = [0, 0]
        vxdma = [0, 0]
        trev = [0, 0]
        r2evac = [0]

        gi = 0
        for is_q, n_groups in ((False, NG), (True, NQG)):
            for gidx in range(n_groups):
                par = gi % 2
                Xsrc = XQ if is_q else XKV
                wTx = wT_q if is_q else wT_k
                Dst = Qs if is_q else Ks
                prx = pr2

                p.wait("g", sT, trlast[par])
                src = Xsrc[512 * gidx:512 * gidx + 512, :].rearrange("(j p) d -> p j d", p=128)
                xdst = xb[par][:, :].rearrange("p (j d) -> p j d", d=D)
                xld[par] = p.op("g", lambda g, s=src, d=xdst: g.dma_start(d, s), dG, 16)

                p.wait("t", dG, xld[par])
                first_ev = True
                for dc in range(8):
                    hb = dc % 2
                    p.wait("t", sV, trev[hb])
                    for j in range(4):
                        p.op("t", lambda t, pr=par, dc=dc, j=j, hb=hb:
                             t.matmul(ptr[:, 512 * hb + 128 * j:512 * hb + 128 * j + 128],
                                      xb[pr][:, 1024 * j + 128 * dc:1024 * j + 128 * dc + 128],
                                      idb[:, :], start=True, stop=True, skip_group_check=True),
                             sT if j == 3 else None, 1)
                    tmark = sT.n
                    p.wait("v", sT, tmark)
                    if first_ev:
                        p.wait("v", sT, gend[par])
                        first_ev = False
                    trev[hb] = p.op("v", lambda v, pr=par, dc=dc, hb=hb:
                                    v.tensor_copy(xfm[pr][:, 512 * dc:512 * dc + 512],
                                                  ptr[:, 512 * hb:512 * hb + 512]), sV, 1)
                trlast[par] = sT.n

                pend_e16 = []
                for fp2 in range(4):
                    sl2 = fp2 % 2
                    p.wait("t", sA, evdone[sl2])
                    if fp2 == 0:
                        p.wait("t", sV, trev[0])
                        p.wait("t", sV, trev[1])
                        p.wait("t", sV, r2evac[0])
                    for dc in range(8):
                        for hb in range(2):
                            fc = 2 * fp2 + hb
                            p.op("t", lambda t, pr=par, fc=fc, dc=dc, hb=hb, sl2=sl2, w=wTx:
                                 t.matmul(pkq[:, 1024 * sl2 + 512 * hb:1024 * sl2 + 512 * hb + 512],
                                          w[:, 1024 * dc + 128 * fc:1024 * dc + 128 * fc + 128],
                                          xfm[pr][:, 512 * dc:512 * dc + 512],
                                          start=(dc == 0), stop=(dc == 7),
                                          skip_group_check=True),
                                 sT if (dc == 7 and hb == 1) else None, 1)
                    mmk = sT.n
                    for (fcp, slp, sqd) in pend_e16:
                        p.wait("t", sA, sqd)
                        for hb2 in range(2):
                            sqmm[slp] = p.op(
                                "t", lambda t, p2=slp, fc=fcp + hb2, hb2=hb2, pr_=prx:
                                t.matmul(pr_[:, :], e16[:, 16 * fc:16 * fc + 16],
                                         ksq[p2][:, 512 * hb2:512 * hb2 + 512],
                                         start=(fc == 0), stop=(fc == 7),
                                         skip_group_check=True), sT, 1)
                    pend_e16 = []
                    p.wait("s", sT, mmk)
                    p.wait("s", dS, kfdma[sl2])
                    kevac_ = p.op("s", lambda s, p2=sl2, sl2=sl2:
                                  s.activation(kf[p2][:, :],
                                               pkq[:, 1024 * sl2:1024 * sl2 + 1024],
                                               AF.Silu), sA, 1)
                    evdone[sl2] = kevac_
                    p.wait("s", sT, sqmm[sl2])
                    sqdone = p.op("s", lambda s, p2=sl2:
                                  s.activation(ksq[p2][:, :], kf[p2][:, :], AF.Square),
                                  sA, 1)
                    pend_e16.append((2 * fp2, sl2, sqdone))
                    p.wait("y", sA, kevac_)
                    for hb in range(2):
                        fc = 2 * fp2 + hb
                        d0 = Dst[2 * fc, :, 512 * gidx:512 * gidx + 512]
                        d1 = Dst[2 * fc + 1, :, 512 * gidx:512 * gidx + 512]
                        p.op("y", lambda g, p2=sl2, d=d0, hb=hb:
                             g.dma_start(d, kf[p2][0:64, 512 * hb:512 * hb + 512]), dS, 16)
                        kfdma[sl2] = p.op(
                            "y", lambda g, p2=sl2, d=d1, hb=hb:
                            g.dma_start(d, kf[p2][64:128, 512 * hb:512 * hb + 512]), dS, 16)

                flush_e16 = pend_e16
                pend_e16 = []

                if not is_q:
                    for j in range(4):
                        pv_ = j % 2
                        sl2 = j % 2
                        p.wait("t", sA, evdone[sl2])
                        p.wait("t", dS, vxdma[pv_])
                        for dc in range(8):
                            for half in range(2):
                                p.op("t", lambda t, pr=par, j=j, dc=dc, hf=half, sl2=sl2:
                                     t.matmul(pkq[:, 1024 * sl2 + 512 * hf:1024 * sl2 + 512 * hf + 512],
                                              xfm[pr][:, 512 * dc + 128 * j:512 * dc + 128 * j + 128],
                                              wT_v[:, 1024 * dc + 512 * hf:1024 * dc + 512 * hf + 512],
                                              start=(dc == 0), stop=(dc == 7),
                                              skip_group_check=True),
                                     sT if (dc == 7 and half == 1) else None, 1)
                        vmm = sT.n
                        if flush_e16:
                            for (fcp, slp, sqd) in flush_e16:
                                p.wait("t", sA, sqd)
                                for hb2 in range(2):
                                    sqmm[slp] = p.op(
                                        "t", lambda t, p2=slp, fc=fcp + hb2, hb2=hb2, pr_=prx:
                                        t.matmul(pr_[:, :], e16[:, 16 * fc:16 * fc + 16],
                                                 ksq[p2][:, 512 * hb2:512 * hb2 + 512],
                                                 start=(fc == 0), stop=(fc == 7),
                                                 skip_group_check=True), sT, 1)
                            flush_e16 = []
                            emit_kchain(par, gidx)
                        p.wait("s", sT, vmm)
                        p.wait("s", dS, vxdma[pv_])
                        dr = vx[pv_][:, :].rearrange("p (h j2) -> p h j2", j2=65)
                        evdone[sl2] = p.op("s", lambda s, pv=pv_, dr=dr, sl2=sl2:
                                           s.activation(dr[:, :, 0:64],
                                                        pkq[:, 1024 * sl2:1024 * sl2 + 1024]
                                                        .rearrange("p (h f) -> p h f", f=64),
                                                        AF.Silu), sA, 1)
                        kt = 4 * gidx + j
                        p.wait("y", sA, evdone[sl2])
                        vxdma[pv_] = p.op("y", lambda g, pv=pv_, kt=kt:
                                          g.dma_start(Vs[kt, :, :], vx[pv][:, :]), dS, 16)
                    emit_ktail(par, gidx)
                else:
                    for (fcp, slp, sqd) in flush_e16:
                        p.wait("t", sA, sqd)
                        for hb2 in range(2):
                            sqmm[slp] = p.op(
                                "t", lambda t, p2=slp, fc=fcp + hb2, hb2=hb2, pr_=prx:
                                t.matmul(pr_[:, :], e16[:, 16 * fc:16 * fc + 16],
                                         ksq[p2][:, 512 * hb2:512 * hb2 + 512],
                                         start=(fc == 0), stop=(fc == 7),
                                         skip_group_check=True), sT, 1)
                    flush_e16 = []
                    p.wait("v", sT, sqmm[1])
                    r2dst = r2q[:, 512 * gidx:512 * gidx + 512]
                    r2evac[0] = p.op("v", lambda v, d=r2dst, pr_=prx:
                                     v.tensor_copy(d, pr_[:, :]), sV, 1)
                gend[par] = sT.n
                gi += 1
        p.wait("g", dG, dG.n)
        p.wait("y", dS, dS.n)
        p.run(nc)
    SW.close()

    attn = S34.enter_context(nc.sbuf_tensor("attn", [128, 8 * NQ], BF16, side="right"))
    l_all = S34.enter_context(nc.sbuf_tensor("l_all", [H, NQ], F32, side="right"))
    with contextlib.ExitStack() as L:
        p = Ph("attn")
        ksb = [L.enter_context(nc.sbuf_tensor(f"ksb{i}", [65, NB], BF16)) for i in range(2)]
        vsb = [L.enter_context(nc.sbuf_tensor(f"vsb{i}", [128, NKT * 65], BF16)) for i in range(2)]
        qsb = [L.enter_context(nc.sbuf_tensor(f"qsb{i}", [65, NQ // 2], BF16)) for i in range(2)]
        arow = [L.enter_context(nc.sbuf_tensor(f"ar{i}", [1, NQ // 2], BF16)) for i in range(2)]
        pts = [L.enter_context(nc.sbuf_tensor(f"pts{i}", [128, 1024], BF16)) for i in range(3)]
        lst = L.enter_context(nc.sbuf_tensor("lst", [1, NQ], F32))
        mq = L.enter_context(nc.sbuf_tensor("mq", [128, NKT * 128], BF16))
        kunion = L.enter_context(nc.sbuf_tensor("kunion", [128, NQ], F32))
        q1 = kunion[0:16, 0:NQ]
        ktmp = kunion[0:128, 0:1024]
        q2 = L.enter_context(nc.sbuf_tensor("cq2", [H, NQ], BF16))
        pst = L.enter_context(nc.psum_tensor("pst3", [128, 3072], F32))
        ppv = L.enter_context(nc.psum_tensor("ppv3", [65, 1024], F32))

        p.op("y", lambda y: y.dma_start(mq[:, :], MQ[:, :, :].rearrange("j p q -> p j q")),
             dS, 16)

        def cv(bld):
            v = p.op("v", bld, sV, 1)
            p.wait("v", sV, v)
            return v

        def ca(bld):
            v = p.op("s", bld, sA, 1)
            p.wait("s", sA, v)
            return v

        ca(lambda s: s.activation(ktmp, bcols[:, :], AF.Ln, bias=np.float32(1e-30)))
        m = p.op("s", lambda s: s.activation(bcols[:, :], ktmp, AF.Exp,
                                             scale=np.float32(-0.5)), sA, 1)
        p.wait("v", sA, m)
        m = cv(lambda v: v.tensor_scalar(bcols[:, :], bcols[:, :], EPS, 1.0,
                                         OP.mult, OP.add))
        p.wait("s", sV, m)
        ca(lambda s: s.activation(ktmp, bcols[:, :], AF.Ln))
        m = p.op("s", lambda s: s.activation(bcols[:, :], ktmp, AF.Exp,
                                             scale=np.float32(-1.0)), sA, 1)
        p.wait("v", sA, m)
        p.wait("s", sA, m)
        nrb_m = [0, 0, 0, 0]
        for chk in range(4):
            c0 = 2048 * chk
            ca(lambda s, c0=c0: s.activation(q1, r2k[:, c0:c0 + 2048], AF.Ln,
                                             bias=np.float32(1e-30)))
            if chk > 0:
                p.wait("s", sV, nrb_m[chk - 1])
            m = p.op("s", lambda s: s.activation(q2[:, :], q1, AF.Exp,
                                                 scale=np.float32(-0.5)), sA, 1)
            p.wait("v", sA, m)
            nrb_m[chk] = cv(lambda v, c0=c0:
                            v.tensor_scalar(nrB[:, c0:c0 + 2048], q2[:, :], -EPS, -1.0,
                                            OP.mult, OP.add))
        ca(lambda s: s.activation(cc[:, 3:4], cc[:, 2:3], AF.Ln, bias=np.float32(1e-30)))
        m = p.op("s", lambda s: s.activation(cc[:, 2:3], cc[:, 3:4], AF.Exp,
                                             scale=np.float32(0.5)), sA, 1)
        p.wait("v", sA, m)
        cv(lambda v: v.tensor_mul(cc[:, 5:6], cc[:, 0:1], cc[:, 2:3]))
        ca(lambda s: s.activation(q1, r2q[:, :], AF.Ln, bias=np.float32(1e-30)))
        m = p.op("s", lambda s: s.activation(r2q[:, :], q1, AF.Exp,
                                             scale=np.float32(0.5)), sA, 1)
        p.wait("s", sA, m)
        ca(lambda s: s.activation(q1, r2q[:, :], AF.Ln, bias=np.float32(EPS)))
        p.wait("s", sV, nrb_m[3])
        m = p.op("s", lambda s: s.activation(q2[:, :], q1, AF.Exp,
                                             scale=np.float32(-1.0)), sA, 1)
        p.wait("v", sA, m)
        cv(lambda v: v.scalar_tensor_tensor(q1, r2q[:, :], qs_c[:, :], q2[:, :],
                                            OP.mult, OP.mult))
        cv(lambda v: v.tensor_scalar(a_own[:, :], q1, ks_c[:, :], None, OP.mult))
        cv(lambda v: v.tensor_scalar(q2[:, :], q1, -1.0, None, OP.mult))
        cv(lambda v: v.tensor_max(q2[:, :], q2[:, :], q1))
        cv(lambda v: v.tensor_mul(q2[:, :], q2[:, :], r2q[:, :]))
        qchain_end = cv(lambda v: v.tensor_scalar(c_own[:, :], q2[:, :], cc[:, 5:6],
                                                  None, OP.mult))

        sweep_done = [0, 0]
        exp_done = [0, 0, 0]
        pv_done = [0, 0, 0]
        pv_evac = {}
        ldma_prev = 0
        lds_a = [0] * 33
        ldg_a = [0] * 33
        asv_a = [0] * 33

        items = [(P_, h) for P_ in range(2) for h in range(H)]

        def emit_loads(it2):
            P2, h2 = items[it2]
            par2 = it2 % 2
            J2 = 32 if P2 == 0 else NKT
            p.wait("y", sV, sweep_done[par2])
            p.op("y", lambda y, h=h2, pr=par2, J=J2:
                 y.dma_start(ksb[pr][0:64, 0:128 * J], Ks[h, :, 0:128 * J]), dS, 16)
            p.op("y", lambda y, h=h2, pr=par2, P_=P2:
                 y.dma_start(qsb[pr][0:64, :], Qs[h, :, 1024 * P_:1024 * P_ + 1024]),
                 dS, 16)
            vsrc = Vs[0:J2, :, 65 * h2:65 * h2 + 65].rearrange("k p j -> p k j")
            vdst3 = vsb[par2][:, 0:65 * J2].rearrange("p (k j) -> p k j", j=65)
            lds_a[it2] = p.op("y", lambda y, s=vsrc, d=vdst3: y.dma_start(d, s), dS, 16)
            p.wait("g", sV, sweep_done[par2])
            if it2 < 2:
                p.wait("g", sV, qchain_end)
            if it2 == 0:
                p.wait("g", sV, nrb_m[1])
            if it2 == 16:
                p.wait("g", sV, nrb_m[3])
            p.op("g", lambda g, h=h2, pr=par2, J=J2:
                 g.dma_start(ksb[pr][64:65, 0:128 * J], nrB[h:h + 1, 0:128 * J]), dG, 16)
            p.op("g", lambda g, h=h2, pr=par2, P_=P2:
                 g.dma_start(qsb[pr][64:65, :], c_own[h:h + 1, 1024 * P_:1024 * P_ + 1024]),
                 dG, 16)
            ldg_a[it2] = p.op("g", lambda g, h=h2, pr=par2, P_=P2:
                              g.dma_start(arow[pr][:, :],
                                          a_own[h:h + 1, 1024 * P_:1024 * P_ + 1024]), dG, 16)

        def emit_ascale(it2, bA):
            par2 = it2 % 2
            p.wait("t", dS, lds_a[it2])
            p.wait("t", dG, ldg_a[it2])
            p.wait("t", sA, exp_done[bA])
            asv = 0
            for i2 in range(2):
                p.op("t", lambda t, pr=par2, i2=i2, bA=bA:
                     t.matmul(pst[0:64, 1024 * bA + 512 * i2:1024 * bA + 512 * i2 + 512],
                              oneb[0:1, 0:64], arow[pr][:, 512 * i2:512 * i2 + 512],
                              start=True, stop=True, skip_group_check=True),
                     sT if i2 == 1 else None, 1)
            am = sT.n
            p.wait("v", sT, am)
            p.wait("v", dS, lds_a[it2])
            for i2 in range(2):
                asv = p.op("v", lambda v, pr=par2, i2=i2, bA=bA:
                           v.tensor_mul(qsb[pr][0:64, 512 * i2:512 * i2 + 512],
                                        qsb[pr][0:64, 512 * i2:512 * i2 + 512],
                                        pst[0:64, 1024 * bA + 512 * i2:1024 * bA + 512 * i2 + 512]),
                           sV, 1)
            asv_a[it2] = asv

        emit_loads(0)
        emit_loads(1)
        emit_ascale(0, 2)
        for it, (P_, h) in enumerate(items):
            par = it % 2
            JP = 32 if P_ == 0 else NKT
            p.wait("t", sV, asv_a[it])

            def emit_pv(j):
                bj = j % 3
                p.wait("t", sA, exp_done[bj])
                s0 = j // 4
                slo = max(s0, 8 * P_) - 8 * P_
                for bl in range(slo // 4, 2):
                    sa = max(slo, 4 * bl)
                    n = (4 * bl + 4 - sa) * 128
                    pcol = 128 * sa
                    first = (j == 0)
                    bg = 2 * P_ + bl
                    lastj = (j == CS[4 * bg + 3] - 1)
                    if first and it > 1:
                        p.wait("t", sV, pv_evac.get((it - 2, bl), 0))
                    p.op("t", lambda t, pr=par, j=j, n=n, pcol=pcol,
                         first=first, lastj=lastj, bj=bj:
                         t.matmul(ppv[:, pcol:pcol + n],
                                  vsb[pr][:, 65 * j:65 * j + 65],
                                  pts[bj][:, pcol:pcol + n],
                                  start=first, stop=lastj,
                                  skip_group_check=True), sT, 1)
                    pv_done[bj] = sT.n
                    if lastj:
                        pvm = sT.n
                        p.wait("v", sT, pvm)
                        sg = 8 * P_ + 4 * bl
                        bc = 512 * bl
                        dsta = attn[64 * (h % 2):64 * (h % 2) + 64,
                                    (h // 2) * NQ + 128 * sg:(h // 2) * NQ + 128 * sg + 512]
                        p.op("v", lambda v, d=dsta, bc=bc:
                             v.tensor_copy(d, ppv[0:64, bc:bc + 512]), sV, 1)
                        if bl == 0:
                            p.wait("v", dG, ldma_prev)
                        pv_evac[(it, bl)] = p.op(
                            "v", lambda v, bc=bc, sg=sg:
                            v.tensor_copy(lst[0:1, 128 * sg:128 * sg + 512],
                                          ppv[64:65, bc:bc + 512]), sV, 1)

            for j in range(JP):
                bj = j % 3
                s0 = j // 4
                slo = max(s0, 8 * P_) - 8 * P_
                p.wait("t", sA, exp_done[bj])
                s = slo
                while s < 8:
                    s2 = min((s // 4) * 4 + 3, 7)
                    n = (s2 - s + 1) * 128
                    tail = (s <= s0 - 8 * P_ <= s2) and (s0 >= 8 * P_)
                    col = 1024 * bj + 128 * s
                    is_last = (s2 == 7) and not tail
                    p.op("t", lambda t, pr=par, j=j, col=col, n=n, s=s:
                         t.matmul(pst[:, col:col + n],
                                  ksb[pr][0:65, 128 * j:128 * j + 128],
                                  qsb[pr][0:65, 128 * s:128 * s + n],
                                  start=True, stop=not tail,
                                  skip_group_check=True),
                         sT if is_last else None, 1)
                    if tail:
                        mcol = 1024 * bj + 128 * (s0 - 8 * P_)
                        is_last2 = (s2 == 7)
                        p.op("t", lambda t, j=j, mcol=mcol:
                             t.matmul(pst[:, mcol:mcol + 128], nbi[:, :],
                                      mq[:, 128 * j:128 * j + 128],
                                      start=False, stop=True,
                                      skip_group_check=True),
                             sT if is_last2 else None, 1)
                    s = s2 + 1
                stm = sT.n
                cL = 128 * slo
                nA = 1024 - cL
                p.wait("s", sT, stm)
                p.wait("s", sT, pv_done[bj])
                exp_done[bj] = p.op("s", lambda s_, h=h, j=j, cL=cL, nA=nA, bj=bj:
                                    s_.activation(pts[bj][:, cL:cL + nA],
                                                  pst[:, 1024 * bj + cL:1024 * bj + cL + nA],
                                                  AF.Exp, bias=np.float32(TSHIFT),
                                                  scale=bcols[:, NKT * h + j:NKT * h + j + 1]),
                                    sA, 1)
                if j == JP - 2 and it + 1 < len(items):
                    emit_ascale(it + 1, JP % 3)
                if j >= 2:
                    emit_pv(j - 2)
            emit_pv(JP - 2)
            emit_pv(JP - 1)
            p.wait("g", sV, pv_evac[(it, 1)])
            ldma_prev = p.op("g", lambda g, h=h, P_=P_:
                             g.dma_start(l_all[h:h + 1, 1024 * P_:1024 * P_ + 1024],
                                         lst[0:1, 1024 * P_:1024 * P_ + 1024]), dG, 16)
            sweep_done[par] = pv_evac[(it, 1)]
            if it + 2 < len(items):
                emit_loads(it + 2)
        p.wait("g", dG, dG.n)
        p.wait("y", dS, dS.n)
        p.run(nc)
    S23.close()

    y_fm = S45.enter_context(nc.sbuf_tensor("y_fm", [128, 8 * NQ], BF16))
    p_all = S45.enter_context(nc.sbuf_tensor("p_all", [128, NQT], F32))
    oh_all = S45.enter_context(nc.sbuf_tensor("oh_all", [128, NQT * 8], F32))
    with contextlib.ExitStack() as L:
        p = Ph("woln")
        wst = [L.enter_context(nc.sbuf_tensor(f"wst4{i}", [128, D], F32)) for i in range(2)]
        wca = [L.enter_context(nc.sbuf_tensor(f"wca4{i}", [128, D], BF16)) for i in range(2)]
        wT_o = L.enter_context(nc.sbuf_tensor("wT_o", [128, 8 * D], BF16))
        lrec = L.enter_context(nc.sbuf_tensor("lrec", [H, NQ], F32))
        lrecb = L.enter_context(nc.sbuf_tensor("lrecb", [H, NQ], BF16))
        xres = [L.enter_context(nc.sbuf_tensor(f"xres{i}", [128, D], F32)) for i in range(3)]
        y0 = [L.enter_context(nc.sbuf_tensor(f"y0_{i}", [128, D], F32)) for i in range(3)]
        y2 = [L.enter_context(nc.sbuf_tensor(f"y2_{i}", [128, D], F32)) for i in range(3)]
        ybf = [L.enter_context(nc.sbuf_tensor(f"ybf{i}", [128, D], BF16)) for i in range(3)]
        sqs = L.enter_context(nc.sbuf_tensor("sqs", [128, D], BF16))
        smc = L.enter_context(nc.sbuf_tensor("smc", [128, 8], F32))
        pwo = L.enter_context(nc.psum_tensor("pwo", [128, 3072], F32))
        ptr4 = L.enter_context(nc.psum_tensor("ptr4", [128, 1024], F32))

        if tap_attn is not None:
            p.op("g", lambda g: g.dma_start(tap_lall[:, :], l_all[:, :]), dG, 16)
        ldmark = [0, 0]
        castmark = [0, 0]
        trmark = [0, 0]
        evmark = [0, 0]
        for fc in range(8):
            par = fc % 2
            p.wait("y", sG, castmark[par])
            ldmark[par] = p.op("y", lambda y, fc=fc, pr=par:
                               y.dma_start(wst[pr][:, :], WO[128 * fc:128 * fc + 128, :]), dS, 16)
            p.wait("g", dS, ldmark[par])
            p.wait("g", sT, trmark[par])
            castmark[par] = p.op("g", lambda g, pr=par:
                                 g.tensor_copy(wca[pr][:, :], wst[pr][:, :]), sG, 1)
            p.wait("t", sG, castmark[par])
            for half in range(2):
                p.wait("t", sV, evmark[half])
                for k4 in range(4):
                    dc = half * 4 + k4
                    p.op("t", lambda t, pr=par, dc=dc, k4=k4, hh=half:
                         t.matmul(ptr4[:, 512 * hh + 128 * k4:512 * hh + 128 * k4 + 128],
                                  wca[pr][:, 128 * dc:128 * dc + 128], idb[:, :],
                                  start=True, stop=True, skip_group_check=True),
                         sT if k4 == 3 else None, 1)
                tmark = sT.n
                dst = wT_o[:, :].rearrange("p (dc f) -> p dc f", f=D)[
                    :, half * 4:half * 4 + 4, 128 * fc:128 * fc + 128]
                src_ = ptr4[:, 512 * half:512 * half + 512].rearrange("p (a b) -> p a b", b=128)
                p.wait("v", sT, tmark)
                evmark[half] = p.op("v", lambda v, d=dst, s=src_: v.tensor_copy(d, s), sV, 1)
            trmark[par] = sT.n

        la = p.op("v", lambda v: v.reciprocal(lrec[:, :], l_all[:, :]), sV, 1)
        p.wait("v", sV, la)
        lr = p.op("v", lambda v: v.tensor_copy(lrecb[:, :], lrec[:, :]), sV, 1)
        p.wait("v", sV, lr)
        p.wait("t", sV, lr)
        nmark = [evmark[0], evmark[1]]
        for u in range(32):
            ch, g2 = u // 4, u % 4
            sl_ = u % 2
            p.wait("t", sV, nmark[sl_])
            bm = p.op("t", lambda t, ch=ch, g2=g2, sl_=sl_:
                      t.matmul(ptr4[:, 512 * sl_:512 * sl_ + 512],
                               ebc[0:16, 128 * ch:128 * ch + 128],
                               lrecb[0:16, 512 * g2:512 * g2 + 512], start=True, stop=True,
                               skip_group_check=True), sT, 1)
            p.wait("v", sT, bm)
            dsta = attn[:, 2048 * ch + 512 * g2:2048 * ch + 512 * g2 + 512]
            nmark[sl_] = p.op("v", lambda v, d=dsta, sl_=sl_:
                              v.tensor_mul(d, d, ptr4[:, 512 * sl_:512 * sl_ + 512]), sV, 1)
        normdone = max(nmark)
        if tap_attn is not None:
            p.wait("g", sV, normdone)
            p.op("g", lambda g: g.dma_start(tap_attn[:, :], attn[:, :]), dG, 16)
        p.wait("t", sV, normdone)

        h1mark = [0, 0, 0]
        ymark = [0, 0, 0]
        ydma = [0, 0, 0]
        ybfm = [0, 0, 0]
        ytr = [0, 0, 0]
        xload = [0, 0, 0]
        tr4ev = [normdone, normdone]
        for qt in range(NQT):
            sl = qt % 3
            if qt == 0:
                p.wait("t", sV, max(evmark))
            p.wait("t", sV, h1mark[sl])
            for dc in range(8):
                for half in range(2):
                    p.op("t", lambda t, qt=qt, dc=dc, hf=half, sl=sl:
                         t.matmul(pwo[:, 1024 * sl + 512 * hf:1024 * sl + 512 * hf + 512],
                                  attn[:, 2048 * dc + 128 * qt:2048 * dc + 128 * qt + 128],
                                  wT_o[:, 1024 * dc + 512 * hf:1024 * dc + 512 * hf + 512],
                                  start=(dc == 0), stop=(dc == 7), skip_group_check=True),
                         sT if (dc == 7 and half == 1) else None, 1)
            womm = sT.n
            p.wait("y", sV, h1mark[sl])
            xload[sl] = p.op("y", lambda y, qt=qt, sl=sl:
                             y.dma_start(xres[sl][:, :], XQ[128 * qt:128 * qt + 128, :]), dS, 16)
            p.wait("v", sT, womm)
            p.wait("v", dS, xload[sl])
            p.wait("v", dG, ydma[sl])
            vv = p.op("v", lambda v, sl=sl:
                      v.tensor_add(y0[sl][:, :], xres[sl][:, :],
                                   pwo[:, 1024 * sl:1024 * sl + 1024]), sV, 1)
            h1mark[sl] = vv
            p.wait("s", sV, vv)
            sqa = p.op("s", lambda s, sl=sl:
                       s.activation(sqs[:, :], y0[sl][:, :], AF.Square,
                                    accum_out=smc[:, 2:3]), sA, 1)
            p.wait("v", sV, vv)
            vv = p.op("v", lambda v, sl=sl:
                      v.reduce_sum(smc[:, 0:1], y0[sl][:, :], axis=AX.X, negate=True), sV, 1)
            p.wait("v", sV, vv)
            vv = p.op("v", lambda v: v.tensor_scalar(smc[:, 1:2], smc[:, 0:1], RD, None,
                                                     OP.mult), sV, 1)
            p.wait("v", sA, sqa)
            p.wait("v", sV, vv)
            vv = p.op("v", lambda v: v.tensor_scalar(smc[:, 5:6], smc[:, 2:3], RD, None,
                                                     OP.mult), sV, 1)
            p.wait("v", sV, vv)
            vv = p.op("v", lambda v: v.tensor_mul(smc[:, 6:7], smc[:, 1:2], smc[:, 1:2]), sV, 1)
            p.wait("v", sV, vv)
            vv = p.op("v", lambda v: v.tensor_sub(smc[:, 5:6], smc[:, 5:6], smc[:, 6:7]), sV, 1)
            p.wait("s", sV, vv)
            aa = p.op("s", lambda s: s.activation(smc[:, 3:4], smc[:, 5:6], AF.Ln,
                                                  bias=np.float32(LNEPS)), sA, 1)
            p.wait("s", sA, aa)
            aa = p.op("s", lambda s: s.activation(smc[:, 4:5], smc[:, 3:4], AF.Exp,
                                                  scale=np.float32(-0.5)), sA, 1)
            p.wait("v", sA, aa)
            vv = p.op("v", lambda v, sl=sl:
                      v.tensor_scalar(y2[sl][:, :], y0[sl][:, :], smc[:, 1:2], smc[:, 4:5],
                                      OP.add, OP.mult), sV, 1)
            ymark[sl] = vv
            p.wait("g", sV, vv)
            ydma[sl] = p.op("g", lambda g, qt=qt, sl=sl:
                            g.dma_start(Ys[128 * qt:128 * qt + 128, :], y2[sl][:, :]), dG, 16)
            p.wait("v", sV, vv)
            p.wait("v", sT, ytr[sl])
            vv = p.op("v", lambda v, sl=sl: v.tensor_copy(ybf[sl][:, :], y2[sl][:, :]), sV, 1)
            ybfm[sl] = vv
            p.wait("t", sV, ybfm[sl])
            for hb in range(2):
                p.wait("t", sV, tr4ev[hb])
                for k4 in range(4):
                    dc = hb * 4 + k4
                    p.op("t", lambda t, sl=sl, dc=dc, k4=k4, hb=hb:
                         t.matmul(ptr4[:, 512 * hb + 128 * k4:512 * hb + 128 * k4 + 128],
                                  ybf[sl][:, 128 * dc:128 * dc + 128], idb[:, :],
                                  start=True, stop=True, skip_group_check=True),
                         sT if k4 == 3 else None, 1)
                tmark = sT.n
                dst = y_fm[:, :].rearrange("p (dc t) -> p dc t", t=NQ)[
                    :, hb * 4:hb * 4 + 4, 128 * qt:128 * qt + 128]
                src_ = ptr4[:, 512 * hb:512 * hb + 512].rearrange("p (a b) -> p a b", b=128)
                p.wait("v", sT, tmark)
                tr4ev[hb] = p.op("v", lambda v, d=dst, s=src_: v.tensor_copy(d, s), sV, 1)
            ytr[sl] = sT.n
        p.wait("g", dG, dG.n)
        p.wait("y", dS, dS.n)
        p.run(nc)
    S34.close()

    with contextlib.ExitStack() as L:
        p = Ph("ffn")
        ffna = L.enter_context(nc.sbuf_tensor("ffna", [128, NQT * D], BF16))
        iosb = L.enter_context(nc.sbuf_tensor("iosb", [128, D], F32))
        big = L.enter_context(nc.sbuf_tensor("big", [128, D], F32))
        wgb = L.enter_context(nc.sbuf_tensor("wgb", [128, 8 * D], BF16))
        web = [L.enter_context(nc.sbuf_tensor(f"web{i}", [128, 8 * D], BF16)) for i in range(2)]
        lg = [L.enter_context(nc.sbuf_tensor(f"lg{i}", [128, D], F32)) for i in range(2)]
        esc = L.enter_context(nc.sbuf_tensor("esc", [128, D], BF16))
        mku = L.enter_context(nc.sbuf_tensor("mku", [128, D], U8))
        tsel = L.enter_context(nc.sbuf_tensor("tsel", [128, D], F32))
        cnd = L.enter_context(nc.sbuf_tensor("cnd", [128, 16], F32))
        he = [L.enter_context(nc.sbuf_tensor(f"he{i}", [128, D], BF16)) for i in range(2)]
        yre = [L.enter_context(nc.sbuf_tensor(f"yre{i}", [128, D], F32)) for i in range(2)]
        h2 = [L.enter_context(nc.sbuf_tensor(f"h2_{i}", [128, D], F32)) for i in range(2)]
        osb = [L.enter_context(nc.sbuf_tensor(f"osb{i}", [128, D], F32)) for i in range(2)]
        sq5 = L.enter_context(nc.sbuf_tensor("sq5", [128, D], BF16))
        sm5 = L.enter_context(nc.sbuf_tensor("sm5", [128, 8], F32))
        pg = L.enter_context(nc.psum_tensor("pg5", [128, 2048], F32))
        pe_ = L.enter_context(nc.psum_tensor("pe5", [128, 2048], F32))

        iota_ld = p.op("y", lambda y: y.dma_start(iosb[:, :], IOTA[:, :]), dS, 16)
        big_ms = p.op("g", lambda g: g.memset(big[:, :], 30000.0), sG, 1)
        wgsrc = WG[:, :].rearrange("(c p) d -> p c d", p=128)
        wgd3 = wgb[:, :].rearrange("p (c d) -> p c d", d=D)
        wgld = p.op("g", lambda g, s=wgsrc, d=wgd3: g.dma_start(d, s), dG, 16)
        weld = [0, 0]
        wesrc0 = WE[0, :, :].rearrange("(c p) d -> p c d", p=128)
        wed0 = web[0][:, :].rearrange("p (c d) -> p c d", d=D)
        weld[0] = p.op("g", lambda g, s=wesrc0, d=wed0: g.dma_start(d, s), dG, 16)

        gevac = [0, 0]
        prev_exp = [0]

        def emit_gate(qt):
            sl = qt % 2
            p.wait("t", dG, wgld)
            p.wait("t", sV, gevac[sl])
            for dc in range(8):
                for half in range(2):
                    p.op("t", lambda t, qt=qt, dc=dc, hf=half, sl=sl:
                         t.matmul(pg[:, 1024 * sl + 512 * hf:1024 * sl + 512 * hf + 512],
                                  y_fm[:, 2048 * dc + 128 * qt:2048 * dc + 128 * qt + 128],
                                  wgb[:, 1024 * dc + 512 * hf:1024 * dc + 512 * hf + 512],
                                  start=(dc == 0), stop=False, skip_group_check=True),
                         None)
            for half in range(2):
                p.op("t", lambda t, hf=half, sl=sl:
                     t.matmul(pg[:, 1024 * sl + 512 * hf:1024 * sl + 512 * hf + 512],
                              oneb[:, :], bgb[:, 512 * hf:512 * hf + 512],
                              start=False, stop=True, skip_group_check=True),
                     sT if half == 1 else None, 1)
            gmm = sT.n
            p.wait("v", sT, gmm)
            p.wait("v", sA, prev_exp[0])
            vv = p.op("v", lambda v, sl=sl:
                      v.reduce_max(sm5[:, 0:1], pg[:, 1024 * sl:1024 * sl + 1024],
                                   axis=AX.X, negate=True), sV, 1)
            p.wait("s", sV, vv)
            aa = p.op("s", lambda s, sl=sl:
                      s.activation(esc[:, :], pg[:, 1024 * sl:1024 * sl + 1024], AF.Exp,
                                   bias=sm5[:, 0:1], accum_out=sm5[:, 1:2]), sA, 1)
            prev_exp[0] = aa
            p.wait("v", sA, aa)
            vv = p.op("v", lambda v, qt=qt:
                      v.reciprocal(p_all[:, qt:qt + 1], sm5[:, 1:2]), sV, 1)
            p.wait("v", sV, vv)
            vv = p.op("v", lambda v, sl=sl:
                      v.tensor_scalar(mku[:, :], pg[:, 1024 * sl:1024 * sl + 1024],
                                      sm5[:, 0:1], 0.0, OP.add, OP.is_ge), sV, 1)
            gevac[sl] = vv
            p.wait("v", sV, vv)
            if qt == 0:
                p.wait("v", dS, iota_ld)
                p.wait("v", sG, big_ms)
            vv = p.op("v", lambda v: v.select(tsel[:, :], mku[:, :], iosb[:, :], big[:, :]),
                      sV, 1)
            p.wait("v", sV, vv)
            vv = p.op("v", lambda v:
                      v.tensor_reduce(cnd[:, 0:8],
                                      tsel[:, :].rearrange("p (g e) -> p e g", e=8),
                                      axis=AX.X, op=OP.min), sV, 1)
            p.wait("v", sV, vv)
            vv = p.op("v", lambda v: v.tensor_reduce(sm5[:, 2:3], cnd[:, 0:8], axis=AX.X,
                                                     op=OP.min), sV, 1)
            p.wait("v", sV, vv)
            vv = p.op("v", lambda v, qt=qt:
                      v.tensor_scalar(oh_all[:, 8 * qt:8 * qt + 8], cnd[:, 0:8], sm5[:, 2:3],
                                      None, OP.is_le), sV, 1)
            p.wait("v", sV, vv)

        yrel = [0, 0]
        h2m = [0, 0]
        outd = [0, 0]

        def emit_final(qt):
            sl = qt % 2
            sl = qt % 2
            p.wait("y", sV, h2m[sl])
            yrel[sl] = p.op("y", lambda y, qt=qt, sl=sl:
                            y.dma_start(yre[sl][:, :], Ys[128 * qt:128 * qt + 128, :]), dS, 16)
            p.wait("v", dS, yrel[sl])
            p.wait("v", sV, acc_mark[qt])
            p.wait("v", dS, outd[sl])
            vv = p.op("v", lambda v, qt=qt, sl=sl:
                      v.scalar_tensor_tensor(h2[sl][:, :], ffna[:, D * qt:D * qt + D],
                                             p_all[:, qt:qt + 1], yre[sl][:, :],
                                             OP.mult, OP.add), sV, 1)
            h2m[sl] = vv
            p.wait("s", sV, vv)
            sqa = p.op("s", lambda s, sl=sl:
                       s.activation(sq5[:, :], h2[sl][:, :], AF.Square,
                                    accum_out=sm5[:, 6:7]), sA, 1)
            p.wait("v", sV, vv)
            vv = p.op("v", lambda v, sl=sl:
                      v.reduce_sum(sm5[:, 4:5], h2[sl][:, :], axis=AX.X, negate=True), sV, 1)
            p.wait("v", sV, vv)
            vv = p.op("v", lambda v: v.tensor_scalar(sm5[:, 5:6], sm5[:, 4:5], RD, None,
                                                     OP.mult), sV, 1)
            p.wait("v", sA, sqa)
            p.wait("v", sV, vv)
            vv = p.op("v", lambda v: v.tensor_scalar(sm5[:, 7:8], sm5[:, 6:7], RD, None,
                                                     OP.mult), sV, 1)
            p.wait("v", sV, vv)
            vv = p.op("v", lambda v: v.tensor_mul(sm5[:, 4:5], sm5[:, 5:6], sm5[:, 5:6]), sV, 1)
            p.wait("v", sV, vv)
            vv = p.op("v", lambda v: v.tensor_sub(sm5[:, 7:8], sm5[:, 7:8], sm5[:, 4:5]), sV, 1)
            p.wait("s", sV, vv)
            aa = p.op("s", lambda s: s.activation(sm5[:, 3:4], sm5[:, 7:8], AF.Ln,
                                                  bias=np.float32(LNEPS)), sA, 1)
            p.wait("s", sA, aa)
            aa = p.op("s", lambda s: s.activation(sm5[:, 2:3], sm5[:, 3:4], AF.Exp,
                                                  scale=np.float32(-0.5)), sA, 1)
            p.wait("v", sA, aa)
            vv = p.op("v", lambda v, sl=sl:
                      v.tensor_scalar(osb[sl][:, :], h2[sl][:, :], sm5[:, 5:6], sm5[:, 2:3],
                                      OP.add, OP.mult), sV, 1)
            p.wait("y", sV, vv)
            outd[sl] = p.op("y", lambda y, qt=qt, sl=sl:
                            y.dma_start(OUT[128 * qt:128 * qt + 128, :], osb[sl][:, :]), dS, 16)

        relu_done = [0, 0]
        stt_done = [0, 0]
        acc_mark = {}
        welast = [0, 0]
        for ei in range(8):
            parw = ei % 2
            if ei < 7:
                nxt = (ei + 1) % 2
                p.wait("g", sT, welast[nxt])
                wes = WE[ei + 1, :, :].rearrange("(c p) d -> p c d", p=128)
                wedn = web[nxt][:, :].rearrange("p (c d) -> p c d", d=D)
                weld[nxt] = p.op("g", lambda g, s=wes, d=wedn: g.dma_start(d, s), dG, 16)
            for qt in range(NQT):
                if ei == 0:
                    emit_gate(qt)
                u = ei * NQT + qt
                sl = u % 2
                p.wait("t", dG, weld[parw])
                p.wait("t", sA, relu_done[sl])
                for dc in range(8):
                    for half in range(2):
                        p.op("t", lambda t, qt=qt, dc=dc, hf=half, sl=sl, pw=parw:
                             t.matmul(pe_[:, 1024 * sl + 512 * hf:1024 * sl + 512 * hf + 512],
                                      y_fm[:, 2048 * dc + 128 * qt:2048 * dc + 128 * qt + 128],
                                      web[pw][:, 1024 * dc + 512 * hf:1024 * dc + 512 * hf + 512],
                                      start=(dc == 0), stop=False, skip_group_check=True),
                             None)
                for half in range(2):
                    p.op("t", lambda t, hf=half, sl=sl, ei=ei:
                         t.matmul(pe_[:, 1024 * sl + 512 * hf:1024 * sl + 512 * hf + 512],
                                  oneb[:, :], beb[:, 1024 * ei + 512 * hf:1024 * ei + 512 * hf + 512],
                                  start=False, stop=True, skip_group_check=True),
                         sT if half == 1 else None, 1)
                emm = sT.n
                welast[parw] = emm
                p.wait("s", sT, emm)
                p.wait("s", sV, stt_done[sl])
                relu_done[sl] = p.op("s", lambda s, sl=sl:
                                     s.activation(he[sl][:, :], pe_[:, 1024 * sl:1024 * sl + 1024],
                                                  AF.Relu), sA, 1)
                p.wait("v", sA, relu_done[sl])
                p.wait("v", sV, acc_mark.get(qt, 0))
                dstf = ffna[:, D * qt:D * qt + D]
                ohap = oh_all[:, 8 * qt + ei:8 * qt + ei + 1]
                if ei == 0:
                    vv = p.op("v", lambda v, d=dstf, sl=sl, o=ohap:
                              v.tensor_scalar(d, he[sl][:, :], o, None, OP.mult), sV, 1)
                else:
                    vv = p.op("v", lambda v, d=dstf, sl=sl, o=ohap:
                              v.scalar_tensor_tensor(d, he[sl][:, :], o, d, OP.mult, OP.add),
                              sV, 1)
                stt_done[sl] = vv
                acc_mark[qt] = vv
                if ei == 7:
                    emit_final(qt)

        for tn, tap in taps.items():
            p.op("g", lambda g, d=tap, s=scr[tn]: g.dma_start(d, s), dG, 16)
        p.wait("g", dG, dG.n)
        p.wait("y", dS, dS.n)
        p.run(nc)
    S45.close()
    top.close()
    return nc


def _host_inputs(inputs):
    x = np.ascontiguousarray(np.asarray(inputs["L0_attn_x"]), dtype=np.float32)
    cores = []
    tri = np.zeros((128, 128), dtype=bfnp)
    tri[np.arange(128)[:, None] > np.arange(128)[None, :]] = 1.0
    e16 = np.zeros((8, 128, 16), dtype=bfnp)
    for c in range(8):
        for pp in range(128):
            e16[c, pp, 2 * c + pp // 64] = 1.0
    ebc = np.transpose(e16, (0, 2, 1)).copy()
    f32a = lambda a: np.ascontiguousarray(np.asarray(a), dtype=np.float32)
    consts = {
        "wq": f32a(inputs["L0_attn_Wq"]), "wk": f32a(inputs["L0_attn_Wk"]),
        "wv": f32a(inputs["L0_attn_Wv"]), "wo": f32a(inputs["L0_attn_Wo"]),
        "wg": f32a(inputs["L0_ffn_Wg"]), "we": f32a(inputs["L0_ffn_We"]),
        "bg": f32a(inputs["L0_ffn_bg"]).reshape(1, D),
        "be": f32a(inputs["L0_ffn_be"]).reshape(1, 8 * D),
        "qscale": f32a(inputs["L0_attn_q_scale"]).reshape(H, 1),
        "kscale": f32a(inputs["L0_attn_k_scale"]).reshape(H, 1),
        "identb": np.eye(128, dtype=bfnp),
        "identf": np.eye(128, dtype=np.float32),
        "e16": e16, "ebc": ebc,
        "iota": np.broadcast_to(np.arange(D, dtype=np.float32), (128, D)).copy(),
        "negbigI": (np.eye(128) * -1.0e34).astype(bfnp),
        "onesb": np.ones((1, 128), dtype=bfnp),
    }
    for core in range(8):
        b, jc = core // 4, core % 4
        qtiles = [4 * s + (jc + s) % 4 for s in range(NQT)]
        xq = np.concatenate([x[b, 128 * t:128 * t + 128, :] for t in qtiles], axis=0)
        maskq = np.zeros((NKT, 128, 128), dtype=bfnp)
        for s in range(NQT):
            sig = (jc + s) % 4
            for r in range(4):
                if r == sig:
                    maskq[4 * s + r] = tri
                elif r > sig:
                    maskq[4 * s + r] = 1.0
        m = dict(consts)
        m["xkv"] = np.ascontiguousarray(x[b])
        m["xq"] = np.ascontiguousarray(xq)
        m["maskq"] = maskq
        cores.append(m)
    return cores


def kernel(**inputs):
    nc = build_nc()
    in_maps = _host_inputs(inputs)
    res = run_bass_kernel_spmd(nc, in_maps, core_ids=list(range(8)))
    if DEBUG_TAPS:
        global LAST_TAPS
        LAST_TAPS = [{k: v for k, v in r.items() if k.startswith("tap_")}
                     for r in res.results]
    out = np.empty((2, NB, D), dtype=np.float32)
    for core in range(8):
        b, jc = core // 4, core % 4
        o = res.results[core]["out"]
        for s in range(NQT):
            t = 4 * s + (jc + s) % 4
            out[b, 128 * t:128 * t + 128, :] = o[128 * s:128 * s + 128, :]
    return out


# version 21
# speedup vs baseline: 1.2896x; 1.0099x over previous
import contextlib
import numpy as np
import ml_dtypes
import concourse.bass as bass
import concourse.mybir as mybir
from concourse.bass_utils import run_bass_kernel_spmd
from concourse.alu_op_type import AluOpType as OP

F32 = mybir.dt.float32
BF16 = mybir.dt.bfloat16
U8 = mybir.dt.uint8
AF = mybir.ActivationFunctionType
AX = mybir.AxisListType
bfnp = ml_dtypes.bfloat16

H, HD, D = 16, 64, 1024
NB, NKT, NG = 8192, 64, 16
NQ, NQT, NQG = 2048, 16, 4
CS = [4 * s + 4 for s in range(NQT)]
TSHIFT = 60.0
EPS = 1e-6
LNEPS = 1e-5
RD = 1.0 / D

DEBUG_TAPS = []


class Sem:
    def __init__(self, h):
        self.h = h
        self.n = 0


class Ph:
    def __init__(self, name):
        self.name = name
        self.q = {"t": [], "s": [], "v": [], "y": [], "g": []}

    def wait(self, ek, sem, val):
        if val > 0:
            self.q[ek].append(lambda e, s=sem.h, v=val: e.wait_ge(s, v))

    def op(self, ek, bld, sem=None, k=1):
        if sem is None:
            self.q[ek].append(lambda e, b=bld: b(e))
            return None
        sem.n += k
        v = sem.n
        self.q[ek].append(lambda e, b=bld, s=sem.h, kk=k: b(e).then_inc(s, kk))
        return v

    def run(self, nc):
        with nc.Block(self.name) as blk:
            for attr, key in (("sync", "y"), ("gpsimd", "g"), ("tensor", "t"),
                              ("scalar", "s"), ("vector", "v")):
                fns = self.q[key]
                if fns:
                    getattr(blk, attr)(lambda e, fns=fns: [f(e) for f in fns])


def build_nc():
    nc = bass.Bass("TRN2", target_bir_lowering=False)

    def din(name, shape, dt):
        return nc.dram_tensor(name, list(shape), dt, kind="ExternalInput").ap()

    XKV = din("xkv", [NB, D], F32)
    XQ = din("xq", [NQ, D], F32)
    WQ = din("wq", [D, D], F32)
    WK = din("wk", [D, D], F32)
    WV = din("wv", [D, D], F32)
    WO = din("wo", [D, D], F32)
    WG = din("wg", [D, D], F32)
    WE = din("we", [8, D, D], F32)
    BG = din("bg", [1, D], F32)
    BE = din("be", [1, 8 * D], F32)
    QS = din("qscale", [H, 1], F32)
    KSc = din("kscale", [H, 1], F32)
    IDB = din("identb", [128, 128], BF16)
    IDF = din("identf", [128, 128], F32)
    E16 = din("e16", [8, 128, 16], BF16)
    EBC = din("ebc", [8, 16, 128], BF16)
    IOTA = din("iota", [128, D], F32)
    NBI = din("negbigI", [128, 128], BF16)
    ONE = din("onesb", [1, 128], BF16)
    MQ = din("maskq", [NKT, 128, 128], BF16)

    OUT = nc.dram_tensor("out", [NQ, D], F32, kind="ExternalOutput").ap()

    Ks = nc.dram_tensor("Ks", [H, HD, NB], BF16).ap()
    Vs = nc.dram_tensor("Vs", [NKT, 128, H * 65], BF16).ap()
    Qs = nc.dram_tensor("Qs", [H, HD, NQ], BF16).ap()
    Ys = nc.dram_tensor("Ys", [NQ, D], F32).ap()
    scr = {"Ks": Ks, "Vs": Vs, "Qs": Qs, "Ys": Ys, "attn": Ys}
    taps = {}
    for tn in DEBUG_TAPS:
        if tn == "attn":
            continue
        dt = F32 if tn == "Ys" else BF16
        taps[tn] = nc.dram_tensor("tap_" + tn, list(scr[tn].shape), dt,
                                  kind="ExternalOutput").ap()
    tap_attn = tap_lall = None
    if "attn" in DEBUG_TAPS:
        taps.pop("attn", None)
        tap_attn = nc.dram_tensor("tap_attn", [128, 8 * NQ], BF16,
                                  kind="ExternalOutput").ap()
        tap_lall = nc.dram_tensor("tap_lall", [H, NQ], F32,
                                  kind="ExternalOutput").ap()

    top = contextlib.ExitStack()
    E = top.enter_context

    dS = Sem(E(nc.semaphore("dS")))
    dG = Sem(E(nc.semaphore("dG")))
    sT = Sem(E(nc.semaphore("sT")))
    sA = Sem(E(nc.semaphore("sA")))
    sV = Sem(E(nc.semaphore("sV")))
    sG = Sem(E(nc.semaphore("sG")))

    idb = E(nc.sbuf_tensor("idb", [128, 128], BF16))
    idf = E(nc.sbuf_tensor("idf", [128, 128], F32))
    e16 = E(nc.sbuf_tensor("e16_sb", [128, 8 * 16], BF16))
    ebc = E(nc.sbuf_tensor("ebc_sb", [16, 8 * 128], BF16))
    nbi = E(nc.sbuf_tensor("nbi", [128, 128], BF16))
    oneb = E(nc.sbuf_tensor("oneb", [1, 128], BF16))
    qs_c = E(nc.sbuf_tensor("qs_c", [H, 1], F32))
    ks_c = E(nc.sbuf_tensor("ks_c", [H, 1], F32))
    bgb = E(nc.sbuf_tensor("bgb", [1, D], BF16))
    beb = E(nc.sbuf_tensor("beb", [1, 8 * D], BF16))

    S23 = contextlib.ExitStack()
    nrB = S23.enter_context(nc.sbuf_tensor("nrB", [H, NB], BF16))
    bcols = S23.enter_context(nc.sbuf_tensor("bcols", [128, H * NKT], F32))
    r2k = S23.enter_context(nc.sbuf_tensor("r2k", [H, NB], BF16))
    c_own = S23.enter_context(nc.sbuf_tensor("c_own", [H, NQ], BF16))
    a_own = S23.enter_context(nc.sbuf_tensor("a_own", [H, NQ], BF16))
    r2q = S23.enter_context(nc.sbuf_tensor("r2q", [H, NQ], BF16))
    cc = S23.enter_context(nc.sbuf_tensor("ccs", [16, 8], F32))
    SW = contextlib.ExitStack()
    wT_q = SW.enter_context(nc.sbuf_tensor("wT_q", [128, 8 * D], BF16))
    wT_k = SW.enter_context(nc.sbuf_tensor("wT_k", [128, 8 * D], BF16))
    wT_v = SW.enter_context(nc.sbuf_tensor("wT_v", [128, 8 * D], BF16))
    S34 = contextlib.ExitStack()
    S45 = contextlib.ExitStack()

    with contextlib.ExitStack() as L:
        p = Ph("prep")
        wst = [L.enter_context(nc.sbuf_tensor(f"wst{i}", [128, D], F32)) for i in range(2)]
        wca = [L.enter_context(nc.sbuf_tensor(f"wca{i}", [128, D], BF16)) for i in range(2)]
        pstr = L.enter_context(nc.psum_tensor("pstr0", [128, 1024], F32))

        for sb, src in [(idb, IDB), (idf, IDF), (nbi, NBI), (oneb, ONE),
                        (qs_c, QS), (ks_c, KSc)]:
            p.op("y", lambda y, a=sb, b=src: y.dma_start(a[:, :], b[:, :]), dS, 16)
        constmark = dS.n
        p.op("y", lambda y: y.dma_start(e16[:, :], E16[:, :, :].rearrange("c p h -> p c h")), dS, 16)
        p.op("y", lambda y: y.dma_start(ebc[:, :], EBC[:, :, :].rearrange("c h p -> h c p")), dS, 16)
        p.op("g", lambda g: g.dma_start(bgb[:, :], BG[:, :]), dG, 16)
        p.op("g", lambda g: g.dma_start(beb[:, :], BE[:, :]), dG, 16)

        ldmark = [0, 0]
        castmark = [0, 0]
        trmark = [0, 0]
        evmark = [0, 0]
        it = 0
        for Wsrc, wT in [(WQ, wT_q), (WK, wT_k), (WV, wT_v)]:
            for fc in range(8):
                par = it % 2
                p.wait("y", sG, castmark[par])
                ldmark[par] = p.op("y", lambda y, w=Wsrc, fc=fc, pr=par:
                                   y.dma_start(wst[pr][:, :], w[128 * fc:128 * fc + 128, :]),
                                   dS, 16)
                p.wait("g", dS, ldmark[par])
                p.wait("g", sT, trmark[par])
                castmark[par] = p.op("g", lambda g, pr=par:
                                     g.tensor_copy(wca[pr][:, :], wst[pr][:, :]), sG, 1)
                p.wait("t", sG, castmark[par])
                for half in range(2):
                    p.wait("t", sV, evmark[half])
                    for k4 in range(4):
                        dc = half * 4 + k4
                        p.op("t", lambda t, pr=par, dc=dc, k4=k4, hh=half:
                             t.matmul(pstr[:, 512 * hh + 128 * k4:512 * hh + 128 * k4 + 128],
                                      wca[pr][:, 128 * dc:128 * dc + 128], idb[:, :],
                                      start=True, stop=True, skip_group_check=True),
                             sT if k4 == 3 else None, 1)
                    tmark = sT.n
                    dst = wT[:, :].rearrange("p (dc f) -> p dc f", f=D)[
                        :, half * 4:half * 4 + 4, 128 * fc:128 * fc + 128]
                    src_ = pstr[:, 512 * half:512 * half + 512].rearrange(
                        "p (a b) -> p a b", b=128)
                    p.wait("v", sT, tmark)
                    evmark[half] = p.op("v", lambda v, d=dst, s=src_: v.tensor_copy(d, s), sV, 1)
                trmark[par] = sT.n
                it += 1
        p.wait("y", dS, dS.n)
        p.wait("g", dG, dG.n)
        p.run(nc)

    with contextlib.ExitStack() as L:
        p = Ph("qkv")
        xb = [L.enter_context(nc.sbuf_tensor(f"xb{i}", [128, 4 * D], BF16)) for i in range(2)]
        xfm = [L.enter_context(nc.sbuf_tensor(f"xfm{i}", [128, 4 * D], BF16)) for i in range(2)]
        kf = [L.enter_context(nc.sbuf_tensor(f"kf{i}", [128, 1024], BF16)) for i in range(2)]
        ksq = [L.enter_context(nc.sbuf_tensor(f"ksq{i}", [128, 1024], BF16)) for i in range(2)]
        vx = [L.enter_context(nc.sbuf_tensor(f"vx{i}", [128, H * 65], BF16)) for i in range(2)]
        r2c = [L.enter_context(nc.sbuf_tensor(f"r2c{i}", [16, 512], F32)) for i in range(2)]
        ptr = L.enter_context(nc.psum_tensor("ptr1", [128, 1024], F32))
        pkq = L.enter_context(nc.psum_tensor("pkq1", [128, 2048], F32))
        pr2 = L.enter_context(nc.psum_tensor("pr2", [16, 512], F32))
        pbc = L.enter_context(nc.psum_tensor("pbc", [128, 32], F32))

        for i in range(2):
            p.op("g", lambda g, i=i: g.memset(vx[i][:, :], 1.0), sG, 1)
        p.op("g", lambda g: g.memset(cc[:, 2:3], 0.0), sG, 1)
        msmark = sG.n

        def cv1(bld):
            v = p.op("v", bld, sV, 1)
            p.wait("v", sV, v)
            return v

        def cs1(bld):
            v = p.op("s", bld, sA, 1)
            p.wait("s", sA, v)
            return v

        p.wait("v", dS, constmark)
        cv1(lambda v: v.tensor_scalar(cc[:, 0:1], ks_c[:, :], -1.0, None, OP.mult))
        cv1(lambda v: v.tensor_max(cc[:, 0:1], cc[:, 0:1], ks_c[:, :]))

        chain_s = [0, 0]
        chain_v = [0, 0]
        bmark = [0, 0]
        btr_t = [0, 0]
        bevac = [0, 0]

        def emit_kchain(par, gidx):
            p.wait("v", sT, sqmm[1])
            p.wait("v", sT, btr_t[par])
            cpm = p.op("v", lambda v, pr=par: v.tensor_copy(r2c[pr][:, :], pr2[:, :]), sV, 1)
            r2evac[0] = cpm
            bmark[par] = cpm
            p.wait("v", sV, cpm)
            cv1(lambda v, pr=par, g2=gidx:
                v.tensor_copy(r2k[:, 512 * g2:512 * g2 + 512], r2c[pr][:, :]))
            cv1(lambda v, pr=par: v.reduce_max(cc[:, 3:4], r2c[pr][:, :], axis=AX.X))
            if gidx == 0:
                p.wait("v", sG, msmark)
            cv1(lambda v: v.tensor_max(cc[:, 2:3], cc[:, 2:3], cc[:, 3:4]))

        def emit_ktail(par, gidx):
            for i in range(4):
                ch = 4 * gidx + i
                sl2b = i % 2
                p.wait("t", sV, bevac[sl2b])
                if i == 0:
                    p.wait("t", sV, bmark[par])
                p.op("t", lambda t, pr=par, i=i, sl2b=sl2b:
                     t.transpose(pbc[:, 16 * sl2b:16 * sl2b + 16],
                                 r2c[pr][0:16, 128 * i:128 * i + 128],
                                 idf[0:16, 0:16]), sT, 1)
                tm = sT.n
                p.wait("v", sT, tm)
                dstb = bcols[:, :].rearrange("p (h k) -> p h k", k=NKT)[:, :, ch:ch + 1]
                bevac[sl2b] = p.op("v", lambda v, d=dstb, sl2b=sl2b:
                                   v.tensor_copy(d, pbc[:, 16 * sl2b:16 * sl2b + 16]), sV, 1)
            btr_t[par] = sT.n

        trlast = [0, 0]
        gend = [0, 0]
        xld = [0, 0]
        evdone = [0, 0]
        kfdma = [0, 0]
        sqmm = [0, 0]
        vxdma = [0, 0]
        trev = [0, 0]
        r2evac = [0]

        gi = 0
        for is_q, n_groups in ((False, NG), (True, NQG)):
            for gidx in range(n_groups):
                par = gi % 2
                Xsrc = XQ if is_q else XKV
                wTx = wT_q if is_q else wT_k
                Dst = Qs if is_q else Ks
                prx = pr2

                p.wait("g", sT, trlast[par])
                src = Xsrc[512 * gidx:512 * gidx + 512, :].rearrange("(j p) d -> p j d", p=128)
                xdst = xb[par][:, :].rearrange("p (j d) -> p j d", d=D)
                xld[par] = p.op("g", lambda g, s=src, d=xdst: g.dma_start(d, s), dG, 16)

                p.wait("t", dG, xld[par])
                first_ev = True
                for dc in range(8):
                    hb = dc % 2
                    p.wait("t", sV, trev[hb])
                    for j in range(4):
                        p.op("t", lambda t, pr=par, dc=dc, j=j, hb=hb:
                             t.matmul(ptr[:, 512 * hb + 128 * j:512 * hb + 128 * j + 128],
                                      xb[pr][:, 1024 * j + 128 * dc:1024 * j + 128 * dc + 128],
                                      idb[:, :], start=True, stop=True, skip_group_check=True),
                             sT if j == 3 else None, 1)
                    tmark = sT.n
                    p.wait("v", sT, tmark)
                    if first_ev:
                        p.wait("v", sT, gend[par])
                        first_ev = False
                    trev[hb] = p.op("v", lambda v, pr=par, dc=dc, hb=hb:
                                    v.tensor_copy(xfm[pr][:, 512 * dc:512 * dc + 512],
                                                  ptr[:, 512 * hb:512 * hb + 512]), sV, 1)
                trlast[par] = sT.n

                pend_e16 = []
                for fp2 in range(4):
                    sl2 = fp2 % 2
                    p.wait("t", sA, evdone[sl2])
                    if fp2 == 0:
                        p.wait("t", sV, trev[0])
                        p.wait("t", sV, trev[1])
                        p.wait("t", sV, r2evac[0])
                    for dc in range(8):
                        for hb in range(2):
                            fc = 2 * fp2 + hb
                            p.op("t", lambda t, pr=par, fc=fc, dc=dc, hb=hb, sl2=sl2, w=wTx:
                                 t.matmul(pkq[:, 1024 * sl2 + 512 * hb:1024 * sl2 + 512 * hb + 512],
                                          w[:, 1024 * dc + 128 * fc:1024 * dc + 128 * fc + 128],
                                          xfm[pr][:, 512 * dc:512 * dc + 512],
                                          start=(dc == 0), stop=(dc == 7),
                                          skip_group_check=True),
                                 sT if (dc == 7 and hb == 1) else None, 1)
                    mmk = sT.n
                    for (fcp, slp, sqd) in pend_e16:
                        p.wait("t", sA, sqd)
                        for hb2 in range(2):
                            sqmm[slp] = p.op(
                                "t", lambda t, p2=slp, fc=fcp + hb2, hb2=hb2, pr_=prx:
                                t.matmul(pr_[:, :], e16[:, 16 * fc:16 * fc + 16],
                                         ksq[p2][:, 512 * hb2:512 * hb2 + 512],
                                         start=(fc == 0), stop=(fc == 7),
                                         skip_group_check=True), sT, 1)
                    pend_e16 = []
                    p.wait("s", sT, mmk)
                    p.wait("s", dS, kfdma[sl2])
                    kevac_ = p.op("s", lambda s, p2=sl2, sl2=sl2:
                                  s.activation(kf[p2][:, :],
                                               pkq[:, 1024 * sl2:1024 * sl2 + 1024],
                                               AF.Silu), sA, 1)
                    evdone[sl2] = kevac_
                    p.wait("s", sT, sqmm[sl2])
                    sqdone = p.op("s", lambda s, p2=sl2:
                                  s.activation(ksq[p2][:, :], kf[p2][:, :], AF.Square),
                                  sA, 1)
                    pend_e16.append((2 * fp2, sl2, sqdone))
                    p.wait("y", sA, kevac_)
                    for hb in range(2):
                        fc = 2 * fp2 + hb
                        d0 = Dst[2 * fc, :, 512 * gidx:512 * gidx + 512]
                        d1 = Dst[2 * fc + 1, :, 512 * gidx:512 * gidx + 512]
                        p.op("y", lambda g, p2=sl2, d=d0, hb=hb:
                             g.dma_start(d, kf[p2][0:64, 512 * hb:512 * hb + 512]), dS, 16)
                        kfdma[sl2] = p.op(
                            "y", lambda g, p2=sl2, d=d1, hb=hb:
                            g.dma_start(d, kf[p2][64:128, 512 * hb:512 * hb + 512]), dS, 16)

                flush_e16 = pend_e16
                pend_e16 = []

                if not is_q:
                    for j in range(4):
                        pv_ = j % 2
                        sl2 = j % 2
                        p.wait("t", sA, evdone[sl2])
                        p.wait("t", dS, vxdma[pv_])
                        for dc in range(8):
                            for half in range(2):
                                p.op("t", lambda t, pr=par, j=j, dc=dc, hf=half, sl2=sl2:
                                     t.matmul(pkq[:, 1024 * sl2 + 512 * hf:1024 * sl2 + 512 * hf + 512],
                                              xfm[pr][:, 512 * dc + 128 * j:512 * dc + 128 * j + 128],
                                              wT_v[:, 1024 * dc + 512 * hf:1024 * dc + 512 * hf + 512],
                                              start=(dc == 0), stop=(dc == 7),
                                              skip_group_check=True),
                                     sT if (dc == 7 and half == 1) else None, 1)
                        vmm = sT.n
                        if flush_e16:
                            for (fcp, slp, sqd) in flush_e16:
                                p.wait("t", sA, sqd)
                                for hb2 in range(2):
                                    sqmm[slp] = p.op(
                                        "t", lambda t, p2=slp, fc=fcp + hb2, hb2=hb2, pr_=prx:
                                        t.matmul(pr_[:, :], e16[:, 16 * fc:16 * fc + 16],
                                                 ksq[p2][:, 512 * hb2:512 * hb2 + 512],
                                                 start=(fc == 0), stop=(fc == 7),
                                                 skip_group_check=True), sT, 1)
                            flush_e16 = []
                            emit_kchain(par, gidx)
                        p.wait("s", sT, vmm)
                        p.wait("s", dS, vxdma[pv_])
                        dr = vx[pv_][:, :].rearrange("p (h j2) -> p h j2", j2=65)
                        evdone[sl2] = p.op("s", lambda s, pv=pv_, dr=dr, sl2=sl2:
                                           s.activation(dr[:, :, 0:64],
                                                        pkq[:, 1024 * sl2:1024 * sl2 + 1024]
                                                        .rearrange("p (h f) -> p h f", f=64),
                                                        AF.Silu), sA, 1)
                        kt = 4 * gidx + j
                        p.wait("y", sA, evdone[sl2])
                        vxdma[pv_] = p.op("y", lambda g, pv=pv_, kt=kt:
                                          g.dma_start(Vs[kt, :, :], vx[pv][:, :]), dS, 16)
                    emit_ktail(par, gidx)
                else:
                    for (fcp, slp, sqd) in flush_e16:
                        p.wait("t", sA, sqd)
                        for hb2 in range(2):
                            sqmm[slp] = p.op(
                                "t", lambda t, p2=slp, fc=fcp + hb2, hb2=hb2, pr_=prx:
                                t.matmul(pr_[:, :], e16[:, 16 * fc:16 * fc + 16],
                                         ksq[p2][:, 512 * hb2:512 * hb2 + 512],
                                         start=(fc == 0), stop=(fc == 7),
                                         skip_group_check=True), sT, 1)
                    flush_e16 = []
                    p.wait("v", sT, sqmm[1])
                    r2dst = r2q[:, 512 * gidx:512 * gidx + 512]
                    r2evac[0] = p.op("v", lambda v, d=r2dst, pr_=prx:
                                     v.tensor_copy(d, pr_[:, :]), sV, 1)
                gend[par] = sT.n
                gi += 1
        p.wait("g", dG, dG.n)
        p.wait("y", dS, dS.n)
        p.run(nc)
    SW.close()

    attn = S34.enter_context(nc.sbuf_tensor("attn", [128, 8 * NQ], BF16, side="right"))
    l_all = S34.enter_context(nc.sbuf_tensor("l_all", [H, NQ], F32, side="right"))
    with contextlib.ExitStack() as L:
        p = Ph("attn")
        ksb = [L.enter_context(nc.sbuf_tensor(f"ksb{i}", [65, NB], BF16)) for i in range(2)]
        vsb = [L.enter_context(nc.sbuf_tensor(f"vsb{i}", [128, NKT * 65], BF16)) for i in range(2)]
        qsb = [L.enter_context(nc.sbuf_tensor(f"qsb{i}", [65, NQ // 2], BF16)) for i in range(2)]
        arow = [L.enter_context(nc.sbuf_tensor(f"ar{i}", [1, NQ // 2], BF16)) for i in range(2)]
        pts = [L.enter_context(nc.sbuf_tensor(f"pts{i}", [128, 1024], BF16)) for i in range(3)]
        lst = L.enter_context(nc.sbuf_tensor("lst", [1, NQ], F32))
        mq = L.enter_context(nc.sbuf_tensor("mq", [128, NKT * 128], BF16))
        kunion = L.enter_context(nc.sbuf_tensor("kunion", [128, NQ], F32))
        q1 = kunion[0:16, 0:NQ]
        ktmp = kunion[0:128, 0:1024]
        q2 = L.enter_context(nc.sbuf_tensor("cq2", [H, NQ], BF16))
        pst = L.enter_context(nc.psum_tensor("pst3", [128, 3072], F32))
        ppv = L.enter_context(nc.psum_tensor("ppv3", [65, 1024], F32))

        p.op("y", lambda y: y.dma_start(mq[:, :], MQ[:, :, :].rearrange("j p q -> p j q")),
             dS, 16)

        def cv(bld):
            v = p.op("v", bld, sV, 1)
            p.wait("v", sV, v)
            return v

        def ca(bld):
            v = p.op("s", bld, sA, 1)
            p.wait("s", sA, v)
            return v

        ca(lambda s: s.activation(ktmp, bcols[:, :], AF.Ln, bias=np.float32(1e-30)))
        m = p.op("s", lambda s: s.activation(bcols[:, :], ktmp, AF.Exp,
                                             scale=np.float32(-0.5)), sA, 1)
        p.wait("v", sA, m)
        m = cv(lambda v: v.tensor_scalar(bcols[:, :], bcols[:, :], EPS, 1.0,
                                         OP.mult, OP.add))
        p.wait("s", sV, m)
        ca(lambda s: s.activation(ktmp, bcols[:, :], AF.Ln))
        m = p.op("s", lambda s: s.activation(bcols[:, :], ktmp, AF.Exp,
                                             scale=np.float32(-1.0)), sA, 1)
        p.wait("v", sA, m)
        p.wait("s", sA, m)
        nrb_m = [0, 0, 0, 0]
        for chk in range(4):
            c0 = 2048 * chk
            ca(lambda s, c0=c0: s.activation(q1, r2k[:, c0:c0 + 2048], AF.Ln,
                                             bias=np.float32(1e-30)))
            if chk > 0:
                p.wait("s", sV, nrb_m[chk - 1])
            m = p.op("s", lambda s: s.activation(q2[:, :], q1, AF.Exp,
                                                 scale=np.float32(-0.5)), sA, 1)
            p.wait("v", sA, m)
            nrb_m[chk] = cv(lambda v, c0=c0:
                            v.tensor_scalar(nrB[:, c0:c0 + 2048], q2[:, :], -EPS, -1.0,
                                            OP.mult, OP.add))
        ca(lambda s: s.activation(cc[:, 3:4], cc[:, 2:3], AF.Ln, bias=np.float32(1e-30)))
        m = p.op("s", lambda s: s.activation(cc[:, 2:3], cc[:, 3:4], AF.Exp,
                                             scale=np.float32(0.5)), sA, 1)
        p.wait("v", sA, m)
        cv(lambda v: v.tensor_mul(cc[:, 5:6], cc[:, 0:1], cc[:, 2:3]))
        ca(lambda s: s.activation(q1, r2q[:, :], AF.Ln, bias=np.float32(1e-30)))
        m = p.op("s", lambda s: s.activation(r2q[:, :], q1, AF.Exp,
                                             scale=np.float32(0.5)), sA, 1)
        p.wait("s", sA, m)
        ca(lambda s: s.activation(q1, r2q[:, :], AF.Ln, bias=np.float32(EPS)))
        p.wait("s", sV, nrb_m[3])
        m = p.op("s", lambda s: s.activation(q2[:, :], q1, AF.Exp,
                                             scale=np.float32(-1.0)), sA, 1)
        p.wait("v", sA, m)
        cv(lambda v: v.scalar_tensor_tensor(q1, r2q[:, :], qs_c[:, :], q2[:, :],
                                            OP.mult, OP.mult))
        cv(lambda v: v.tensor_scalar(a_own[:, :], q1, ks_c[:, :], None, OP.mult))
        cv(lambda v: v.tensor_scalar(q2[:, :], q1, -1.0, None, OP.mult))
        cv(lambda v: v.tensor_max(q2[:, :], q2[:, :], q1))
        cv(lambda v: v.tensor_mul(q2[:, :], q2[:, :], r2q[:, :]))
        qchain_end = cv(lambda v: v.tensor_scalar(c_own[:, :], q2[:, :], cc[:, 5:6],
                                                  None, OP.mult))

        sweep_done = [0, 0]
        exp_done = [0, 0, 0]
        pv_done = [0, 0, 0]
        pv_evac = {}
        ldma_prev = 0
        lds_a = [0] * 33
        ldg_a = [0] * 33
        asv_a = [0] * 33

        items = [(P_, h) for P_ in range(2) for h in range(H)]

        def emit_loads(it2):
            P2, h2 = items[it2]
            par2 = it2 % 2
            J2 = 32 if P2 == 0 else NKT
            p.wait("y", sV, sweep_done[par2])
            p.op("y", lambda y, h=h2, pr=par2, J=J2:
                 y.dma_start(ksb[pr][0:64, 0:128 * J], Ks[h, :, 0:128 * J]), dS, 16)
            p.op("y", lambda y, h=h2, pr=par2, P_=P2:
                 y.dma_start(qsb[pr][0:64, :], Qs[h, :, 1024 * P_:1024 * P_ + 1024]),
                 dS, 16)
            vsrc = Vs[0:J2, :, 65 * h2:65 * h2 + 65].rearrange("k p j -> p k j")
            vdst3 = vsb[par2][:, 0:65 * J2].rearrange("p (k j) -> p k j", j=65)
            lds_a[it2] = p.op("y", lambda y, s=vsrc, d=vdst3: y.dma_start(d, s), dS, 16)
            p.wait("g", sV, sweep_done[par2])
            if it2 < 2:
                p.wait("g", sV, qchain_end)
            if it2 == 0:
                p.wait("g", sV, nrb_m[1])
            if it2 == 16:
                p.wait("g", sV, nrb_m[3])
            p.op("g", lambda g, h=h2, pr=par2, J=J2:
                 g.dma_start(ksb[pr][64:65, 0:128 * J], nrB[h:h + 1, 0:128 * J]), dG, 16)
            p.op("g", lambda g, h=h2, pr=par2, P_=P2:
                 g.dma_start(qsb[pr][64:65, :], c_own[h:h + 1, 1024 * P_:1024 * P_ + 1024]),
                 dG, 16)
            ldg_a[it2] = p.op("g", lambda g, h=h2, pr=par2, P_=P2:
                              g.dma_start(arow[pr][:, :],
                                          a_own[h:h + 1, 1024 * P_:1024 * P_ + 1024]), dG, 16)

        def emit_ascale(it2, bA):
            par2 = it2 % 2
            p.wait("t", dS, lds_a[it2])
            p.wait("t", dG, ldg_a[it2])
            p.wait("t", sA, exp_done[bA])
            asv = 0
            for i2 in range(2):
                p.op("t", lambda t, pr=par2, i2=i2, bA=bA:
                     t.matmul(pst[0:64, 1024 * bA + 512 * i2:1024 * bA + 512 * i2 + 512],
                              oneb[0:1, 0:64], arow[pr][:, 512 * i2:512 * i2 + 512],
                              start=True, stop=True, skip_group_check=True),
                     sT if i2 == 1 else None, 1)
            am = sT.n
            p.wait("v", sT, am)
            p.wait("v", dS, lds_a[it2])
            for i2 in range(2):
                asv = p.op("v", lambda v, pr=par2, i2=i2, bA=bA:
                           v.tensor_mul(qsb[pr][0:64, 512 * i2:512 * i2 + 512],
                                        qsb[pr][0:64, 512 * i2:512 * i2 + 512],
                                        pst[0:64, 1024 * bA + 512 * i2:1024 * bA + 512 * i2 + 512]),
                           sV, 1)
            asv_a[it2] = asv

        emit_loads(0)
        emit_loads(1)
        emit_ascale(0, 2)
        for it, (P_, h) in enumerate(items):
            par = it % 2
            JP = 32 if P_ == 0 else NKT
            p.wait("t", sV, asv_a[it])

            def emit_pv(j):
                bj = j % 3
                p.wait("t", sA, exp_done[bj])
                s0 = j // 4
                slo = max(s0, 8 * P_) - 8 * P_
                for bl in range(slo // 4, 2):
                    sa = max(slo, 4 * bl)
                    n = (4 * bl + 4 - sa) * 128
                    pcol = 128 * sa
                    first = (j == 0)
                    bg = 2 * P_ + bl
                    lastj = (j == CS[4 * bg + 3] - 1)
                    if first and it > 1:
                        p.wait("t", sV, pv_evac.get((it - 2, bl), 0))
                    p.op("t", lambda t, pr=par, j=j, n=n, pcol=pcol,
                         first=first, lastj=lastj, bj=bj:
                         t.matmul(ppv[:, pcol:pcol + n],
                                  vsb[pr][:, 65 * j:65 * j + 65],
                                  pts[bj][:, pcol:pcol + n],
                                  start=first, stop=lastj,
                                  skip_group_check=True), sT, 1)
                    pv_done[bj] = sT.n
                    if lastj:
                        pvm = sT.n
                        p.wait("v", sT, pvm)
                        sg = 8 * P_ + 4 * bl
                        bc = 512 * bl
                        dsta = attn[64 * (h % 2):64 * (h % 2) + 64,
                                    (h // 2) * NQ + 128 * sg:(h // 2) * NQ + 128 * sg + 512]
                        p.op("v", lambda v, d=dsta, bc=bc:
                             v.tensor_copy(d, ppv[0:64, bc:bc + 512]), sV, 1)
                        if bl == 0:
                            p.wait("v", dG, ldma_prev)
                        pv_evac[(it, bl)] = p.op(
                            "v", lambda v, bc=bc, sg=sg:
                            v.tensor_copy(lst[0:1, 128 * sg:128 * sg + 512],
                                          ppv[64:65, bc:bc + 512]), sV, 1)

            for j in range(JP):
                bj = j % 3
                s0 = j // 4
                slo = max(s0, 8 * P_) - 8 * P_
                p.wait("t", sA, exp_done[bj])
                s = slo
                while s < 8:
                    s2 = min((s // 4) * 4 + 3, 7)
                    n = (s2 - s + 1) * 128
                    tail = (s <= s0 - 8 * P_ <= s2) and (s0 >= 8 * P_)
                    col = 1024 * bj + 128 * s
                    is_last = (s2 == 7) and not tail
                    p.op("t", lambda t, pr=par, j=j, col=col, n=n, s=s:
                         t.matmul(pst[:, col:col + n],
                                  ksb[pr][0:65, 128 * j:128 * j + 128],
                                  qsb[pr][0:65, 128 * s:128 * s + n],
                                  start=True, stop=not tail,
                                  skip_group_check=True),
                         sT if is_last else None, 1)
                    if tail:
                        mcol = 1024 * bj + 128 * (s0 - 8 * P_)
                        is_last2 = (s2 == 7)
                        p.op("t", lambda t, j=j, mcol=mcol:
                             t.matmul(pst[:, mcol:mcol + 128], nbi[:, :],
                                      mq[:, 128 * j:128 * j + 128],
                                      start=False, stop=True,
                                      skip_group_check=True),
                             sT if is_last2 else None, 1)
                    s = s2 + 1
                stm = sT.n
                cL = 128 * slo
                nA = 1024 - cL
                p.wait("s", sT, stm)
                p.wait("s", sT, pv_done[bj])
                exp_done[bj] = p.op("s", lambda s_, h=h, j=j, cL=cL, nA=nA, bj=bj:
                                    s_.activation(pts[bj][:, cL:cL + nA],
                                                  pst[:, 1024 * bj + cL:1024 * bj + cL + nA],
                                                  AF.Exp, bias=np.float32(TSHIFT),
                                                  scale=bcols[:, NKT * h + j:NKT * h + j + 1]),
                                    sA, 1)
                if j == JP - 2 and it + 1 < len(items):
                    emit_ascale(it + 1, JP % 3)
                if j >= 2:
                    emit_pv(j - 2)
            emit_pv(JP - 2)
            emit_pv(JP - 1)
            p.wait("g", sV, pv_evac[(it, 1)])
            ldma_prev = p.op("g", lambda g, h=h, P_=P_:
                             g.dma_start(l_all[h:h + 1, 1024 * P_:1024 * P_ + 1024],
                                         lst[0:1, 1024 * P_:1024 * P_ + 1024]), dG, 16)
            sweep_done[par] = pv_evac[(it, 1)]
            if it + 2 < len(items):
                emit_loads(it + 2)
        p.wait("g", dG, dG.n)
        p.wait("y", dS, dS.n)
        p.run(nc)
    S23.close()

    y_fm = S45.enter_context(nc.sbuf_tensor("y_fm", [128, 8 * NQ], BF16))
    p_all = S45.enter_context(nc.sbuf_tensor("p_all", [128, NQT], F32))
    oh_all = S45.enter_context(nc.sbuf_tensor("oh_all", [128, NQT * 8], F32))
    with contextlib.ExitStack() as L:
        p = Ph("woln")
        wst = [L.enter_context(nc.sbuf_tensor(f"wst4{i}", [128, D], F32)) for i in range(2)]
        wca = [L.enter_context(nc.sbuf_tensor(f"wca4{i}", [128, D], BF16)) for i in range(2)]
        wT_o = L.enter_context(nc.sbuf_tensor("wT_o", [128, 8 * D], BF16))
        lrec = L.enter_context(nc.sbuf_tensor("lrec", [H, NQ], F32))
        lrecb = L.enter_context(nc.sbuf_tensor("lrecb", [H, NQ], BF16))
        xres = [L.enter_context(nc.sbuf_tensor(f"xres{i}", [128, D], F32)) for i in range(3)]
        y0 = [L.enter_context(nc.sbuf_tensor(f"y0_{i}", [128, D], F32)) for i in range(3)]
        y2 = [L.enter_context(nc.sbuf_tensor(f"y2_{i}", [128, D], F32)) for i in range(3)]
        ybf = [L.enter_context(nc.sbuf_tensor(f"ybf{i}", [128, D], BF16)) for i in range(3)]
        sqs = L.enter_context(nc.sbuf_tensor("sqs", [128, D], BF16))
        smc = L.enter_context(nc.sbuf_tensor("smc", [128, 24], F32))
        pwo = L.enter_context(nc.psum_tensor("pwo", [128, 3072], F32))
        ptr4 = L.enter_context(nc.psum_tensor("ptr4", [128, 1024], F32))

        if tap_attn is not None:
            p.op("g", lambda g: g.dma_start(tap_lall[:, :], l_all[:, :]), dG, 16)
        ldmark = [0, 0]
        castmark = [0, 0]
        trmark = [0, 0]
        evmark = [0, 0]
        for fc in range(8):
            par = fc % 2
            p.wait("y", sG, castmark[par])
            ldmark[par] = p.op("y", lambda y, fc=fc, pr=par:
                               y.dma_start(wst[pr][:, :], WO[128 * fc:128 * fc + 128, :]), dS, 16)
            p.wait("g", dS, ldmark[par])
            p.wait("g", sT, trmark[par])
            castmark[par] = p.op("g", lambda g, pr=par:
                                 g.tensor_copy(wca[pr][:, :], wst[pr][:, :]), sG, 1)
            p.wait("t", sG, castmark[par])
            for half in range(2):
                p.wait("t", sV, evmark[half])
                for k4 in range(4):
                    dc = half * 4 + k4
                    p.op("t", lambda t, pr=par, dc=dc, k4=k4, hh=half:
                         t.matmul(ptr4[:, 512 * hh + 128 * k4:512 * hh + 128 * k4 + 128],
                                  wca[pr][:, 128 * dc:128 * dc + 128], idb[:, :],
                                  start=True, stop=True, skip_group_check=True),
                         sT if k4 == 3 else None, 1)
                tmark = sT.n
                dst = wT_o[:, :].rearrange("p (dc f) -> p dc f", f=D)[
                    :, half * 4:half * 4 + 4, 128 * fc:128 * fc + 128]
                src_ = ptr4[:, 512 * half:512 * half + 512].rearrange("p (a b) -> p a b", b=128)
                p.wait("v", sT, tmark)
                evmark[half] = p.op("v", lambda v, d=dst, s=src_: v.tensor_copy(d, s), sV, 1)
            trmark[par] = sT.n

        la = p.op("v", lambda v: v.reciprocal(lrec[:, :], l_all[:, :]), sV, 1)
        p.wait("v", sV, la)
        lr = p.op("v", lambda v: v.tensor_copy(lrecb[:, :], lrec[:, :]), sV, 1)
        p.wait("v", sV, lr)
        p.wait("t", sV, lr)
        nmark = [evmark[0], evmark[1]]
        for u in range(32):
            ch, g2 = u // 4, u % 4
            sl_ = u % 2
            p.wait("t", sV, nmark[sl_])
            bm = p.op("t", lambda t, ch=ch, g2=g2, sl_=sl_:
                      t.matmul(ptr4[:, 512 * sl_:512 * sl_ + 512],
                               ebc[0:16, 128 * ch:128 * ch + 128],
                               lrecb[0:16, 512 * g2:512 * g2 + 512], start=True, stop=True,
                               skip_group_check=True), sT, 1)
            p.wait("v", sT, bm)
            dsta = attn[:, 2048 * ch + 512 * g2:2048 * ch + 512 * g2 + 512]
            nmark[sl_] = p.op("v", lambda v, d=dsta, sl_=sl_:
                              v.tensor_mul(d, d, ptr4[:, 512 * sl_:512 * sl_ + 512]), sV, 1)
        normdone = max(nmark)
        if tap_attn is not None:
            p.wait("g", sV, normdone)
            p.op("g", lambda g: g.dma_start(tap_attn[:, :], attn[:, :]), dG, 16)
        p.wait("t", sV, normdone)

        h1mark = [0, 0, 0]
        ymark = [0, 0, 0]
        ydma = [0, 0, 0]
        ybfm = [0, 0, 0]
        ytr = [0, 0, 0]
        xload = [0, 0, 0]
        tr4ev = [normdone, normdone]
        expm = [0] * NQT

        def wo_A(qt):
            sl = qt % 3
            b = 8 * sl
            if qt == 0:
                p.wait("t", sV, max(evmark))
            p.wait("t", sV, h1mark[sl])
            for dc in range(8):
                for half in range(2):
                    p.op("t", lambda t, qt=qt, dc=dc, hf=half, sl=sl:
                         t.matmul(pwo[:, 1024 * sl + 512 * hf:1024 * sl + 512 * hf + 512],
                                  attn[:, 2048 * dc + 128 * qt:2048 * dc + 128 * qt + 128],
                                  wT_o[:, 1024 * dc + 512 * hf:1024 * dc + 512 * hf + 512],
                                  start=(dc == 0), stop=(dc == 7), skip_group_check=True),
                         sT if (dc == 7 and half == 1) else None, 1)
            womm = sT.n
            p.wait("y", sV, h1mark[sl])
            xload[sl] = p.op("y", lambda y, qt=qt, sl=sl:
                             y.dma_start(xres[sl][:, :], XQ[128 * qt:128 * qt + 128, :]), dS, 16)
            p.wait("v", sT, womm)
            p.wait("v", dS, xload[sl])
            vv = p.op("v", lambda v, sl=sl:
                      v.tensor_add(y0[sl][:, :], xres[sl][:, :],
                                   pwo[:, 1024 * sl:1024 * sl + 1024]), sV, 1)
            h1mark[sl] = vv
            p.wait("s", sV, vv)
            sqa = p.op("s", lambda s, sl=sl, b=b:
                       s.activation(sqs[:, :], y0[sl][:, :], AF.Square,
                                    accum_out=smc[:, b + 2:b + 3]), sA, 1)
            p.wait("v", sV, vv)
            vv = p.op("v", lambda v, sl=sl, b=b:
                      v.reduce_sum(smc[:, b:b + 1], y0[sl][:, :], axis=AX.X, negate=True),
                      sV, 1)
            p.wait("v", sV, vv)
            vv = p.op("v", lambda v, b=b: v.tensor_scalar(smc[:, b + 1:b + 2],
                                                          smc[:, b:b + 1], RD, None,
                                                          OP.mult), sV, 1)
            p.wait("v", sA, sqa)
            p.wait("v", sV, vv)
            vv = p.op("v", lambda v, b=b: v.tensor_scalar(smc[:, b + 5:b + 6],
                                                          smc[:, b + 2:b + 3], RD, None,
                                                          OP.mult), sV, 1)
            p.wait("v", sV, vv)
            vv = p.op("v", lambda v, b=b: v.tensor_mul(smc[:, b + 6:b + 7],
                                                       smc[:, b + 1:b + 2],
                                                       smc[:, b + 1:b + 2]), sV, 1)
            p.wait("v", sV, vv)
            vv = p.op("v", lambda v, b=b: v.tensor_sub(smc[:, b + 5:b + 6],
                                                       smc[:, b + 5:b + 6],
                                                       smc[:, b + 6:b + 7]), sV, 1)
            p.wait("s", sV, vv)
            aa = p.op("s", lambda s, b=b: s.activation(smc[:, b + 3:b + 4],
                                                       smc[:, b + 5:b + 6], AF.Ln,
                                                       bias=np.float32(LNEPS)), sA, 1)
            p.wait("s", sA, aa)
            expm[qt] = p.op("s", lambda s, b=b: s.activation(smc[:, b + 4:b + 5],
                                                             smc[:, b + 3:b + 4], AF.Exp,
                                                             scale=np.float32(-0.5)), sA, 1)

        def wo_B(qt):
            sl = qt % 3
            b = 8 * sl
            p.wait("v", sA, expm[qt])
            p.wait("v", dG, ydma[sl])
            vv = p.op("v", lambda v, sl=sl, b=b:
                      v.tensor_scalar(y2[sl][:, :], y0[sl][:, :], smc[:, b + 1:b + 2],
                                      smc[:, b + 4:b + 5], OP.add, OP.mult), sV, 1)
            ymark[sl] = vv
            p.wait("g", sV, vv)
            ydma[sl] = p.op("g", lambda g, qt=qt, sl=sl:
                            g.dma_start(Ys[128 * qt:128 * qt + 128, :], y2[sl][:, :]), dG, 16)
            p.wait("v", sV, vv)
            p.wait("v", sT, ytr[sl])
            vv = p.op("v", lambda v, sl=sl: v.tensor_copy(ybf[sl][:, :], y2[sl][:, :]), sV, 1)
            ybfm[sl] = vv
            p.wait("t", sV, ybfm[sl])
            for hb in range(2):
                p.wait("t", sV, tr4ev[hb])
                for k4 in range(4):
                    dc = hb * 4 + k4
                    p.op("t", lambda t, sl=sl, dc=dc, k4=k4, hb=hb:
                         t.matmul(ptr4[:, 512 * hb + 128 * k4:512 * hb + 128 * k4 + 128],
                                  ybf[sl][:, 128 * dc:128 * dc + 128], idb[:, :],
                                  start=True, stop=True, skip_group_check=True),
                         sT if k4 == 3 else None, 1)
                tmark = sT.n
                dst = y_fm[:, :].rearrange("p (dc t) -> p dc t", t=NQ)[
                    :, hb * 4:hb * 4 + 4, 128 * qt:128 * qt + 128]
                src_ = ptr4[:, 512 * hb:512 * hb + 512].rearrange("p (a b) -> p a b", b=128)
                p.wait("v", sT, tmark)
                tr4ev[hb] = p.op("v", lambda v, d=dst, s=src_: v.tensor_copy(d, s), sV, 1)
            ytr[sl] = sT.n

        for qt in range(NQT):
            wo_A(qt)
            if qt > 0:
                wo_B(qt - 1)
        wo_B(NQT - 1)
        p.wait("g", dG, dG.n)
        p.wait("y", dS, dS.n)
        p.run(nc)
    S34.close()

    with contextlib.ExitStack() as L:
        p = Ph("ffn")
        ffna = L.enter_context(nc.sbuf_tensor("ffna", [128, NQT * D], BF16))
        iosb = L.enter_context(nc.sbuf_tensor("iosb", [128, D], F32))
        big = L.enter_context(nc.sbuf_tensor("big", [128, D], F32))
        wgb = L.enter_context(nc.sbuf_tensor("wgb", [128, 8 * D], BF16))
        web = [L.enter_context(nc.sbuf_tensor(f"web{i}", [128, 8 * D], BF16)) for i in range(2)]
        lg = [L.enter_context(nc.sbuf_tensor(f"lg{i}", [128, D], F32)) for i in range(2)]
        esc = L.enter_context(nc.sbuf_tensor("esc", [128, D], BF16))
        mku = L.enter_context(nc.sbuf_tensor("mku", [128, D], U8))
        tsel = L.enter_context(nc.sbuf_tensor("tsel", [128, D], F32))
        cnd = L.enter_context(nc.sbuf_tensor("cnd", [128, 16], F32))
        he = [L.enter_context(nc.sbuf_tensor(f"he{i}", [128, D], BF16)) for i in range(2)]
        yre = [L.enter_context(nc.sbuf_tensor(f"yre{i}", [128, D], F32)) for i in range(2)]
        h2 = [L.enter_context(nc.sbuf_tensor(f"h2_{i}", [128, D], F32)) for i in range(2)]
        osb = [L.enter_context(nc.sbuf_tensor(f"osb{i}", [128, D], F32)) for i in range(2)]
        sq5 = L.enter_context(nc.sbuf_tensor("sq5", [128, D], BF16))
        sm5 = L.enter_context(nc.sbuf_tensor("sm5", [128, 8], F32))
        pg = L.enter_context(nc.psum_tensor("pg5", [128, 2048], F32))
        pe_ = L.enter_context(nc.psum_tensor("pe5", [128, 2048], F32))

        iota_ld = p.op("y", lambda y: y.dma_start(iosb[:, :], IOTA[:, :]), dS, 16)
        big_ms = p.op("g", lambda g: g.memset(big[:, :], 30000.0), sG, 1)
        wgsrc = WG[:, :].rearrange("(c p) d -> p c d", p=128)
        wgd3 = wgb[:, :].rearrange("p (c d) -> p c d", d=D)
        wgld = p.op("g", lambda g, s=wgsrc, d=wgd3: g.dma_start(d, s), dG, 16)
        weld = [0, 0]
        wesrc0 = WE[0, :, :].rearrange("(c p) d -> p c d", p=128)
        wed0 = web[0][:, :].rearrange("p (c d) -> p c d", d=D)
        weld[0] = p.op("g", lambda g, s=wesrc0, d=wed0: g.dma_start(d, s), dG, 16)

        gevac = [0, 0]
        prev_exp = [0]

        def emit_gate(qt):
            sl = qt % 2
            p.wait("t", dG, wgld)
            p.wait("t", sV, gevac[sl])
            for dc in range(8):
                for half in range(2):
                    p.op("t", lambda t, qt=qt, dc=dc, hf=half, sl=sl:
                         t.matmul(pg[:, 1024 * sl + 512 * hf:1024 * sl + 512 * hf + 512],
                                  y_fm[:, 2048 * dc + 128 * qt:2048 * dc + 128 * qt + 128],
                                  wgb[:, 1024 * dc + 512 * hf:1024 * dc + 512 * hf + 512],
                                  start=(dc == 0), stop=False, skip_group_check=True),
                         None)
            for half in range(2):
                p.op("t", lambda t, hf=half, sl=sl:
                     t.matmul(pg[:, 1024 * sl + 512 * hf:1024 * sl + 512 * hf + 512],
                              oneb[:, :], bgb[:, 512 * hf:512 * hf + 512],
                              start=False, stop=True, skip_group_check=True),
                     sT if half == 1 else None, 1)
            gmm = sT.n
            p.wait("v", sT, gmm)
            p.wait("v", sA, prev_exp[0])
            vv = p.op("v", lambda v, sl=sl:
                      v.reduce_max(sm5[:, 0:1], pg[:, 1024 * sl:1024 * sl + 1024],
                                   axis=AX.X, negate=True), sV, 1)
            p.wait("s", sV, vv)
            aa = p.op("s", lambda s, sl=sl:
                      s.activation(esc[:, :], pg[:, 1024 * sl:1024 * sl + 1024], AF.Exp,
                                   bias=sm5[:, 0:1], accum_out=sm5[:, 1:2]), sA, 1)
            prev_exp[0] = aa
            p.wait("v", sA, aa)
            vv = p.op("v", lambda v, qt=qt:
                      v.reciprocal(p_all[:, qt:qt + 1], sm5[:, 1:2]), sV, 1)
            p.wait("v", sV, vv)
            vv = p.op("v", lambda v, sl=sl:
                      v.tensor_scalar(mku[:, :], pg[:, 1024 * sl:1024 * sl + 1024],
                                      sm5[:, 0:1], 0.0, OP.add, OP.is_ge), sV, 1)
            gevac[sl] = vv
            p.wait("v", sV, vv)
            if qt == 0:
                p.wait("v", dS, iota_ld)
                p.wait("v", sG, big_ms)
            vv = p.op("v", lambda v: v.select(tsel[:, :], mku[:, :], iosb[:, :], big[:, :]),
                      sV, 1)
            p.wait("v", sV, vv)
            vv = p.op("v", lambda v:
                      v.tensor_reduce(cnd[:, 0:8],
                                      tsel[:, :].rearrange("p (g e) -> p e g", e=8),
                                      axis=AX.X, op=OP.min), sV, 1)
            p.wait("v", sV, vv)
            vv = p.op("v", lambda v: v.tensor_reduce(sm5[:, 2:3], cnd[:, 0:8], axis=AX.X,
                                                     op=OP.min), sV, 1)
            p.wait("v", sV, vv)
            vv = p.op("v", lambda v, qt=qt:
                      v.tensor_scalar(oh_all[:, 8 * qt:8 * qt + 8], cnd[:, 0:8], sm5[:, 2:3],
                                      None, OP.is_le), sV, 1)
            p.wait("v", sV, vv)

        yrel = [0, 0]
        h2m = [0, 0]
        outd = [0, 0]

        def emit_final(qt):
            sl = qt % 2
            sl = qt % 2
            p.wait("y", sV, h2m[sl])
            yrel[sl] = p.op("y", lambda y, qt=qt, sl=sl:
                            y.dma_start(yre[sl][:, :], Ys[128 * qt:128 * qt + 128, :]), dS, 16)
            p.wait("v", dS, yrel[sl])
            p.wait("v", sV, acc_mark[qt])
            p.wait("v", dS, outd[sl])
            vv = p.op("v", lambda v, qt=qt, sl=sl:
                      v.scalar_tensor_tensor(h2[sl][:, :], ffna[:, D * qt:D * qt + D],
                                             p_all[:, qt:qt + 1], yre[sl][:, :],
                                             OP.mult, OP.add), sV, 1)
            h2m[sl] = vv
            p.wait("s", sV, vv)
            sqa = p.op("s", lambda s, sl=sl:
                       s.activation(sq5[:, :], h2[sl][:, :], AF.Square,
                                    accum_out=sm5[:, 6:7]), sA, 1)
            p.wait("v", sV, vv)
            vv = p.op("v", lambda v, sl=sl:
                      v.reduce_sum(sm5[:, 4:5], h2[sl][:, :], axis=AX.X, negate=True), sV, 1)
            p.wait("v", sV, vv)
            vv = p.op("v", lambda v: v.tensor_scalar(sm5[:, 5:6], sm5[:, 4:5], RD, None,
                                                     OP.mult), sV, 1)
            p.wait("v", sA, sqa)
            p.wait("v", sV, vv)
            vv = p.op("v", lambda v: v.tensor_scalar(sm5[:, 7:8], sm5[:, 6:7], RD, None,
                                                     OP.mult), sV, 1)
            p.wait("v", sV, vv)
            vv = p.op("v", lambda v: v.tensor_mul(sm5[:, 4:5], sm5[:, 5:6], sm5[:, 5:6]), sV, 1)
            p.wait("v", sV, vv)
            vv = p.op("v", lambda v: v.tensor_sub(sm5[:, 7:8], sm5[:, 7:8], sm5[:, 4:5]), sV, 1)
            p.wait("s", sV, vv)
            aa = p.op("s", lambda s: s.activation(sm5[:, 3:4], sm5[:, 7:8], AF.Ln,
                                                  bias=np.float32(LNEPS)), sA, 1)
            p.wait("s", sA, aa)
            aa = p.op("s", lambda s: s.activation(sm5[:, 2:3], sm5[:, 3:4], AF.Exp,
                                                  scale=np.float32(-0.5)), sA, 1)
            p.wait("v", sA, aa)
            vv = p.op("v", lambda v, sl=sl:
                      v.tensor_scalar(osb[sl][:, :], h2[sl][:, :], sm5[:, 5:6], sm5[:, 2:3],
                                      OP.add, OP.mult), sV, 1)
            p.wait("y", sV, vv)
            outd[sl] = p.op("y", lambda y, qt=qt, sl=sl:
                            y.dma_start(OUT[128 * qt:128 * qt + 128, :], osb[sl][:, :]), dS, 16)

        relu_done = [0, 0]
        stt_done = [0, 0]
        acc_mark = {}
        welast = [0, 0]
        for ei in range(8):
            parw = ei % 2
            if ei < 7:
                nxt = (ei + 1) % 2
                p.wait("g", sT, welast[nxt])
                wes = WE[ei + 1, :, :].rearrange("(c p) d -> p c d", p=128)
                wedn = web[nxt][:, :].rearrange("p (c d) -> p c d", d=D)
                weld[nxt] = p.op("g", lambda g, s=wes, d=wedn: g.dma_start(d, s), dG, 16)
            for qt in range(NQT):
                if ei == 0:
                    emit_gate(qt)
                u = ei * NQT + qt
                sl = u % 2
                p.wait("t", dG, weld[parw])
                p.wait("t", sA, relu_done[sl])
                for dc in range(8):
                    for half in range(2):
                        p.op("t", lambda t, qt=qt, dc=dc, hf=half, sl=sl, pw=parw:
                             t.matmul(pe_[:, 1024 * sl + 512 * hf:1024 * sl + 512 * hf + 512],
                                      y_fm[:, 2048 * dc + 128 * qt:2048 * dc + 128 * qt + 128],
                                      web[pw][:, 1024 * dc + 512 * hf:1024 * dc + 512 * hf + 512],
                                      start=(dc == 0), stop=False, skip_group_check=True),
                             None)
                for half in range(2):
                    p.op("t", lambda t, hf=half, sl=sl, ei=ei:
                         t.matmul(pe_[:, 1024 * sl + 512 * hf:1024 * sl + 512 * hf + 512],
                                  oneb[:, :], beb[:, 1024 * ei + 512 * hf:1024 * ei + 512 * hf + 512],
                                  start=False, stop=True, skip_group_check=True),
                         sT if half == 1 else None, 1)
                emm = sT.n
                welast[parw] = emm
                p.wait("s", sT, emm)
                p.wait("s", sV, stt_done[sl])
                relu_done[sl] = p.op("s", lambda s, sl=sl:
                                     s.activation(he[sl][:, :], pe_[:, 1024 * sl:1024 * sl + 1024],
                                                  AF.Relu), sA, 1)
                p.wait("v", sA, relu_done[sl])
                p.wait("v", sV, acc_mark.get(qt, 0))
                dstf = ffna[:, D * qt:D * qt + D]
                ohap = oh_all[:, 8 * qt + ei:8 * qt + ei + 1]
                if ei == 0:
                    vv = p.op("v", lambda v, d=dstf, sl=sl, o=ohap:
                              v.tensor_scalar(d, he[sl][:, :], o, None, OP.mult), sV, 1)
                else:
                    vv = p.op("v", lambda v, d=dstf, sl=sl, o=ohap:
                              v.scalar_tensor_tensor(d, he[sl][:, :], o, d, OP.mult, OP.add),
                              sV, 1)
                stt_done[sl] = vv
                acc_mark[qt] = vv
                if ei == 7:
                    emit_final(qt)

        for tn, tap in taps.items():
            p.op("g", lambda g, d=tap, s=scr[tn]: g.dma_start(d, s), dG, 16)
        p.wait("g", dG, dG.n)
        p.wait("y", dS, dS.n)
        p.run(nc)
    S45.close()
    top.close()
    return nc


def _host_inputs(inputs):
    x = np.ascontiguousarray(np.asarray(inputs["L0_attn_x"]), dtype=np.float32)
    cores = []
    tri = np.zeros((128, 128), dtype=bfnp)
    tri[np.arange(128)[:, None] > np.arange(128)[None, :]] = 1.0
    e16 = np.zeros((8, 128, 16), dtype=bfnp)
    for c in range(8):
        for pp in range(128):
            e16[c, pp, 2 * c + pp // 64] = 1.0
    ebc = np.transpose(e16, (0, 2, 1)).copy()
    f32a = lambda a: np.ascontiguousarray(np.asarray(a), dtype=np.float32)
    consts = {
        "wq": f32a(inputs["L0_attn_Wq"]), "wk": f32a(inputs["L0_attn_Wk"]),
        "wv": f32a(inputs["L0_attn_Wv"]), "wo": f32a(inputs["L0_attn_Wo"]),
        "wg": f32a(inputs["L0_ffn_Wg"]), "we": f32a(inputs["L0_ffn_We"]),
        "bg": f32a(inputs["L0_ffn_bg"]).reshape(1, D),
        "be": f32a(inputs["L0_ffn_be"]).reshape(1, 8 * D),
        "qscale": f32a(inputs["L0_attn_q_scale"]).reshape(H, 1),
        "kscale": f32a(inputs["L0_attn_k_scale"]).reshape(H, 1),
        "identb": np.eye(128, dtype=bfnp),
        "identf": np.eye(128, dtype=np.float32),
        "e16": e16, "ebc": ebc,
        "iota": np.broadcast_to(np.arange(D, dtype=np.float32), (128, D)).copy(),
        "negbigI": (np.eye(128) * -1.0e34).astype(bfnp),
        "onesb": np.ones((1, 128), dtype=bfnp),
    }
    for core in range(8):
        b, jc = core // 4, core % 4
        qtiles = [4 * s + (jc + s) % 4 for s in range(NQT)]
        xq = np.concatenate([x[b, 128 * t:128 * t + 128, :] for t in qtiles], axis=0)
        maskq = np.zeros((NKT, 128, 128), dtype=bfnp)
        for s in range(NQT):
            sig = (jc + s) % 4
            for r in range(4):
                if r == sig:
                    maskq[4 * s + r] = tri
                elif r > sig:
                    maskq[4 * s + r] = 1.0
        m = dict(consts)
        m["xkv"] = np.ascontiguousarray(x[b])
        m["xq"] = np.ascontiguousarray(xq)
        m["maskq"] = maskq
        cores.append(m)
    return cores


def kernel(**inputs):
    nc = build_nc()
    in_maps = _host_inputs(inputs)
    res = run_bass_kernel_spmd(nc, in_maps, core_ids=list(range(8)))
    if DEBUG_TAPS:
        global LAST_TAPS
        LAST_TAPS = [{k: v for k, v in r.items() if k.startswith("tap_")}
                     for r in res.results]
    out = np.empty((2, NB, D), dtype=np.float32)
    for core in range(8):
        b, jc = core // 4, core % 4
        o = res.results[core]["out"]
        for s in range(NQT):
            t = 4 * s + (jc + s) % 4
            out[b, 128 * t:128 * t + 128, :] = o[128 * s:128 * s + 128, :]
    return out
